# version 11
# baseline (speedup 1.0000x reference)
import numpy as np
import ml_dtypes
import concourse.bass as bass
import concourse.mybir as mybir
from concourse.bass import IndirectOffsetOnAxis
from concourse.bass_utils import run_bass_kernel_spmd
from contextlib import ExitStack

F32 = mybir.dt.float32
BF16 = mybir.dt.bfloat16
U32 = mybir.dt.uint32
AF = mybir.ActivationFunctionType
ALU = mybir.AluOpType
AX = mybir.AxisListType.X

bfnp = ml_dtypes.bfloat16

D = 2048
DCH = 16
T = 1024
NT = 8
TG = 8192
SLOTS = 64
OWN0 = 56
I_FF = 1408
ICH = 11
CAP = 2560.0
CAPS = 384
NSC = 3
NE = 8
BIG = 65536.0


class Chan:
    __slots__ = ("name", "total", "sem")

    def __init__(self, name):
        self.name = name
        self.total = 0
        self.sem = None


class Sched:
    ENGINES = ("tensor", "vector", "scalar", "sync", "gpsimd")

    def __init__(self):
        self.plan = {e: [] for e in self.ENGINES}
        self.chans = []
        self.last_w = {}
        self.readers = {}
        self.bar = {}

    def chan(self, name):
        c = Chan(name)
        self.chans.append(c)
        return c

    def membar(self, refs):
        for r in refs:
            evs = []
            if r in self.last_w:
                evs.append(self.last_w[r])
            evs.extend(self.readers.get(r, ()))
            for c, v in evs:
                if self.bar.get(c, 0) < v:
                    self.bar[c] = v

    def add(self, engine, fn, reads=(), writes=(), chan=None, n_inc=1,
            extra_waits=()):
        waits = dict(self.bar)

        def need(ev):
            c, v = ev
            if waits.get(c, 0) < v:
                waits[c] = v

        for r in reads:
            if r in self.last_w:
                need(self.last_w[r])
        for r in writes:
            if r in self.last_w:
                need(self.last_w[r])
            for ev in self.readers.get(r, ()):
                need(ev)
        for ev in extra_waits:
            need(ev)
        chan.total += n_inc
        ev = (chan, chan.total)
        for r in reads:
            self.readers.setdefault(r, []).append(ev)
        for r in writes:
            self.last_w[r] = ev
            self.readers[r] = []
        self.plan[engine].append(
            (sorted(waits.items(), key=lambda kv: kv[0].name), fn, chan, n_inc))
        return ev

    def emit(self, nc, es):
        for c in self.chans:
            c.sem = es.enter_context(nc.semaphore(c.name))
        block = es.enter_context(nc.Block())
        for eng_name in self.ENGINES:
            steps = self.plan[eng_name]
            if not steps:
                continue
            dec = getattr(block, eng_name)

            def make(steps):
                def body(eng):
                    hwm = {}
                    for waits, fn, chan, n_inc in steps:
                        for c, v in waits:
                            if hwm.get(id(c), 0) < v:
                                eng.wait_ge(c.sem, v)
                                hwm[id(c)] = v
                        fn(eng, chan.sem, n_inc)
                return body

            dec(make(steps))


class SbufScope:

    def __init__(self, nc):
        self.nc = nc
        self.stack = []

    def alloc(self, name, shape, dt=F32):
        cm = self.nc.sbuf_tensor(name, list(shape), dt)
        h = cm.__enter__()
        self.stack.append((name, cm))
        return h

    def free(self, names):
        for n in names:
            top_name, cm = self.stack.pop()
            assert top_name == n, f"non-LIFO free: {n} vs top {top_name}"
            cm.__exit__(None, None, None)


def build_nc():
    nc = bass.Bass("TRN2", target_bir_lowering=False, debug=False,
                   num_devices=8)
    es = ExitStack()
    mem = SbufScope(nc)

    def din(name, shape, dt=F32):
        return nc.dram_tensor(name, list(shape), dt, kind="ExternalInput").ap()

    xg = din("xg", [TG, D])
    xo = din("xo", [T, D])
    wsrc = {"q": din("wq", [D, D]), "k": din("wk", [D, D]),
            "v": din("wv", [D, D]), "g": din("wgt", [D, D])}
    bcols_h = din("bcols", [128, 64])
    rhost = din("rhost", [128, 128])
    wug = din("wug", [8, D, 2 * I_FF])
    wdn = din("wdn", [8, I_FF, D])
    sug = din("sug", [D, 11264])
    sdn = din("sdn", [5632, D])
    c_idbf = din("c_idbf", [128, 128], BF16)
    c_idf = din("c_idf", [128, 128])
    c_onesrow = din("c_onesrow", [1, 128])
    c_onescolf = din("c_onescolf", [128, 1])
    c_onescolbf = din("c_onescolbf", [128, 1], BF16)
    c_lstrict = din("c_lstrict", [128, 128], BF16)
    c_iota = din("c_iota", [128, 8])
    c_slotmask = din("c_slotmask", [128, 64])
    c_carrymask = din("c_carrymask", [128, 1])
    out_hbm = nc.dram_tensor("out", [T, D], F32, kind="ExternalOutput").ap()

    xbd = nc.dram_tensor("xb", [TG, D], BF16, kind="Internal").ap()
    wb = {w: nc.dram_tensor(f"wb_{w}", [D, D], BF16, kind="Internal").ap()
          for w in "qkvg"}
    wugb = nc.dram_tensor("wugb", [8, D, 2 * I_FF], BF16, kind="Internal").ap()
    wdnb = nc.dram_tensor("wdnb", [8, I_FF, D], BF16, kind="Internal").ap()
    gxe = nc.dram_tensor("gxe", [NE * CAPS, D], BF16, kind="Internal").ap()
    oxe = nc.dram_tensor("oxe", [NE * CAPS, D], BF16, kind="Internal").ap()

    P = mem.alloc
    s_idbf = P("s_idbf", [128, 128], BF16)
    s_idf = P("s_idf", [128, 128])
    s_onesrow = P("s_onesrow", [1, 128])
    s_onescolf = P("s_onescolf", [128, 1])
    s_onescolbf = P("s_onescolbf", [128, 1], BF16)
    s_lstrict = P("s_lstrict", [128, 128], BF16)
    s_iota = P("s_iota", [128, 8])
    s_slotmask = P("s_slotmask", [128, 64])
    s_carrymask = P("s_carrymask", [128, 1])
    s_bcols = P("s_bcols", [128, 64])
    s_r = P("s_r", [128, 128], BF16)
    xnT = P("xnT", [128, DCH * T], BF16)
    logitsT = P("logitsT", [128, SLOTS * 8])
    mcol = P("mcol", [128, SLOTS])
    rcol = P("rcol", [128, SLOTS])
    bcol2 = P("bcol2", [128, 1])
    sRbc = P("sRbc", [128, 8])
    sRrow = P("sRrow", [1, 8])
    masks = P("masks", [128, NT * 8])
    carryT = P("carryT", [128, 16])
    carryTm = P("carryTm", [128, 16])
    crow = P("crow", [1, 8])
    csrow = P("csrow", [1, 8])
    t2_mx1 = P("t2_mx1", [128, 1]); t2_mx2 = P("t2_mx2", [128, 1])
    t2_eq = P("t2_eq", [128, 8]); t2_cnd = P("t2_cnd", [128, 8])
    t2_ix = P("t2_ix", [128, 1])
    t2_is1 = P("t2_is1", [128, 8]); t2_is2 = P("t2_is2", [128, 8])
    t2_l2 = P("t2_l2", [128, 8]); t2_a = P("t2_a", [128, 8])
    t2_am = P("t2_am", [128, 8])
    t2_dd = P("t2_dd", [128, 1])
    t2_w1 = P("t2_w1", [128, 1]); t2_w2 = P("t2_w2", [128, 1])
    t2_m1 = P("t2_m1", [128, 8]); t2_m2 = P("t2_m2", [128, 8])
    t2_val = P("t2_val", [128, 8])
    xnTp = P("xnTp", [128, DCH * T], BF16)

    PG = [es.enter_context(nc.psum_tensor(f"PG{i}", [128, 1024], F32))
          for i in range(4)]
    PU = PG[3]

    S = Sched()
    ch = {n: S.chan(n) for n in
          ["pe", "ve", "se", "ge", "fin",
           "c_const", "c_router", "c_xo0", "c_xo1",
           "c_wbq", "c_wbk", "c_wbv", "c_wbg", "c_pc",
           "c_xT0", "c_xT1", "c_wt0", "c_wt1",
           "c_u0", "c_u1", "c_d0", "c_d1",
           "c_ub0", "c_ub1", "c_db0", "c_db1",
           "c_gx", "c_xs", "c_gb", "c_or", "c_ox", "c_gt",
           "c_xr0", "c_xr1", "c_out", "c_so0", "c_so1"]
          + [f"c_xb{j}" for j in range(8)]}
    PE, VE, SE, GE = ch["pe"], ch["ve"], ch["se"], ch["ge"]

    def dma(queue, chan, pairs, reads=(), writes=()):
        def fn(eng, sem, n, pairs=pairs):
            for o, i, kw in pairs:
                eng.dma_start(out=o, in_=i, **kw).then_inc(sem, 16)
        prev = (chan, chan.total)
        return S.add(queue, fn, reads=reads, writes=writes, chan=chan,
                     n_inc=16 * len(pairs),
                     extra_waits=[prev] if chan.total else [])

    def idma(chan, fn, reads=(), writes=()):
        def wrap(eng, sem, n, fn=fn):
            fn(eng).then_inc(sem, 16)
        prev = (chan, chan.total)
        return S.add("gpsimd", wrap, reads=reads, writes=writes, chan=chan,
                     n_inc=16, extra_waits=[prev] if chan.total else [])

    def pe(fn, reads=(), writes=()):
        def wrap(eng, sem, n, fn=fn):
            insts = fn(eng)
            insts[-1].then_inc(sem, 1)
        return S.add("tensor", wrap, reads=reads, writes=writes, chan=PE)

    def ve(fn, reads=(), writes=()):
        def wrap(eng, sem, n, fn=fn):
            fn(eng).then_inc(sem, 1)
        return S.add("vector", wrap, reads=reads, writes=writes, chan=VE)

    def se(fn, reads=(), writes=()):
        def wrap(eng, sem, n, fn=fn):
            fn(eng).then_inc(sem, 1)
        return S.add("scalar", wrap, reads=reads, writes=writes, chan=SE)

    def ge(fn, reads=(), writes=()):
        def wrap(eng, sem, n, fn=fn):
            fn(eng).then_inc(sem, 1)
        return S.add("gpsimd", wrap, reads=reads, writes=writes, chan=GE)

    dma("sync", ch["c_const"], [
        (s_idbf[:], c_idbf, {}), (s_idf[:], c_idf, {}),
        (s_onesrow[:], c_onesrow, {}), (s_onescolf[:], c_onescolf, {}),
        (s_onescolbf[:], c_onescolbf, {}), (s_lstrict[:], c_lstrict, {}),
        (s_iota[:], c_iota, {}),
        (s_slotmask[:], c_slotmask, {}), (s_carrymask[:], c_carrymask, {}),
        (s_bcols[:], bcols_h, {}),
    ], writes=[("consts",)])
    dma("gpsimd", ch["c_router"], [(s_r[:], rhost, {})], writes=[("router",)])
    def cast_w_piece(w, i):
        dma("gpsimd", ch[f"c_wb{w}"],
            [(wb[w][i * 512:(i + 1) * 512, :],
              wsrc[w][i * 512:(i + 1) * 512, :], {})],
            writes=[("wb", w, i)])

    def cast_xb_half(j, h):
        r0 = j * 1024 + h * 512
        dma("gpsimd", ch[f"c_xb{j}"],
            [(xbd[r0:r0 + 512, :], xg[r0:r0 + 512, :], {})],
            writes=[("xbh", 2 * j + h)])

    cast_w_piece("k", 0)
    cast_xb_half(7, 0)
    cast_xb_half(7, 1)
    cast_w_piece("v", 0)
    cast_xb_half(6, 0)
    cast_xb_half(6, 1)
    for i in (1, 2, 3):
        cast_w_piece("k", i)
        cast_w_piece("v", i)
    for j in [0, 1, 2, 3, 4, 5]:
        for h in (0, 1):
            cast_xb_half(j, h)
    for w in ("g", "q"):
        for i in range(4):
            cast_w_piece(w, i)
    pc_thunks = []
    for e in range(NE):
        for qh in range(4):
            def pcu(e=e, qh=qh):
                dma("gpsimd", ch["c_pc"],
                    [(wugb[e, qh * 512:(qh + 1) * 512, :],
                      wug[e, qh * 512:(qh + 1) * 512, :],
                      {"max_dma_last_dim": 8192})],
                    writes=[("wugb", e, qh)])
            pc_thunks.append(pcu)
        for hh in range(2):
            def pcd(e=e, hh=hh):
                dma("gpsimd", ch["c_pc"],
                    [(wdnb[e, hh * 704:(hh + 1) * 704, :],
                      wdn[e, hh * 704:(hh + 1) * 704, :],
                      {"max_dma_last_dim": 8192})],
                    writes=[("wdnb", e, hh)])
            pc_thunks.append(pcd)

    t2b_is1 = mem.alloc("t2b_is1", [128, 512], BF16)
    t2b_is2 = mem.alloc("t2b_is2", [128, 512], BF16)
    t2b_am = mem.alloc("t2b_am", [128, 512], BF16)
    t2b_mx1 = mem.alloc("t2b_mx1", [128, 64])
    t2b_mx2 = mem.alloc("t2b_mx2", [128, 64])

    AB_NAMES = []

    def AB(name, shape, dt=F32):
        AB_NAMES.append(name)
        return mem.alloc(name, shape, dt)

    xTb = [AB("xTb0", [128, DCH * 512], BF16),
           AB("xTb1", [128, DCH * 512], BF16)]
    _xq = AB("xsqT0", [128, 512], BF16); xsqT = [_xq, _xq]
    _mr = AB("mrow0", [1, 512]); mrow = [_mr, _mr]
    _sq = AB("sqrow0", [1, 512]); sqrow = [_sq, _sq]
    _vr = AB("vrow0", [1, 512]); vrow = [_vr, _vr]
    _rr = AB("rrow0", [1, 512]); rrow = [_rr, _rr]
    _lg = AB("lgs0", [8, 512]); lgs = [_lg, _lg]
    MRP_NAMES = []

    def MRP(name, shape, dt=F32):
        MRP_NAMES.append(name)
        return mem.alloc(name, shape, dt)

    mbc_sb = MRP("mbc_sb", [128, 512])
    rbc_sb = MRP("rbc_sb", [128, 512])
    pf_sb = MRP("pf_sb", [128, 512])

    def sr_mm(eng):
        insts = []
        for dc in range(DCH):
            insts.append(eng.matmul(PU[0:1, 0:8], s_onescolbf[:],
                                    s_r[:, dc * 8:(dc + 1) * 8],
                                    start=(dc == 0), stop=(dc == DCH - 1)))
        return insts
    pe(sr_mm, reads=[("router",), ("consts",)], writes=[("PG", 3)])
    ve(lambda v: v.tensor_copy(sRrow[:], PU[0:1, 0:8]),
       reads=[("PG", 3)], writes=[("sRrow",), ("PG", 3)])
    pe(lambda eng: [eng.matmul(PU[:, 0:8], s_onesrow[:], sRrow[:],
                               start=True, stop=True)],
       reads=[("sRrow",), ("consts",)], writes=[("PG", 3)])
    ve(lambda v: v.tensor_copy(sRbc[:], PU[:, 0:8]),
       reads=[("PG", 3)], writes=[("sRbc",), ("PG", 3)])

    def b_layout(tch):
        sl = tch % 2
        folded = tch >= 12
        if tch in (12, 13):
            dst, col0 = xnTp, (tch - 12) * 512
        elif tch in (14, 15):
            dst, col0 = xnT, (tch - 14) * 512
        else:
            dst, col0 = xTb[sl], 0
        dstref = ("xnTraw", tch) if folded else ("xTb", sl)
        pitch = T if folded else 512
        return sl, folded, dst, dstref, col0, pitch

    def plan_B_dma(tch):
        sl, partner, dst, dstref, col0, pitch = b_layout(tch)
        dma("sync", ch[f"c_xT{sl}"],
            [(dst[:, dc * pitch + col0: dc * pitch + col0 + 512],
              xbd[tch * 512:(tch + 1) * 512, dc * 128:(dc + 1) * 128],
              {"transpose": True}) for dc in range(DCH)],
            reads=[("xbh", tch)], writes=[dstref])

    def plan_B_compute(tch):
        sl, folded, dst, dstref, col0, pitch = b_layout(tch)

        g = tch % 3
        for dc in range(DCH):
            def xt_ap(dc=dc, dst=dst, col0=col0, pitch=pitch):
                return dst[:, dc * pitch + col0: dc * pitch + col0 + 512]
            s2 = dc % 2
            se(lambda s, s2=s2, xt=xt_ap: s.activation(xsqT[s2][:], xt(),
                                                       AF.Square),
               reads=[dstref], writes=[("xsqT", 0)])

            def mms(eng, dc=dc, s2=s2, g=g, xt=xt_ap):
                return [
                    eng.matmul(PU[0:1, 0:512], s_onescolbf[:], xt(),
                               start=(dc == 0), stop=(dc == DCH - 1)),
                    eng.matmul(PG[g][32:33, 512:1024], s_onescolbf[:],
                               xsqT[s2][:],
                               start=(dc == 0), stop=(dc == DCH - 1)),
                    eng.matmul(PG[g][0:8, 0:512], s_r[:, dc * 8:(dc + 1) * 8],
                               xt(), start=(dc == 0), stop=(dc == DCH - 1)),
                ]
            pe(mms, reads=[dstref, ("xsqT", 0), ("router",), ("consts",)],
               writes=[("PG", 3), ("PG", g)])

        ve(lambda v, sl=sl: v.tensor_single_scalar(mrow[sl][:],
                                                   PU[0:1, 0:512],
                                                   1.0 / D, ALU.mult),
           reads=[("PG", 3)], writes=[("mrow", 0), ("PG", 3)])
        ve(lambda v, sl=sl, g=g: v.tensor_single_scalar(
               sqrow[sl][:], PG[g][32:33, 512:1024], 1.0 / D, ALU.mult),
           reads=[("PG", g)], writes=[("sqrow", 0)])
        ve(lambda v, sl=sl: v.tensor_mul(vrow[sl][:], mrow[sl][:],
                                         mrow[sl][:]),
           reads=[("mrow", 0)], writes=[("vrow", 0)])
        ve(lambda v, sl=sl: v.tensor_tensor(vrow[sl][:], sqrow[sl][:],
                                            vrow[sl][:], ALU.subtract),
           reads=[("sqrow", 0), ("vrow", 0)], writes=[("vrow", 0)])
        ve(lambda v, sl=sl: v.tensor_single_scalar(vrow[sl][:], vrow[sl][:],
                                                   1e-5, ALU.add),
           reads=[("vrow", 0)], writes=[("vrow", 0)])
        se(lambda s, sl=sl: s.activation(vrow[sl][:], vrow[sl][:], AF.Ln),
           reads=[("vrow", 0)], writes=[("vrow", 0)])
        se(lambda s, sl=sl: s.activation(rrow[sl][:], vrow[sl][:], AF.Exp,
                                         scale=-0.5),
           reads=[("vrow", 0)], writes=[("rrow", 0)])

        ve(lambda v, g=g, sl=sl: v.tensor_copy(lgs[sl][:], PG[g][0:8, 0:512]),
           reads=[("PG", g)], writes=[("lgs", 0), ("PG", g)])

        def tpb(eng, sl=sl):
            insts = []
            for q in range(4):
                insts.append(eng.transpose(
                    PU[:, q:q + 1], mrow[sl][0:1, q * 128:(q + 1) * 128],
                    s_idf[0:1, 0:1]))
                insts.append(eng.transpose(
                    PU[:, 4 + q:5 + q], rrow[sl][0:1, q * 128:(q + 1) * 128],
                    s_idf[0:1, 0:1]))
                insts.append(eng.transpose(
                    PU[:, 8 + q * 8: 16 + q * 8],
                    lgs[sl][0:8, q * 128:(q + 1) * 128], s_idf[0:8, 0:8]))
            return insts
        pe(tpb, reads=[("mrow", 0), ("rrow", 0), ("lgs", 0), ("consts",)],
           writes=[("PG", 3)])

        def fold_ev(v, tch=tch):
            for q in range(4):
                j = tch * 4 + q
                v.tensor_copy(mcol[:, j:j + 1], PU[:, q:q + 1])
                v.tensor_copy(rcol[:, j:j + 1], PU[:, 4 + q:5 + q])
                inst = v.tensor_copy(logitsT[:, j * 8:(j + 1) * 8],
                                     PU[:, 8 + q * 8:16 + q * 8])
            return inst
        ve(fold_ev, reads=[("PG", 3)],
           writes=[("mrcol", tch), ("lraw", tch), ("PG", 3)])

        for q in range(4):
            j = tch * 4 + q
            ve(lambda v, j=j: v.tensor_mul(bcol2[:], mcol[:, j:j + 1],
                                           rcol[:, j:j + 1]),
               reads=[("mrcol", tch)], writes=[("bcol2",)])
            ve(lambda v, j=j: v.tensor_scalar(
                   logitsT[:, j * 8:(j + 1) * 8],
                   logitsT[:, j * 8:(j + 1) * 8],
                   rcol[:, j:j + 1], None, ALU.mult),
               reads=[("mrcol", tch), ("lraw", tch)],
               writes=[("lraw", tch)])
            ve(lambda v: v.tensor_scalar(t2_eq[:], sRbc[:], bcol2[:], None,
                                         ALU.mult),
               reads=[("sRbc",), ("bcol2",)], writes=[("t2",)])
            ve(lambda v, j=j: v.tensor_tensor(
                   logitsT[:, j * 8:(j + 1) * 8],
                   logitsT[:, j * 8:(j + 1) * 8],
                   t2_eq[:], ALU.subtract),
               reads=[("t2",), ("lraw", tch)],
               writes=[("lraw", tch)])
        S.last_w[("logits", tch)] = S.last_w[("lraw", tch)]

        if folded:
            g2 = (g + 1) % 3

            def bcst(eng, sl=sl, g2=g2):
                return [
                    eng.matmul(PG[g2][:, 0:512], s_onesrow[:], mrow[sl][:],
                               start=True, stop=True),
                    eng.matmul(PG[g2][:, 512:1024], s_onesrow[:], rrow[sl][:],
                               start=True, stop=True)]
            pe(bcst, reads=[("mrow", 0), ("rrow", 0), ("consts",)],
               writes=[("PG", g2)])
            ve(lambda v, g2=g2: v.tensor_copy(mbc_sb[:], PG[g2][:, 0:512]),
               reads=[("PG", g2)], writes=[("mbc",)])
            ve(lambda v, g2=g2: v.tensor_copy(rbc_sb[:], PG[g2][:, 512:1024]),
               reads=[("PG", g2)], writes=[("rbc",), ("PG", g2)])
            for dc in range(DCH):
                ve(lambda v, dc=dc, col0=col0, dst=dst: v.tensor_tensor(
                       pf_sb[:],
                       dst[:, dc * T + col0: dc * T + col0 + 512],
                       mbc_sb[:], ALU.subtract),
                   reads=[("xnTraw", tch), ("mbc",)], writes=[("pf",)])
                ve(lambda v, dc=dc, col0=col0, dst=dst: v.tensor_tensor(
                       dst[:, dc * T + col0: dc * T + col0 + 512],
                       pf_sb[:], rbc_sb[:], ALU.mult),
                   reads=[("pf",), ("rbc",)],
                   writes=[("xnTraw", tch), ("xnTf", tch)])


    b_early = [14, 15, 12, 13]
    for k, tch in enumerate(b_early):
        plan_B_dma(tch)
    for tch in b_early:
        plan_B_compute(tch)
    S.membar([("mbc",), ("rbc",), ("pf",)])
    mem.free(list(reversed(MRP_NAMES)))
    t2b_ec = mem.alloc("t2b_ec", [128, 512])
    t2b_ix = mem.alloc("t2b_ix", [128, 64])

    xnT_ref = [("xnTf", 14), ("xnTf", 15)]

    kv = mem.alloc("kv", [128, DCH * T], BF16)
    s_scr = [mem.alloc("s_scr0", [128, T], BF16),
             mem.alloc("s_scr1", [128, T], BF16)]
    wtbuf = [mem.alloc("wtbuf0", [128, DCH * 512], BF16),
             mem.alloc("wtbuf1", [128, DCH * 512], BF16)]
    k_sb = mem.alloc("k_sb", [128, 2048], BF16)
    kvp = mem.alloc("kvp", [128, 512], BF16)
    red0 = mem.alloc("red0", [128, 1])
    red1 = mem.alloc("red1", [128, 1])

    gcyc = [0]

    def next_g():
        g = gcyc[0] % 3
        gcyc[0] += 1
        return g

    g4cyc = [0]

    def next_g4():
        g = g4cyc[0] % 4
        g4cyc[0] += 1
        return g

    wslot = [0]

    def wt_group(w, e512):
        slw = wslot[0] % 2
        wslot[0] += 1
        dma("sync", ch[f"c_wt{slw}"],
            [(wtbuf[slw][:, dc * 512:(dc + 1) * 512],
              wb[w][e512 * 512:(e512 + 1) * 512, dc * 128:(dc + 1) * 128],
              {"transpose": True}) for dc in range(DCH)],
            reads=[("wb", w, e512)], writes=[("wt", slw)])
        return slw

    for e512 in range(4):
        slk = wt_group("k", e512)
        slv = wt_group("v", e512)
        for ec4 in range(4):
            echunk = e512 * 4 + ec4
            gk = [next_g4(), next_g4()]

            def kmm(eng, slk=slk, ec4=ec4, gk=gk):
                insts = []
                for dc in range(DCH):
                    lhs = wtbuf[slk][:, dc * 512 + ec4 * 128:
                                     dc * 512 + (ec4 + 1) * 128]
                    for q in range(4):
                        src = xnT if q < 2 else xnTp
                        mv = src[:, dc * T + (q % 2) * 512:
                                 dc * T + (q % 2) * 512 + 512]
                        insts.append(eng.matmul(
                            PG[gk[q // 2]][:, (q % 2) * 512:(q % 2) * 512 + 512],
                            lhs, mv, start=(dc == 0), stop=(dc == DCH - 1)))
                return insts
            pe(kmm, reads=[("wt", slk), ("xnTf", 12), ("xnTf", 13),
               ("xnTf", 14), ("xnTf", 15)],
               writes=[("PG", gk[0]), ("PG", gk[1])])

            def kev(v, ec=echunk, gk=gk):
                for q in range(4):
                    inst = v.tensor_scalar(
                        k_sb[:, q * 512:(q + 1) * 512],
                        PG[gk[q // 2]][:, (q % 2) * 512:(q % 2) * 512 + 512],
                        s_bcols[:, 16 + ec:17 + ec], None, ALU.add)
                return inst
            ve(kev, reads=[("PG", gk[0]), ("PG", gk[1]), ("consts",)],
               writes=[("k_sb",), ("PG", gk[0]), ("PG", gk[1])])

            gv = [next_g4(), next_g4()]

            def vmm(eng, slv=slv, ec4=ec4, gv=gv):
                insts = []
                for dc in range(DCH):
                    lhs = wtbuf[slv][:, dc * 512 + ec4 * 128:
                                     dc * 512 + (ec4 + 1) * 128]
                    for q in range(4):
                        src = xnT if q < 2 else xnTp
                        mv = src[:, dc * T + (q % 2) * 512:
                                 dc * T + (q % 2) * 512 + 512]
                        insts.append(eng.matmul(
                            PG[gv[q // 2]][:, (q % 2) * 512:(q % 2) * 512 + 512],
                            lhs, mv, start=(dc == 0), stop=(dc == DCH - 1)))
                return insts
            pe(vmm, reads=[("wt", slv), ("xnTf", 12), ("xnTf", 13),
               ("xnTf", 14), ("xnTf", 15)],
               writes=[("PG", gv[0]), ("PG", gv[1])])

            def kvf(v, ec=echunk, gv=gv):
                for q in range(2):
                    inst = v.scalar_tensor_tensor(
                        kv[:, ec * T + q * 512: ec * T + q * 512 + 512],
                        PG[gv[0]][:, q * 512: q * 512 + 512],
                        s_bcols[:, 32 + ec:33 + ec],
                        k_sb[:, q * 512:(q + 1) * 512], ALU.add, ALU.mult)
                return inst
            ve(kvf, reads=[("PG", gv[0]), ("k_sb",), ("consts",)],
               writes=[("kv", echunk), ("PG", gv[0])])

            def kvp1(v, ec=echunk, gv=gv):
                return v.scalar_tensor_tensor(
                    kvp[:], PG[gv[1]][:, 0:512], s_bcols[:, 32 + ec:33 + ec],
                    k_sb[:, 1024:1536], ALU.add, ALU.mult)
            ve(kvp1, reads=[("PG", gv[1]), ("k_sb",), ("consts",)],
               writes=[("kvp",)])
            ve(lambda v: v.reduce_sum(red0[:], kvp[:], AX),
               reads=[("kvp",)], writes=[("red0",)])

            def kvp2(v, ec=echunk, gv=gv):
                return v.scalar_tensor_tensor(
                    kvp[:], PG[gv[1]][:, 512:1024],
                    s_bcols[:, 32 + ec:33 + ec],
                    k_sb[:, 1536:2048], ALU.add, ALU.mult)
            ve(kvp2, reads=[("PG", gv[1]), ("k_sb",), ("consts",)],
               writes=[("kvp",), ("PG", gv[1])])
            ve(lambda v: v.reduce_sum(red1[:], kvp[:], AX),
               reads=[("kvp",)], writes=[("red1",)])

            ve(lambda v: v.tensor_add(red0[:], red0[:], red1[:]),
               reads=[("red0",), ("red1",)], writes=[("red0",)])
            ve(lambda v, ec=echunk: v.tensor_copy(carryT[:, ec:ec + 1],
                                                  red0[:]),
               reads=[("red0",)], writes=[("carryT", echunk)])
            if pc_thunks:
                pc_thunks.pop(0)()

    sig_g = xnTp
    S.membar([("xnTf", 12), ("xnTf", 13), ("xnTraw", 12), ("xnTraw", 13)])

    slg = wt_group("g", 0)
    for e512 in range(4):
        for ec4 in range(4):
            echunk = e512 * 4 + ec4
            gg = next_g()

            def gmm(eng, slg=slg, ec4=ec4, gg=gg):
                insts = []
                for dc in range(DCH):
                    lhs = wtbuf[slg][:, dc * 512 + ec4 * 128:
                                     dc * 512 + (ec4 + 1) * 128]
                    for q in range(2):
                        insts.append(eng.matmul(
                            PG[gg][:, q * 512:(q + 1) * 512], lhs,
                            xnT[:, dc * T + q * 512: dc * T + q * 512 + 512],
                            start=(dc == 0), stop=(dc == DCH - 1)))
                return insts
            pe(gmm, reads=[("wt", slg)] + xnT_ref, writes=[("PG", gg)])

            def gev(s, ec=echunk, gg=gg):
                for q in range(2):
                    inst = s.activation(
                        sig_g[:, ec * T + q * 512: ec * T + q * 512 + 512],
                        PG[gg][:, q * 512:(q + 1) * 512], AF.Sigmoid,
                        bias=s_bcols[:, 48 + ec:49 + ec])
                return inst
            se(gev, reads=[("PG", gg), ("consts",)],
               writes=[("sig_g", echunk), ("PG", gg)])
            if pc_thunks:
                pc_thunks.pop(0)()
            if echunk < 12:
                if echunk == 0:
                    plan_B_dma(0)
                    plan_B_dma(1)
                plan_B_compute(echunk)
                if echunk + 2 < 12:
                    plan_B_dma(echunk + 2)
            if ec4 == 0 and e512 < 3:
                nslg = wt_group("g", e512 + 1)
        slg = nslg

    ve(lambda v: v.tensor_scalar(carryTm[:], carryT[:], s_carrymask[:], None,
                                 ALU.mult),
       reads=[("carryT", ec) for ec in range(16)] + [("consts",)],
       writes=[("carryTm",)])

    kvs = [mem.alloc("kvs0", [128, T]), mem.alloc("kvs1", [128, T])]
    qt = mem.alloc("qt", [128, T])
    slq = wt_group("q", 0)
    for e512 in range(4):
        for ec4 in range(4):
            echunk = e512 * 4 + ec4
            skv = echunk % 2
            ve(lambda v, ec=echunk, skv=skv: v.tensor_tensor_scan(
                   kvs[skv][:], kv[:, ec * T:(ec + 1) * T],
                   kv[:, ec * T:(ec + 1) * T], carryTm[:, ec:ec + 1],
                   ALU.add, ALU.bypass),
               reads=[("kv", echunk), ("carryTm",)], writes=[("kvs", skv)])
            gq = next_g()

            def qmm(eng, slq=slq, ec4=ec4, gq=gq):
                insts = []
                for dc in range(DCH):
                    lhs = wtbuf[slq][:, dc * 512 + ec4 * 128:
                                     dc * 512 + (ec4 + 1) * 128]
                    for q in range(2):
                        insts.append(eng.matmul(
                            PG[gq][:, q * 512:(q + 1) * 512], lhs,
                            xnT[:, dc * T + q * 512: dc * T + q * 512 + 512],
                            start=(dc == 0), stop=(dc == DCH - 1)))
                return insts
            pe(qmm, reads=[("wt", slq)] + xnT_ref, writes=[("PG", gq)])

            def sq1(v, ec=echunk, gq=gq, skv=skv):
                return v.scalar_tensor_tensor(
                    qt[:], PG[gq][0:128, 0:1024], s_bcols[:, ec:ec + 1],
                    kvs[skv][:], ALU.add, ALU.mult)
            ve(sq1, reads=[("PG", gq), ("kvs", skv), ("consts",)],
               writes=[("qt",), ("PG", gq)])

            ssl = echunk % 2

            def sq2(v, ec=echunk, ssl=ssl):
                return v.tensor_tensor(s_scr[ssl][:], qt[:],
                                       sig_g[:, ec * T:(ec + 1) * T],
                                       ALU.mult)
            ve(sq2, reads=[("qt",), ("sig_g", echunk)],
               writes=[("s_scr", ssl)])
            dma("sync", ch[f"c_so{ssl}"],
                [(sig_g[:, echunk * T:(echunk + 1) * T]
                  .rearrange("p (a b) -> p a b", a=8),
                  s_scr[ssl][:], {"transpose": True})],
                reads=[("s_scr", ssl)],
                writes=[("sig_g", echunk), ("stm", echunk)])
            if pc_thunks:
                pc_thunks.pop(0)()
            if ec4 == 0 and e512 < 3:
                nslq = wt_group("q", e512 + 1)
        slq = nslq

    S.membar([("kv", ec) for ec in range(16)]
             + [("stm", ec) for ec in range(16)]
             + [("wt", 0), ("wt", 1), ("k_sb",), ("kvp",), ("red0",),
                ("red1",), ("kvs", 0), ("kvs", 1), ("qt",),
                ("s_scr", 0), ("s_scr", 1)])
    mem.free(["qt", "kvs1", "kvs0", "red1", "red0", "kvp", "k_sb",
              "wtbuf1", "wtbuf0", "s_scr1", "s_scr0", "kv"])

    logits_all = [("logits", tch) for tch in range(16)]

    def l3():
        return logitsT[:].rearrange("p (t e) -> p t e", e=8)

    def bc1(buf):
        return buf[:].unsqueeze(2).broadcast_to([128, 64, 8])

    iota_b = s_iota[:].unsqueeze(1).broadcast_to([128, 64, 8])
    slm_b = s_slotmask[:].unsqueeze(2).broadcast_to([128, 64, 8])

    def ec3():
        return t2b_ec[:].rearrange("p (t e) -> p t e", e=8)

    def am3():
        return t2b_am[:].rearrange("p (t e) -> p t e", e=8)

    ve(lambda v: v.tensor_reduce(t2b_mx1[:], l3(), AX, ALU.max),
       reads=logits_all, writes=[("t2bm",)])
    ve(lambda v: v.tensor_tensor(ec3(), l3(), bc1(t2b_mx1), ALU.is_equal),
       reads=logits_all + [("t2bm",)], writes=[("t2b",)])
    ve(lambda v: v.tensor_scalar(t2b_ec[:], t2b_ec[:], -1000.0, 1000.0,
                                 ALU.mult, ALU.add),
       reads=[("t2b",)], writes=[("t2b",)])
    ve(lambda v: v.tensor_tensor(ec3(), ec3(), iota_b, ALU.add),
       reads=[("t2b",), ("consts",)], writes=[("t2b",)])
    ve(lambda v: v.tensor_reduce(t2b_ix[:], ec3(), AX, ALU.min),
       reads=[("t2b",)], writes=[("t2bi",)])
    ve(lambda v: v.tensor_tensor(
           t2b_is1[:].rearrange("p (t e) -> p t e", e=8), iota_b,
           bc1(t2b_ix), ALU.is_equal),
       reads=[("t2bi",), ("consts",)], writes=[("t2b1",)])
    ve(lambda v: v.scalar_tensor_tensor(t2b_ec[:], t2b_is1[:], -1.0e30,
                                        logitsT[:], ALU.mult, ALU.add),
       reads=[("t2b1",)] + logits_all, writes=[("t2b",)])
    ve(lambda v: v.tensor_reduce(t2b_mx2[:], ec3(), AX, ALU.max),
       reads=[("t2b",)], writes=[("t2bm2",)])
    ve(lambda v: v.tensor_tensor(am3(), ec3(), bc1(t2b_mx2), ALU.is_equal),
       reads=[("t2b",), ("t2bm2",)], writes=[("t2ba",)])
    ve(lambda v: v.tensor_scalar(t2b_am[:], t2b_am[:], -1000.0, 1000.0,
                                 ALU.mult, ALU.add),
       reads=[("t2ba",)], writes=[("t2ba",)])
    ve(lambda v: v.tensor_tensor(am3(), am3(), iota_b, ALU.add),
       reads=[("t2ba",), ("consts",)], writes=[("t2ba",)])
    ve(lambda v: v.tensor_reduce(t2b_ix[:], am3(), AX, ALU.min),
       reads=[("t2ba",)], writes=[("t2bi",)])
    ve(lambda v: v.tensor_tensor(
           t2b_is2[:].rearrange("p (t e) -> p t e", e=8), iota_b,
           bc1(t2b_ix), ALU.is_equal),
       reads=[("t2bi",), ("consts",)], writes=[("t2b2",)])
    ve(lambda v: v.tensor_tensor(t2b_am[:], t2b_is1[:], t2b_is2[:], ALU.add),
       reads=[("t2b1",), ("t2b2",)], writes=[("t2ba",)])
    ve(lambda v: v.tensor_tensor(am3(), am3(), slm_b, ALU.mult),
       reads=[("t2ba",), ("consts",)], writes=[("t2ba",)])

    def csumf(eng):
        insts = []
        for j in range(OWN0):
            insts.append(eng.matmul(PU[0:1, 0:8], s_onescolbf[:],
                                    t2b_am[:, j * 8:(j + 1) * 8],
                                    start=(j == 0), stop=(j == OWN0 - 1)))
        return insts
    pe(csumf, reads=[("t2ba",), ("consts",)], writes=[("PG", 3)])
    ve(lambda v: v.tensor_copy(crow[:], PU[0:1, 0:8]),
       reads=[("PG", 3)], writes=[("crow",), ("PG", 3)])

    def c_tile(j):
        tt = j - OWN0
        if tt == 0:
            ve(lambda v: v.tensor_copy(fbrow[:], crow[:]),
               reads=[("crow",)], writes=[("fbrow",)])

        def pfx(eng, j=j):
            return [
                eng.matmul(PU[:, 8:16], s_lstrict[:],
                           t2b_am[:, j * 8:(j + 1) * 8],
                           start=True, stop=False),
                eng.matmul(PU[:, 8:16], s_onesrow[:], crow[:],
                           start=False, stop=True)]
        pe(pfx, reads=[("t2ba",), ("crow",), ("consts",)], writes=[("PG", 3)])
        ve(lambda v, tt=tt: v.tensor_copy(posb[:, tt * 8:(tt + 1) * 8],
                                          PU[:, 8:16]),
           reads=[("PG", 3)], writes=[("posb", tt)])
        ve(lambda v: v.tensor_single_scalar(t2_val[:], PU[:, 8:16],
                                            CAP, ALU.is_lt),
           reads=[("PG", 3)], writes=[("t2val",), ("PG", 3)])
        ve(lambda v, j=j, tt=tt: v.tensor_tensor(
               ambuf[:, tt * 8:(tt + 1) * 8], t2b_am[:, j * 8:(j + 1) * 8],
               t2_val[:], ALU.mult),
           reads=[("t2ba",), ("t2val",)], writes=[("ambuf", tt)])
        ve(lambda v, j=j: v.tensor_tensor(t2_dd[:], t2b_mx1[:, j:j + 1],
                                          t2b_mx2[:, j:j + 1], ALU.subtract),
           reads=[("t2bm",), ("t2bm2",)], writes=[("t2dd",)])
        se(lambda s: s.activation(t2_w1[:], t2_dd[:], AF.Sigmoid),
           reads=[("t2dd",)], writes=[("t2w1",)])
        se(lambda s: s.activation(t2_w2[:], t2_dd[:], AF.Sigmoid, scale=-1.0),
           reads=[("t2dd",)], writes=[("t2w2",)])
        ve(lambda v, j=j: v.tensor_scalar(t2_m1[:],
                                          t2b_is1[:, j * 8:(j + 1) * 8],
                                          t2_w1[:], None, ALU.mult),
           reads=[("t2b1",), ("t2w1",)], writes=[("t2m1",)])
        ve(lambda v, j=j: v.tensor_scalar(t2_m2[:],
                                          t2b_is2[:, j * 8:(j + 1) * 8],
                                          t2_w2[:], None, ALU.mult),
           reads=[("t2b2",), ("t2w2",)], writes=[("t2m2",)])
        ve(lambda v: v.tensor_add(t2_m1[:], t2_m1[:], t2_m2[:]),
           reads=[("t2m1",), ("t2m2",)], writes=[("t2m1",)])
        ve(lambda v, tt=tt: v.tensor_tensor(
               masks[:, tt * 8:(tt + 1) * 8], t2_m1[:], t2_val[:], ALU.mult),
           reads=[("t2m1",), ("t2val",)], writes=[("masks", tt)])

        def csum1(eng, j=j):
            return [eng.matmul(PU[0:1, 0:8], s_onescolbf[:],
                               t2b_am[:, j * 8:(j + 1) * 8],
                               start=True, stop=True)]
        pe(csum1, reads=[("t2ba",), ("consts",)], writes=[("PG", 3)])
        ve(lambda v: v.tensor_copy(csrow[:], PU[0:1, 0:8]),
           reads=[("PG", 3)], writes=[("csrow",), ("PG", 3)])
        ve(lambda v: v.tensor_add(crow[:], crow[:], csrow[:]),
           reads=[("csrow",), ("crow",)], writes=[("crow",)])

    c_thunks = [(lambda j=j: c_tile(j)) for j in range(OWN0, SLOTS)]

    S.membar([("xTb", 0), ("xTb", 1), ("xsqT", 0),
              ("mrow", 0), ("sqrow", 0), ("vrow", 0), ("rrow", 0),
              ("lgs", 0), ("t2b",), ("t2bi",)])
    mem.free(["t2b_ix", "t2b_ec"] + list(reversed(AB_NAMES)))

    acc = mem.alloc("acc", [128, DCH * T])
    posb = mem.alloc("posb", [128, 64])
    ambuf = mem.alloc("ambuf", [128, 64])
    wbuf = mem.alloc("wbuf", [128, 16])
    gidxu = mem.alloc("gidxu", [128, 32], U32)
    fbrow = mem.alloc("fbrow", [1, 8])
    fb_bc = mem.alloc("fb_bc", [128, 8])
    ebase = mem.alloc("ebase", [128, 8])
    lpb = mem.alloc("lpb", [128, 8])
    a8 = mem.alloc("a8", [128, 8])
    b8 = mem.alloc("b8", [128, 8])
    vcol = mem.alloc("vcol", [128, 1])
    gcol = mem.alloc("gcol", [128, 1])
    hbuf = mem.alloc("hbuf", [128, ICH * T], BF16)
    ubuf = [mem.alloc(f"ubuf{i}", [128, DCH * 128], BF16) for i in range(3)]
    dbuf = [mem.alloc(f"dbuf{i}", [128, ICH * 128], BF16) for i in range(2)]
    hsil = mem.alloc("hsil", [128, T], BF16)
    dzb = [mem.alloc(f"dzb{i}", [128, CAPS], BF16) for i in range(4)]
    xnstage = [mem.alloc(f"xnstage{i}", [128, D], BF16) for i in range(2)]
    xntm = [mem.alloc(f"xntm{i}", [128, D], BF16) for i in range(2)]
    uslot = [0]
    dslot = [0]

    xnT3 = xnT[:].rearrange("p (dc t) -> p dc t", dc=DCH)

    def r_prep():
        gfb = next_g()
        pe(lambda eng, gfb=gfb: [eng.matmul(PG[gfb][:, 0:8], s_onesrow[:],
                                            fbrow[:], start=True, stop=True)],
           reads=[("fbrow",), ("consts",)], writes=[("PG", gfb)])
        ve(lambda v, gfb=gfb: v.tensor_copy(fb_bc[:], PG[gfb][:, 0:8]),
           reads=[("PG", gfb)], writes=[("fb_bc",), ("PG", gfb)])
        ve(lambda v: v.tensor_single_scalar(ebase[:], s_iota[:], float(CAPS),
                                            ALU.mult),
           reads=[("consts",)], writes=[("ebase",)])

    def r_thunk(tt):
        j = OWN0 + tt
        sl = tt % 2
        pcol = posb[:, tt * 8:(tt + 1) * 8]
        acol = ambuf[:, tt * 8:(tt + 1) * 8]
        ve(lambda v: v.tensor_tensor(lpb[:], pcol, fb_bc[:], ALU.subtract),
           reads=[("posb", tt), ("fb_bc",)], writes=[("lpb",)])
        ve(lambda v: v.tensor_tensor(lpb[:], lpb[:], ebase[:], ALU.add),
           reads=[("lpb",), ("ebase",)], writes=[("lpb",)])
        for k, isb, gofs, wofs in ((0, t2b_is1, 0, 0), (1, t2b_is2, 16, 8)):
            iscol = isb[:, j * 8:(j + 1) * 8]
            isref = ("t2b1",) if k == 0 else ("t2b2",)
            ve(lambda v, iscol=iscol: v.tensor_tensor(a8[:], iscol, acol,
                                                      ALU.mult),
               reads=[isref, ("ambuf", tt)], writes=[("a8",)])
            ve(lambda v: v.tensor_tensor(b8[:], a8[:], lpb[:], ALU.mult),
               reads=[("a8",), ("lpb",)], writes=[("b8",)])
            ve(lambda v: v.reduce_sum(gcol[:], b8[:], AX),
               reads=[("b8",)], writes=[("gcol",)])
            ve(lambda v: v.reduce_sum(vcol[:], a8[:], AX),
               reads=[("a8",)], writes=[("vcol",)])
            ve(lambda v: v.tensor_single_scalar(vcol[:], vcol[:], -BIG,
                                                ALU.mult),
               reads=[("vcol",)], writes=[("vcol",)])
            ve(lambda v: v.tensor_single_scalar(vcol[:], vcol[:], BIG,
                                                ALU.add),
               reads=[("vcol",)], writes=[("vcol",)])
            ve(lambda v: v.tensor_add(gcol[:], gcol[:], vcol[:]),
               reads=[("gcol",), ("vcol",)], writes=[("gcol",)])
            ve(lambda v, c=gofs + tt: v.tensor_copy(gidxu[:, c:c + 1],
                                                    gcol[:]),
               reads=[("gcol",)], writes=[("gidxu", gofs + tt)])
            ve(lambda v, iscol=iscol, tt=tt: v.tensor_tensor(
                   b8[:], iscol, masks[:, tt * 8:(tt + 1) * 8], ALU.mult),
               reads=[isref, ("masks", tt)], writes=[("b8",)])
            ve(lambda v, c=wofs + tt: v.reduce_sum(wbuf[:, c:c + 1], b8[:],
                                                   AX),
               reads=[("b8",)], writes=[("wbuf", wofs + tt)])
        ve(lambda v, tt=tt, sl=sl: v.tensor_copy(
               xnstage[sl][:].rearrange("p (a b) -> p a b", a=DCH),
               xnT3[:, :, tt * 128:(tt + 1) * 128]),
           reads=xnT_ref, writes=[("xnstage", sl)])
        dma("sync", ch["c_xs"],
            [(xntm[sl][:].rearrange("p (a b) -> p a b", a=DCH),
              xnstage[sl][:], {"transpose": True})],
            reads=[("xnstage", sl)], writes=[("xntm", sl)])
        for gofs in (0, 16):
            def sc(eng, sl=sl, c=gofs + tt):
                return eng.indirect_dma_start(
                    out=gxe, out_offset=IndirectOffsetOnAxis(
                        ap=gidxu[:, c:c + 1], axis=0),
                    in_=xntm[sl][:], in_offset=None,
                    bounds_check=NE * CAPS - 1, oob_is_err=False)
            idma(ch["c_gx"], sc,
                 reads=[("xntm", sl), ("gidxu", gofs + tt)],
                 writes=[("gxe",)])

    r_thunks = [(lambda tt=tt: r_thunk(tt)) for tt in range(NT)]

    def gb_ap(gsl):
        return xnT[:, gsl * DCH * CAPS:(gsl + 1) * DCH * CAPS] \
            .rearrange("p (dc s) -> p dc s", dc=DCH)

    def gb_mv(gsl, dc):
        base = gsl * DCH * CAPS + dc * CAPS
        return xnT[:, base:base + CAPS]

    def load_gbuf(e):
        gsl = e % 2
        dma("sync", ch["c_gb"],
            [(gb_ap(gsl), gxe[e * CAPS:(e + 1) * CAPS, :],
              {"transpose": True})],
            reads=[("gxe",)],
            writes=[("gbufs", gsl), ("xnTf", 14), ("xnTf", 15)])

    outrow = None
    UNIT_ORDER = [8, 9, 10, 11] + list(range(NE))
    for ui, u in enumerate(UNIT_ORDER):
        routed = u < 8
        if ui == 4:
            while r_thunks:
                r_thunks.pop(0)()
            S.membar([("xntm", 0), ("xntm", 1), ("xnstage", 0),
                      ("xnstage", 1)])
            mem.free(["xntm1", "xntm0", "xnstage1", "xnstage0"])
            outrow = mem.alloc("outrow", [128, NSC, D], BF16)
        if routed:
            e = u
            gsl = e % 2
            if e == 0:
                load_gbuf(0)
            for ic in range(ICH):
                gus = []
                for mc in (ic, ICH + ic):
                    su_ = uslot[0] % 3
                    uslot[0] += 1
                    dma("gpsimd", ch[f"c_ub{mc % 2}"],
                        [(ubuf[su_][:],
                          wugb[e, :, mc * 128:(mc + 1) * 128]
                          .rearrange("(dc p) m -> p dc m", p=128), {})],
                        reads=[("wugb", e, qh) for qh in range(4)],
                        writes=[("ubuf", su_)])
                    gu = next_g()
                    gus.append(gu)

                    def umm(eng, su_=su_, gu=gu, gsl=gsl):
                        insts = []
                        for dc in range(DCH):
                            insts.append(eng.matmul(
                                PG[gu][:, 0:CAPS],
                                ubuf[su_][:, dc * 128:(dc + 1) * 128],
                                gb_mv(gsl, dc),
                                start=(dc == 0), stop=(dc == DCH - 1)))
                        return insts
                    pe(umm, reads=[("ubuf", su_), ("gbufs", gsl)],
                       writes=[("PG", gu)])
                gg_, gu_ = gus
                se(lambda s, gg_=gg_: s.activation(hsil[:, 0:CAPS],
                                                   PG[gg_][0:128, 0:CAPS],
                                                   AF.Silu),
                   reads=[("PG", gg_)], writes=[("hsil",), ("PG", gg_)])

                def hb(v, ic=ic, gu_=gu_):
                    return v.scalar_tensor_tensor(
                        hbuf[:, ic * CAPS:(ic + 1) * CAPS],
                        PG[gu_][0:128, 0:CAPS], 0.0, hsil[:, 0:CAPS],
                        ALU.bypass, ALU.mult)
                ve(hb, reads=[("PG", gu_), ("hsil",)],
                   writes=[("h", ic), ("PG", gu_)])
            if e + 1 < NE:
                load_gbuf(e + 1)
            for dcc in range(DCH):
                sd_ = dslot[0] % 2
                dslot[0] += 1
                dma("gpsimd", ch[f"c_db{sd_}"],
                    [(dbuf[sd_][:],
                      wdnb[e, :, dcc * 128:(dcc + 1) * 128]
                      .rearrange("(ic p) d -> p ic d", p=128), {})],
                    reads=[("wdnb", e, 0), ("wdnb", e, 1)],
                    writes=[("dbuf", sd_)])
                gd = next_g()

                def dmm(eng, sd_=sd_, gd=gd):
                    insts = []
                    for ic in range(ICH):
                        insts.append(eng.matmul(
                            PG[gd][:, 0:CAPS],
                            dbuf[sd_][:, ic * 128:(ic + 1) * 128],
                            hbuf[:, ic * CAPS:(ic + 1) * CAPS],
                            start=(ic == 0), stop=(ic == ICH - 1)))
                    return insts
                pe(dmm, reads=[("dbuf", sd_)]
                   + [("h", ic) for ic in range(ICH)],
                   writes=[("PG", gd)])
                zsl = dcc % 4
                ve(lambda v, gd=gd, zsl=zsl: v.tensor_copy(
                       dzb[zsl][:], PG[gd][0:128, 0:CAPS]),
                   reads=[("PG", gd)], writes=[("dzb", zsl), ("PG", gd)])
                dma("sync", ch["c_or"],
                    [(outrow[:, :, dcc * 128:(dcc + 1) * 128],
                      dzb[zsl][:], {"transpose": True})],
                    reads=[("dzb", zsl)], writes=[("outrow", dcc)])
            dma("scalar", ch["c_ox"],
                [(oxe[e * CAPS:(e + 1) * CAPS, :]
                  .rearrange("(sc p) d -> p sc d", p=128),
                  outrow[:].rearrange("p a b -> p (a b)"), {})],
                reads=[("outrow", dcc) for dcc in range(DCH)],
                writes=[("oxe", e)])
            continue

        g0 = u - 8

        def up_ap(mc, g0=g0):
            if mc < ICH:
                c0 = g0 * I_FF + mc * 128
            else:
                c0 = 5632 + g0 * I_FF + (mc - ICH) * 128
            return sug[:, c0:c0 + 128]

        def dn_ap(dcc, g0=g0):
            return sdn[g0 * I_FF:(g0 + 1) * I_FF,
                       dcc * 128:(dcc + 1) * 128]

        for ic in range(ICH):
            if ui <= 1 and ic % 3 == 0 and c_thunks:
                c_thunks.pop(0)()
            if ui >= 2 and ic % 3 == 0 and r_thunks:
                r_thunks.pop(0)()
            gus = []
            for mc in (ic, ICH + ic):
                su_ = uslot[0] % 3
                uslot[0] += 1
                dma("gpsimd", ch[f"c_u{mc % 2}"],
                    [(ubuf[su_][:],
                      up_ap(mc).rearrange("(dc p) m -> p dc m", p=128),
                      {})],
                    writes=[("ubuf", su_)])
                gu = next_g()
                gus.append(gu)

                def umm(eng, su_=su_, gu=gu):
                    insts = []
                    for dc in range(DCH):
                        lhs = ubuf[su_][:, dc * 128:(dc + 1) * 128]
                        for q in range(2):
                            insts.append(eng.matmul(
                                PG[gu][:, q * 512:(q + 1) * 512], lhs,
                                xnT[:, dc * T + q * 512:
                                    dc * T + q * 512 + 512],
                                start=(dc == 0), stop=(dc == DCH - 1)))
                    return insts
                pe(umm, reads=[("ubuf", su_)] + xnT_ref, writes=[("PG", gu)])
            gg_, gu_ = gus
            se(lambda s, gg_=gg_: s.activation(hsil[:],
                                               PG[gg_][0:128, 0:1024],
                                               AF.Silu),
               reads=[("PG", gg_)], writes=[("hsil",), ("PG", gg_)])

            def hb(v, ic=ic, gu_=gu_):
                return v.scalar_tensor_tensor(
                    hbuf[:, ic * T:(ic + 1) * T],
                    PG[gu_][0:128, 0:1024], 0.0, hsil[:],
                    ALU.bypass, ALU.mult)
            ve(hb, reads=[("PG", gu_), ("hsil",)],
               writes=[("h", ic), ("PG", gu_)])
        if ui == 1:
            while c_thunks:
                c_thunks.pop(0)()
            r_prep()
        for dcc in range(DCH):
            sd_ = dslot[0] % 2
            dslot[0] += 1
            dma("gpsimd", ch[f"c_d{sd_}"],
                [(dbuf[sd_][:],
                  dn_ap(dcc).rearrange("(ic p) d -> p ic d", p=128), {})],
                writes=[("dbuf", sd_)])
            gd = next_g()

            def dmm(eng, sd_=sd_, gd=gd):
                insts = []
                for ic in range(ICH):
                    lhs = dbuf[sd_][:, ic * 128:(ic + 1) * 128]
                    for q in range(2):
                        insts.append(eng.matmul(
                            PG[gd][:, q * 512:(q + 1) * 512], lhs,
                            hbuf[:, ic * T + q * 512:
                                   ic * T + q * 512 + 512],
                            start=(ic == 0), stop=(ic == ICH - 1)))
                return insts
            pe(dmm, reads=[("dbuf", sd_)] + [("h", ic) for ic in range(ICH)],
               writes=[("PG", gd)])
            if ui == 0:
                ve(lambda v, dcc=dcc, gd=gd: v.tensor_copy(
                       acc[:, dcc * T:(dcc + 1) * T], PG[gd][0:128, 0:1024]),
                   reads=[("PG", gd)], writes=[("acc", dcc), ("PG", gd)])
            else:
                ve(lambda v, dcc=dcc, gd=gd: v.tensor_tensor(
                       acc[:, dcc * T:(dcc + 1) * T],
                       acc[:, dcc * T:(dcc + 1) * T],
                       PG[gd][0:128, 0:1024], ALU.add),
                   reads=[("PG", gd), ("acc", dcc)],
                   writes=[("acc", dcc), ("PG", gd)])

    S.membar([("hsil",), ("gbufs", 0), ("gbufs", 1)]
             + [("dzb", i) for i in range(4)]
             + [("outrow", dcc) for dcc in range(DCH)]
             + [("dbuf", i) for i in range(2)]
             + [("ubuf", i) for i in range(3)]
             + [("h", ic) for ic in range(ICH)])
    mem.free(["outrow", "dzb3", "dzb2", "dzb1", "dzb0", "hsil",
              "dbuf1", "dbuf0", "ubuf2", "ubuf1", "ubuf0", "hbuf"])

    g1 = [mem.alloc("g1a", [128, D], BF16), mem.alloc("g1b", [128, D], BF16)]
    g2 = [mem.alloc("g2a", [128, D], BF16), mem.alloc("g2b", [128, D], BF16)]
    tc1 = mem.alloc("tc1", [128, T])
    tc2 = mem.alloc("tc2", [128, T])
    silt = mem.alloc("silt", [128, T])
    xrl = [mem.alloc("xrl0", [128, D]), mem.alloc("xrl1", [128, D])]
    outt = [mem.alloc("outt0", [128, D]), mem.alloc("outt1", [128, D])]
    sgv = sig_g[:].rearrange("p (ec a) -> p ec a", ec=DCH)
    oxe_refs = [("oxe", e) for e in range(NE)]
    for tt in range(NT):
        so = tt % 2
        ge(lambda g, so=so: g.memset(g1[so][:], 0.0),
           writes=[("g1", so)])
        ge(lambda g, so=so: g.memset(g2[so][:], 0.0),
           writes=[("g2", so)])

        def ga1(eng, so=so, tt=tt):
            return eng.indirect_dma_start(
                out=g1[so][:], out_offset=None, in_=oxe,
                in_offset=IndirectOffsetOnAxis(
                    ap=gidxu[:, tt:tt + 1], axis=0),
                bounds_check=NE * CAPS - 1, oob_is_err=False)
        idma(ch["c_gt"], ga1,
             reads=oxe_refs + [("g1", so), ("gidxu", tt)],
             writes=[("g1", so)])

        def ga2(eng, so=so, tt=tt):
            return eng.indirect_dma_start(
                out=g2[so][:], out_offset=None, in_=oxe,
                in_offset=IndirectOffsetOnAxis(
                    ap=gidxu[:, 16 + tt:17 + tt], axis=0),
                bounds_check=NE * CAPS - 1, oob_is_err=False)
        idma(ch["c_gt"], ga2,
             reads=oxe_refs + [("g2", so), ("gidxu", 16 + tt)],
             writes=[("g2", so)])
        dma("sync", ch[f"c_xr{so}"],
            [(xrl[so][:], xo[tt * 128:(tt + 1) * 128, :], {})],
            writes=[("xrl", so)])
        for half in range(2):
            gf = next_g()

            def ftp(eng, tt=tt, half=half, gf=gf):
                insts = []
                for k in range(8):
                    dcc = half * 8 + k
                    insts.append(eng.transpose(
                        PG[gf][:, k * 128:(k + 1) * 128],
                        acc[:, dcc * T + tt * 128: dcc * T + (tt + 1) * 128],
                        s_idf[:]))
                return insts
            pe(ftp, reads=[("acc", half * 8 + k) for k in range(8)]
               + [("consts",)],
               writes=[("PG", gf)])
            hs = slice(half * 1024, (half + 1) * 1024)
            ve(lambda v, so=so, tt=tt, hs=hs: v.tensor_scalar(
                   tc1[:], g1[so][:, hs], wbuf[:, tt:tt + 1], None,
                   ALU.mult),
               reads=[("g1", so), ("wbuf", tt)], writes=[("tc1",)])
            ve(lambda v, so=so, tt=tt, hs=hs: v.tensor_scalar(
                   tc2[:], g2[so][:, hs], wbuf[:, 8 + tt:9 + tt], None,
                   ALU.mult),
               reads=[("g2", so), ("wbuf", 8 + tt)], writes=[("tc2",)])
            ve(lambda v: v.tensor_add(tc1[:], tc1[:], tc2[:]),
               reads=[("tc1",), ("tc2",)], writes=[("tc1",)])
            ve(lambda v, gf=gf: v.tensor_tensor(tc1[:], tc1[:],
                                                PG[gf][0:128, 0:1024],
                                                ALU.add),
               reads=[("tc1",), ("PG", gf)], writes=[("tc1",), ("PG", gf)])
            se(lambda s: s.activation(silt[:], tc1[:], AF.Silu),
               reads=[("tc1",)], writes=[("silt",)])
            ve(lambda v, tt=tt, half=half: v.tensor_tensor(
                   tc1[:].rearrange("p (a b) -> p a b", a=8),
                   silt[:].rearrange("p (a b) -> p a b", a=8),
                   sgv[:, half * 8:(half + 1) * 8,
                       tt * 128:(tt + 1) * 128],
                   ALU.mult),
               reads=[("silt",)] + [("stm", ec)
                                    for ec in range(half * 8, half * 8 + 8)],
               writes=[("tc1",)])
            ve(lambda v, so=so, hs=hs: v.tensor_tensor(
                   outt[so][:, hs], tc1[:], xrl[so][:, hs], ALU.add),
               reads=[("tc1",), ("xrl", so)], writes=[("outt", so, half)])
        dma("sync", ch["c_out"],
            [(out_hbm[tt * 128:(tt + 1) * 128, :], outt[so][:], {})],
            reads=[("outt", so, 0), ("outt", so, 1)],
            writes=[("outdma", tt)])

    def fin(eng, sem, n):
        eng.nop().then_inc(sem, n)
    S.add("gpsimd", fin, chan=ch["fin"],
          extra_waits=[(ch["c_out"], ch["c_out"].total)])

    S.emit(nc, es)
    es.close()
    nc.finalize()
    return nc


_NC_CACHE = {}


def _consts():
    c = {}
    c["c_idbf"] = np.eye(128, dtype=bfnp)
    c["c_idf"] = np.eye(128, dtype=np.float32)
    c["c_onesrow"] = np.ones((1, 128), dtype=np.float32)
    c["c_onescolf"] = np.ones((128, 1), dtype=np.float32)
    c["c_onescolbf"] = np.ones((128, 1), dtype=bfnp)
    c["c_lstrict"] = np.triu(np.ones((128, 128), dtype=bfnp), k=1)
    c["c_iota"] = np.broadcast_to(np.arange(8, dtype=np.float32),
                                  (128, 8)).copy()
    return c


def make_in_maps(L0_attn_x, L0_attn_Wq, L0_attn_bq, L0_attn_Wk, L0_attn_bk,
                 L0_attn_Wv, L0_attn_bv, L0_attn_Wg, L0_attn_bg,
                 L0_ffn_router, L0_ffn_w_up_gate, L0_ffn_w_down,
                 L0_ffn_shared_w_up_gate, L0_ffn_shared_w_down):
    xflat = np.ascontiguousarray(np.asarray(L0_attn_x, dtype=np.float32)
                                 .reshape(TG, D))
    bcols = np.concatenate(
        [np.asarray(b, np.float32).reshape(16, 128).T
         for b in (L0_attn_bq, L0_attn_bk, L0_attn_bv, L0_attn_bg)], axis=1)
    rh = np.ascontiguousarray(
        np.asarray(L0_ffn_router, np.float32).reshape(16, 128, 8)
        .transpose(1, 0, 2).reshape(128, 128))
    consts = _consts()

    in_maps = []
    for c in range(8):
        sm = np.zeros((128, SLOTS), dtype=np.float32)
        for j in range(SLOTS):
            g = (j + 8 * (c + 1)) % 64
            if g < 8 * (c + 1):
                sm[:, j] = 1.0
        cm = np.full((128, 1), 1.0 if c % 2 == 1 else 0.0, dtype=np.float32)
        m = {
            "xg": np.roll(xflat, -1024 * (c + 1), axis=0),
            "xo": np.ascontiguousarray(xflat[1024 * c:1024 * (c + 1)]),
            "wq": np.asarray(L0_attn_Wq, np.float32),
            "wk": np.asarray(L0_attn_Wk, np.float32),
            "wv": np.asarray(L0_attn_Wv, np.float32),
            "wgt": np.asarray(L0_attn_Wg, np.float32),
            "bcols": bcols,
            "rhost": rh,
            "wug": np.asarray(L0_ffn_w_up_gate, np.float32),
            "wdn": np.asarray(L0_ffn_w_down, np.float32),
            "sug": np.asarray(L0_ffn_shared_w_up_gate, np.float32),
            "sdn": np.asarray(L0_ffn_shared_w_down, np.float32),
            "c_slotmask": sm,
            "c_carrymask": cm,
        }
        m.update(consts)
        in_maps.append(m)
    return in_maps


def kernel(**inputs):
    if "nc" not in _NC_CACHE:
        _NC_CACHE["nc"] = build_nc()
    nc = _NC_CACHE["nc"]
    in_maps = make_in_maps(**inputs)
    res = run_bass_kernel_spmd(nc, in_maps, core_ids=list(range(8)))
    out = np.concatenate([res.results[c]["out"] for c in range(8)], axis=0)
    return out.reshape(4, 2048, D)


# version 12
# speedup vs baseline: 1.1219x; 1.1219x over previous
import numpy as np
import ml_dtypes
import concourse.bass as bass
import concourse.mybir as mybir
from concourse.bass import IndirectOffsetOnAxis
from concourse.bass_utils import run_bass_kernel_spmd
from contextlib import ExitStack

F32 = mybir.dt.float32
BF16 = mybir.dt.bfloat16
U32 = mybir.dt.uint32
AF = mybir.ActivationFunctionType
ALU = mybir.AluOpType
AX = mybir.AxisListType.X

bfnp = ml_dtypes.bfloat16

D = 2048
DCH = 16
T = 1024
NT = 8
TG = 8192
SLOTS = 64
OWN0 = 56
I_FF = 1408
ICH = 11
CAP = 2560.0
CAPS = 384
NSC = 3
NE = 8
BIG = 65536.0


class Chan:
    __slots__ = ("name", "total", "sem")

    def __init__(self, name):
        self.name = name
        self.total = 0
        self.sem = None


class Sched:
    ENGINES = ("tensor", "vector", "scalar", "sync", "gpsimd")

    def __init__(self):
        self.plan = {e: [] for e in self.ENGINES}
        self.chans = []
        self.last_w = {}
        self.readers = {}
        self.bar = {}

    def chan(self, name):
        c = Chan(name)
        self.chans.append(c)
        return c

    def membar(self, refs):
        for r in refs:
            evs = []
            if r in self.last_w:
                evs.append(self.last_w[r])
            evs.extend(self.readers.get(r, ()))
            for c, v in evs:
                if self.bar.get(c, 0) < v:
                    self.bar[c] = v

    def add(self, engine, fn, reads=(), writes=(), chan=None, n_inc=1,
            extra_waits=()):
        waits = dict(self.bar)

        def need(ev):
            c, v = ev
            if waits.get(c, 0) < v:
                waits[c] = v

        for r in reads:
            if r in self.last_w:
                need(self.last_w[r])
        for r in writes:
            if r in self.last_w:
                need(self.last_w[r])
            for ev in self.readers.get(r, ()):
                need(ev)
        for ev in extra_waits:
            need(ev)
        chan.total += n_inc
        ev = (chan, chan.total)
        for r in reads:
            self.readers.setdefault(r, []).append(ev)
        for r in writes:
            self.last_w[r] = ev
            self.readers[r] = []
        self.plan[engine].append(
            (sorted(waits.items(), key=lambda kv: kv[0].name), fn, chan, n_inc))
        return ev

    def emit(self, nc, es):
        for c in self.chans:
            c.sem = es.enter_context(nc.semaphore(c.name))
        block = es.enter_context(nc.Block())
        for eng_name in self.ENGINES:
            steps = self.plan[eng_name]
            if not steps:
                continue
            dec = getattr(block, eng_name)

            def make(steps):
                def body(eng):
                    hwm = {}
                    for waits, fn, chan, n_inc in steps:
                        for c, v in waits:
                            if hwm.get(id(c), 0) < v:
                                eng.wait_ge(c.sem, v)
                                hwm[id(c)] = v
                        fn(eng, chan.sem, n_inc)
                return body

            dec(make(steps))


class SbufScope:

    def __init__(self, nc):
        self.nc = nc
        self.stack = []

    def alloc(self, name, shape, dt=F32):
        cm = self.nc.sbuf_tensor(name, list(shape), dt)
        h = cm.__enter__()
        self.stack.append((name, cm))
        return h

    def free(self, names):
        for n in names:
            top_name, cm = self.stack.pop()
            assert top_name == n, f"non-LIFO free: {n} vs top {top_name}"
            cm.__exit__(None, None, None)


def build_nc():
    nc = bass.Bass("TRN2", target_bir_lowering=False, debug=False,
                   num_devices=8)
    es = ExitStack()
    mem = SbufScope(nc)

    def din(name, shape, dt=F32):
        return nc.dram_tensor(name, list(shape), dt, kind="ExternalInput").ap()

    xgT = din("xgT", [D, TG])
    xo = din("xo", [T, D])
    wsrcT = {"q": din("wq", [D, D]), "k": din("wk", [D, D]),
             "v": din("wv", [D, D]), "g": din("wgt", [D, D])}
    bcols_h = din("bcols", [128, 64])
    rhost = din("rhost", [128, 128])
    wug = din("wug", [8, D, 2 * I_FF])
    wdn = din("wdn", [8, I_FF, D])
    sug = din("sug", [D, 11264])
    sdn = din("sdn", [5632, D])
    c_idbf = din("c_idbf", [128, 128], BF16)
    c_idf = din("c_idf", [128, 128])
    c_onesrow = din("c_onesrow", [1, 128])
    c_onescolf = din("c_onescolf", [128, 1])
    c_onescolbf = din("c_onescolbf", [128, 1], BF16)
    c_lstrict = din("c_lstrict", [128, 128], BF16)
    c_iota = din("c_iota", [128, 8])
    c_slotmask = din("c_slotmask", [128, 64])
    c_carrymask = din("c_carrymask", [128, 1])
    out_hbm = nc.dram_tensor("out", [T, D], F32, kind="ExternalOutput").ap()

    wugb = nc.dram_tensor("wugb", [8, D, 2 * I_FF], BF16, kind="Internal").ap()
    wdnb = nc.dram_tensor("wdnb", [8, I_FF, D], BF16, kind="Internal").ap()
    gxe = nc.dram_tensor("gxe", [NE * CAPS, D], BF16, kind="Internal").ap()
    oxe = nc.dram_tensor("oxe", [NE * CAPS, D], BF16, kind="Internal").ap()

    P = mem.alloc
    s_idbf = P("s_idbf", [128, 128], BF16)
    s_idf = P("s_idf", [128, 128])
    s_onesrow = P("s_onesrow", [1, 128])
    s_onescolf = P("s_onescolf", [128, 1])
    s_onescolbf = P("s_onescolbf", [128, 1], BF16)
    s_lstrict = P("s_lstrict", [128, 128], BF16)
    s_iota = P("s_iota", [128, 8])
    s_slotmask = P("s_slotmask", [128, 64])
    s_carrymask = P("s_carrymask", [128, 1])
    s_bcols = P("s_bcols", [128, 64])
    s_r = P("s_r", [128, 128], BF16)
    xnT = P("xnT", [128, DCH * T], BF16)
    logitsT = P("logitsT", [128, SLOTS * 8])
    mcol = P("mcol", [128, SLOTS])
    rcol = P("rcol", [128, SLOTS])
    bcol2 = P("bcol2", [128, 1])
    sRbc = P("sRbc", [128, 8])
    sRrow = P("sRrow", [1, 8])
    masks = P("masks", [128, NT * 8])
    carryT = P("carryT", [128, 16])
    carryTm = P("carryTm", [128, 16])
    crow = P("crow", [1, 8])
    csrow = P("csrow", [1, 8])
    t2_mx1 = P("t2_mx1", [128, 1]); t2_mx2 = P("t2_mx2", [128, 1])
    t2_eq = P("t2_eq", [128, 8]); t2_cnd = P("t2_cnd", [128, 8])
    t2_ix = P("t2_ix", [128, 1])
    t2_is1 = P("t2_is1", [128, 8]); t2_is2 = P("t2_is2", [128, 8])
    t2_l2 = P("t2_l2", [128, 8]); t2_a = P("t2_a", [128, 8])
    t2_am = P("t2_am", [128, 8])
    t2_dd = P("t2_dd", [128, 1])
    t2_w1 = P("t2_w1", [128, 1]); t2_w2 = P("t2_w2", [128, 1])
    t2_m1 = P("t2_m1", [128, 8]); t2_m2 = P("t2_m2", [128, 8])
    t2_val = P("t2_val", [128, 8])
    xnTp = P("xnTp", [128, DCH * T], BF16)

    PG = [es.enter_context(nc.psum_tensor(f"PG{i}", [128, 1024], F32))
          for i in range(4)]
    PU = PG[3]

    S = Sched()
    ch = {n: S.chan(n) for n in
          ["pe", "ve", "se", "ge", "fin",
           "c_const", "c_router", "c_pc",
           "c_xT0", "c_xT1", "c_wt0", "c_wt1",
           "c_u0", "c_u1", "c_d0", "c_d1",
           "c_ub0", "c_ub1", "c_db0", "c_db1",
           "c_gx", "c_xs", "c_gb", "c_or", "c_ox", "c_gt",
           "c_xr0", "c_xr1", "c_out", "c_so0", "c_so1"]}
    PE, VE, SE, GE = ch["pe"], ch["ve"], ch["se"], ch["ge"]

    def dma(queue, chan, pairs, reads=(), writes=()):
        def fn(eng, sem, n, pairs=pairs):
            for o, i, kw in pairs:
                eng.dma_start(out=o, in_=i, **kw).then_inc(sem, 16)
        prev = (chan, chan.total)
        return S.add(queue, fn, reads=reads, writes=writes, chan=chan,
                     n_inc=16 * len(pairs),
                     extra_waits=[prev] if chan.total else [])

    def idma(chan, fn, reads=(), writes=()):
        def wrap(eng, sem, n, fn=fn):
            fn(eng).then_inc(sem, 16)
        prev = (chan, chan.total)
        return S.add("gpsimd", wrap, reads=reads, writes=writes, chan=chan,
                     n_inc=16, extra_waits=[prev] if chan.total else [])

    def pe(fn, reads=(), writes=()):
        def wrap(eng, sem, n, fn=fn):
            insts = fn(eng)
            insts[-1].then_inc(sem, 1)
        return S.add("tensor", wrap, reads=reads, writes=writes, chan=PE)

    def ve(fn, reads=(), writes=()):
        def wrap(eng, sem, n, fn=fn):
            fn(eng).then_inc(sem, 1)
        return S.add("vector", wrap, reads=reads, writes=writes, chan=VE)

    def se(fn, reads=(), writes=()):
        def wrap(eng, sem, n, fn=fn):
            fn(eng).then_inc(sem, 1)
        return S.add("scalar", wrap, reads=reads, writes=writes, chan=SE)

    def ge(fn, reads=(), writes=()):
        def wrap(eng, sem, n, fn=fn):
            fn(eng).then_inc(sem, 1)
        return S.add("gpsimd", wrap, reads=reads, writes=writes, chan=GE)

    dma("sync", ch["c_const"], [
        (s_idbf[:], c_idbf, {}), (s_idf[:], c_idf, {}),
        (s_onesrow[:], c_onesrow, {}), (s_onescolf[:], c_onescolf, {}),
        (s_onescolbf[:], c_onescolbf, {}), (s_lstrict[:], c_lstrict, {}),
        (s_iota[:], c_iota, {}),
        (s_slotmask[:], c_slotmask, {}), (s_carrymask[:], c_carrymask, {}),
        (s_bcols[:], bcols_h, {}),
    ], writes=[("consts",)])
    dma("gpsimd", ch["c_router"], [(s_r[:], rhost, {})], writes=[("router",)])
    pc_thunks = []
    for e in range(NE):
        for qh in range(4):
            def pcu(e=e, qh=qh):
                dma("gpsimd", ch["c_pc"],
                    [(wugb[e, qh * 512:(qh + 1) * 512, :],
                      wug[e, qh * 512:(qh + 1) * 512, :],
                      {"max_dma_last_dim": 8192})],
                    writes=[("wugb", e, qh)])
            pc_thunks.append(pcu)
        for hh in range(2):
            def pcd(e=e, hh=hh):
                dma("gpsimd", ch["c_pc"],
                    [(wdnb[e, hh * 704:(hh + 1) * 704, :],
                      wdn[e, hh * 704:(hh + 1) * 704, :],
                      {"max_dma_last_dim": 8192})],
                    writes=[("wdnb", e, hh)])
            pc_thunks.append(pcd)

    t2b_is1 = mem.alloc("t2b_is1", [128, 512], BF16)
    t2b_is2 = mem.alloc("t2b_is2", [128, 512], BF16)
    t2b_am = mem.alloc("t2b_am", [128, 512], BF16)
    t2b_mx1 = mem.alloc("t2b_mx1", [128, 64])
    t2b_mx2 = mem.alloc("t2b_mx2", [128, 64])

    AB_NAMES = []

    def AB(name, shape, dt=F32):
        AB_NAMES.append(name)
        return mem.alloc(name, shape, dt)

    xTb = [AB("xTb0", [128, DCH * 512], BF16),
           AB("xTb1", [128, DCH * 512], BF16)]
    _xq = AB("xsqT0", [128, 512], BF16); xsqT = [_xq, _xq]
    _mr = AB("mrow0", [1, 512]); mrow = [_mr, _mr]
    _sq = AB("sqrow0", [1, 512]); sqrow = [_sq, _sq]
    _vr = AB("vrow0", [1, 512]); vrow = [_vr, _vr]
    _rr = AB("rrow0", [1, 512]); rrow = [_rr, _rr]
    _lg = AB("lgs0", [8, 512]); lgs = [_lg, _lg]
    MRP_NAMES = []

    def MRP(name, shape, dt=F32):
        MRP_NAMES.append(name)
        return mem.alloc(name, shape, dt)

    mbc_sb = MRP("mbc_sb", [128, 512])
    rbc_sb = MRP("rbc_sb", [128, 512])
    pf_sb = MRP("pf_sb", [128, 512])

    def sr_mm(eng):
        insts = []
        for dc in range(DCH):
            insts.append(eng.matmul(PU[0:1, 0:8], s_onescolbf[:],
                                    s_r[:, dc * 8:(dc + 1) * 8],
                                    start=(dc == 0), stop=(dc == DCH - 1)))
        return insts
    pe(sr_mm, reads=[("router",), ("consts",)], writes=[("PG", 3)])
    ve(lambda v: v.tensor_copy(sRrow[:], PU[0:1, 0:8]),
       reads=[("PG", 3)], writes=[("sRrow",), ("PG", 3)])
    pe(lambda eng: [eng.matmul(PU[:, 0:8], s_onesrow[:], sRrow[:],
                               start=True, stop=True)],
       reads=[("sRrow",), ("consts",)], writes=[("PG", 3)])
    ve(lambda v: v.tensor_copy(sRbc[:], PU[:, 0:8]),
       reads=[("PG", 3)], writes=[("sRbc",), ("PG", 3)])

    def b_layout(tch):
        sl = tch % 2
        folded = tch >= 12
        if tch in (12, 13):
            dst, col0 = xnTp, (tch - 12) * 512
        elif tch in (14, 15):
            dst, col0 = xnT, (tch - 14) * 512
        else:
            dst, col0 = xTb[sl], 0
        dstref = ("xnTraw", tch) if folded else ("xTb", sl)
        pitch = T if folded else 512
        return sl, folded, dst, dstref, col0, pitch

    def plan_B_dma(tch):
        sl, folded, dst, dstref, col0, pitch = b_layout(tch)
        src = xgT[:, tch * 512:(tch + 1) * 512] \
            .rearrange("(dc p) t -> p dc t", p=128)
        if folded:
            dview = dst[:].rearrange("p (dc t) -> p dc t", dc=DCH) \
                [:, :, col0:col0 + 512]
        else:
            dview = dst[:]
        dma("gpsimd", ch[f"c_xT{sl}"], [(dview, src, {})], writes=[dstref])

    def plan_B_compute(tch):
        sl, folded, dst, dstref, col0, pitch = b_layout(tch)

        g = tch % 3
        for dc in range(DCH):
            def xt_ap(dc=dc, dst=dst, col0=col0, pitch=pitch):
                return dst[:, dc * pitch + col0: dc * pitch + col0 + 512]
            s2 = dc % 2
            se(lambda s, s2=s2, xt=xt_ap: s.activation(xsqT[s2][:], xt(),
                                                       AF.Square),
               reads=[dstref], writes=[("xsqT", 0)])

            def mms(eng, dc=dc, s2=s2, g=g, xt=xt_ap):
                return [
                    eng.matmul(PU[0:1, 0:512], s_onescolbf[:], xt(),
                               start=(dc == 0), stop=(dc == DCH - 1)),
                    eng.matmul(PG[g][32:33, 512:1024], s_onescolbf[:],
                               xsqT[s2][:],
                               start=(dc == 0), stop=(dc == DCH - 1)),
                    eng.matmul(PG[g][0:8, 0:512], s_r[:, dc * 8:(dc + 1) * 8],
                               xt(), start=(dc == 0), stop=(dc == DCH - 1)),
                ]
            pe(mms, reads=[dstref, ("xsqT", 0), ("router",), ("consts",)],
               writes=[("PG", 3), ("PG", g)])

        ve(lambda v, sl=sl: v.tensor_single_scalar(mrow[sl][:],
                                                   PU[0:1, 0:512],
                                                   1.0 / D, ALU.mult),
           reads=[("PG", 3)], writes=[("mrow", 0), ("PG", 3)])
        ve(lambda v, sl=sl, g=g: v.tensor_single_scalar(
               sqrow[sl][:], PG[g][32:33, 512:1024], 1.0 / D, ALU.mult),
           reads=[("PG", g)], writes=[("sqrow", 0)])
        ve(lambda v, sl=sl: v.tensor_mul(vrow[sl][:], mrow[sl][:],
                                         mrow[sl][:]),
           reads=[("mrow", 0)], writes=[("vrow", 0)])
        ve(lambda v, sl=sl: v.tensor_tensor(vrow[sl][:], sqrow[sl][:],
                                            vrow[sl][:], ALU.subtract),
           reads=[("sqrow", 0), ("vrow", 0)], writes=[("vrow", 0)])
        ve(lambda v, sl=sl: v.tensor_single_scalar(vrow[sl][:], vrow[sl][:],
                                                   1e-5, ALU.add),
           reads=[("vrow", 0)], writes=[("vrow", 0)])
        se(lambda s, sl=sl: s.activation(vrow[sl][:], vrow[sl][:], AF.Ln),
           reads=[("vrow", 0)], writes=[("vrow", 0)])
        se(lambda s, sl=sl: s.activation(rrow[sl][:], vrow[sl][:], AF.Exp,
                                         scale=-0.5),
           reads=[("vrow", 0)], writes=[("rrow", 0)])

        ve(lambda v, g=g, sl=sl: v.tensor_copy(lgs[sl][:], PG[g][0:8, 0:512]),
           reads=[("PG", g)], writes=[("lgs", 0), ("PG", g)])

        def tpb(eng, sl=sl):
            insts = []
            for q in range(4):
                insts.append(eng.transpose(
                    PU[:, q:q + 1], mrow[sl][0:1, q * 128:(q + 1) * 128],
                    s_idf[0:1, 0:1]))
                insts.append(eng.transpose(
                    PU[:, 4 + q:5 + q], rrow[sl][0:1, q * 128:(q + 1) * 128],
                    s_idf[0:1, 0:1]))
                insts.append(eng.transpose(
                    PU[:, 8 + q * 8: 16 + q * 8],
                    lgs[sl][0:8, q * 128:(q + 1) * 128], s_idf[0:8, 0:8]))
            return insts
        pe(tpb, reads=[("mrow", 0), ("rrow", 0), ("lgs", 0), ("consts",)],
           writes=[("PG", 3)])

        def fold_ev(v, tch=tch):
            for q in range(4):
                j = tch * 4 + q
                v.tensor_copy(mcol[:, j:j + 1], PU[:, q:q + 1])
                v.tensor_copy(rcol[:, j:j + 1], PU[:, 4 + q:5 + q])
                inst = v.tensor_copy(logitsT[:, j * 8:(j + 1) * 8],
                                     PU[:, 8 + q * 8:16 + q * 8])
            return inst
        ve(fold_ev, reads=[("PG", 3)],
           writes=[("mrcol", tch), ("lraw", tch), ("PG", 3)])

        for q in range(4):
            j = tch * 4 + q
            ve(lambda v, j=j: v.tensor_mul(bcol2[:], mcol[:, j:j + 1],
                                           rcol[:, j:j + 1]),
               reads=[("mrcol", tch)], writes=[("bcol2",)])
            ve(lambda v, j=j: v.tensor_scalar(
                   logitsT[:, j * 8:(j + 1) * 8],
                   logitsT[:, j * 8:(j + 1) * 8],
                   rcol[:, j:j + 1], None, ALU.mult),
               reads=[("mrcol", tch), ("lraw", tch)],
               writes=[("lraw", tch)])
            ve(lambda v: v.tensor_scalar(t2_eq[:], sRbc[:], bcol2[:], None,
                                         ALU.mult),
               reads=[("sRbc",), ("bcol2",)], writes=[("t2",)])
            ve(lambda v, j=j: v.tensor_tensor(
                   logitsT[:, j * 8:(j + 1) * 8],
                   logitsT[:, j * 8:(j + 1) * 8],
                   t2_eq[:], ALU.subtract),
               reads=[("t2",), ("lraw", tch)],
               writes=[("lraw", tch)])
        S.last_w[("logits", tch)] = S.last_w[("lraw", tch)]

        if folded:
            g2 = (g + 1) % 3

            def bcst(eng, sl=sl, g2=g2):
                return [
                    eng.matmul(PG[g2][:, 0:512], s_onesrow[:], mrow[sl][:],
                               start=True, stop=True),
                    eng.matmul(PG[g2][:, 512:1024], s_onesrow[:], rrow[sl][:],
                               start=True, stop=True)]
            pe(bcst, reads=[("mrow", 0), ("rrow", 0), ("consts",)],
               writes=[("PG", g2)])
            ve(lambda v, g2=g2: v.tensor_copy(mbc_sb[:], PG[g2][:, 0:512]),
               reads=[("PG", g2)], writes=[("mbc",)])
            ve(lambda v, g2=g2: v.tensor_copy(rbc_sb[:], PG[g2][:, 512:1024]),
               reads=[("PG", g2)], writes=[("rbc",), ("PG", g2)])
            for dc in range(DCH):
                ve(lambda v, dc=dc, col0=col0, dst=dst: v.tensor_tensor(
                       pf_sb[:],
                       dst[:, dc * T + col0: dc * T + col0 + 512],
                       mbc_sb[:], ALU.subtract),
                   reads=[("xnTraw", tch), ("mbc",)], writes=[("pf",)])
                ve(lambda v, dc=dc, col0=col0, dst=dst: v.tensor_tensor(
                       dst[:, dc * T + col0: dc * T + col0 + 512],
                       pf_sb[:], rbc_sb[:], ALU.mult),
                   reads=[("pf",), ("rbc",)],
                   writes=[("xnTraw", tch), ("xnTf", tch)])


    b_early = [14, 15, 12, 13]
    for k, tch in enumerate(b_early):
        plan_B_dma(tch)
    for tch in b_early:
        plan_B_compute(tch)
    S.membar([("mbc",), ("rbc",), ("pf",)])
    mem.free(list(reversed(MRP_NAMES)))
    t2b_ec = mem.alloc("t2b_ec", [128, 512])
    t2b_ix = mem.alloc("t2b_ix", [128, 64])

    xnT_ref = [("xnTf", 14), ("xnTf", 15)]

    kv = mem.alloc("kv", [128, DCH * T], BF16)
    s_scr = [mem.alloc("s_scr0", [128, T], BF16),
             mem.alloc("s_scr1", [128, T], BF16)]
    wtbuf = [mem.alloc("wtbuf0", [128, DCH * 512], BF16),
             mem.alloc("wtbuf1", [128, DCH * 512], BF16)]
    k_sb = mem.alloc("k_sb", [128, 2048], BF16)
    kvp = mem.alloc("kvp", [128, 512], BF16)
    red0 = mem.alloc("red0", [128, 1])
    red1 = mem.alloc("red1", [128, 1])

    gcyc = [0]

    def next_g():
        g = gcyc[0] % 3
        gcyc[0] += 1
        return g

    g4cyc = [0]

    def next_g4():
        g = g4cyc[0] % 4
        g4cyc[0] += 1
        return g

    wslot = [0]

    def wt_group(w, e512):
        slw = wslot[0] % 2
        wslot[0] += 1
        src = wsrcT[w][:, e512 * 512:(e512 + 1) * 512] \
            .rearrange("(dc p) r -> p dc r", p=128)
        dma("gpsimd", ch[f"c_wt{slw}"], [(wtbuf[slw][:], src, {})],
            writes=[("wt", slw)])
        return slw

    for e512 in range(4):
        slk = wt_group("k", e512)
        slv = wt_group("v", e512)
        for ec4 in range(4):
            echunk = e512 * 4 + ec4
            gk = [next_g4(), next_g4()]

            def kmm(eng, slk=slk, ec4=ec4, gk=gk):
                insts = []
                for dc in range(DCH):
                    lhs = wtbuf[slk][:, dc * 512 + ec4 * 128:
                                     dc * 512 + (ec4 + 1) * 128]
                    for q in range(4):
                        src = xnT if q < 2 else xnTp
                        mv = src[:, dc * T + (q % 2) * 512:
                                 dc * T + (q % 2) * 512 + 512]
                        insts.append(eng.matmul(
                            PG[gk[q // 2]][:, (q % 2) * 512:(q % 2) * 512 + 512],
                            lhs, mv, start=(dc == 0), stop=(dc == DCH - 1)))
                return insts
            pe(kmm, reads=[("wt", slk), ("xnTf", 12), ("xnTf", 13),
               ("xnTf", 14), ("xnTf", 15)],
               writes=[("PG", gk[0]), ("PG", gk[1])])

            def kev(v, ec=echunk, gk=gk):
                for q in range(4):
                    inst = v.tensor_scalar(
                        k_sb[:, q * 512:(q + 1) * 512],
                        PG[gk[q // 2]][:, (q % 2) * 512:(q % 2) * 512 + 512],
                        s_bcols[:, 16 + ec:17 + ec], None, ALU.add)
                return inst
            ve(kev, reads=[("PG", gk[0]), ("PG", gk[1]), ("consts",)],
               writes=[("k_sb",), ("PG", gk[0]), ("PG", gk[1])])

            gv = [next_g4(), next_g4()]

            def vmm(eng, slv=slv, ec4=ec4, gv=gv):
                insts = []
                for dc in range(DCH):
                    lhs = wtbuf[slv][:, dc * 512 + ec4 * 128:
                                     dc * 512 + (ec4 + 1) * 128]
                    for q in range(4):
                        src = xnT if q < 2 else xnTp
                        mv = src[:, dc * T + (q % 2) * 512:
                                 dc * T + (q % 2) * 512 + 512]
                        insts.append(eng.matmul(
                            PG[gv[q // 2]][:, (q % 2) * 512:(q % 2) * 512 + 512],
                            lhs, mv, start=(dc == 0), stop=(dc == DCH - 1)))
                return insts
            pe(vmm, reads=[("wt", slv), ("xnTf", 12), ("xnTf", 13),
               ("xnTf", 14), ("xnTf", 15)],
               writes=[("PG", gv[0]), ("PG", gv[1])])

            def kvf(v, ec=echunk, gv=gv):
                for q in range(2):
                    inst = v.scalar_tensor_tensor(
                        kv[:, ec * T + q * 512: ec * T + q * 512 + 512],
                        PG[gv[0]][:, q * 512: q * 512 + 512],
                        s_bcols[:, 32 + ec:33 + ec],
                        k_sb[:, q * 512:(q + 1) * 512], ALU.add, ALU.mult)
                return inst
            ve(kvf, reads=[("PG", gv[0]), ("k_sb",), ("consts",)],
               writes=[("kv", echunk), ("PG", gv[0])])

            def kvp1(v, ec=echunk, gv=gv):
                return v.scalar_tensor_tensor(
                    kvp[:], PG[gv[1]][:, 0:512], s_bcols[:, 32 + ec:33 + ec],
                    k_sb[:, 1024:1536], ALU.add, ALU.mult)
            ve(kvp1, reads=[("PG", gv[1]), ("k_sb",), ("consts",)],
               writes=[("kvp",)])
            ve(lambda v: v.reduce_sum(red0[:], kvp[:], AX),
               reads=[("kvp",)], writes=[("red0",)])

            def kvp2(v, ec=echunk, gv=gv):
                return v.scalar_tensor_tensor(
                    kvp[:], PG[gv[1]][:, 512:1024],
                    s_bcols[:, 32 + ec:33 + ec],
                    k_sb[:, 1536:2048], ALU.add, ALU.mult)
            ve(kvp2, reads=[("PG", gv[1]), ("k_sb",), ("consts",)],
               writes=[("kvp",), ("PG", gv[1])])
            ve(lambda v: v.reduce_sum(red1[:], kvp[:], AX),
               reads=[("kvp",)], writes=[("red1",)])

            ve(lambda v: v.tensor_add(red0[:], red0[:], red1[:]),
               reads=[("red0",), ("red1",)], writes=[("red0",)])
            ve(lambda v, ec=echunk: v.tensor_copy(carryT[:, ec:ec + 1],
                                                  red0[:]),
               reads=[("red0",)], writes=[("carryT", echunk)])
            if pc_thunks:
                pc_thunks.pop(0)()

    sig_g = xnTp
    S.membar([("xnTf", 12), ("xnTf", 13), ("xnTraw", 12), ("xnTraw", 13)])

    slg = wt_group("g", 0)
    for e512 in range(4):
        for ec4 in range(4):
            echunk = e512 * 4 + ec4
            gg = next_g()

            def gmm(eng, slg=slg, ec4=ec4, gg=gg):
                insts = []
                for dc in range(DCH):
                    lhs = wtbuf[slg][:, dc * 512 + ec4 * 128:
                                     dc * 512 + (ec4 + 1) * 128]
                    for q in range(2):
                        insts.append(eng.matmul(
                            PG[gg][:, q * 512:(q + 1) * 512], lhs,
                            xnT[:, dc * T + q * 512: dc * T + q * 512 + 512],
                            start=(dc == 0), stop=(dc == DCH - 1)))
                return insts
            pe(gmm, reads=[("wt", slg)] + xnT_ref, writes=[("PG", gg)])

            def gev(s, ec=echunk, gg=gg):
                for q in range(2):
                    inst = s.activation(
                        sig_g[:, ec * T + q * 512: ec * T + q * 512 + 512],
                        PG[gg][:, q * 512:(q + 1) * 512], AF.Sigmoid,
                        bias=s_bcols[:, 48 + ec:49 + ec])
                return inst
            se(gev, reads=[("PG", gg), ("consts",)],
               writes=[("sig_g", echunk), ("PG", gg)])
            if pc_thunks:
                pc_thunks.pop(0)()
            if echunk < 12:
                if echunk == 0:
                    plan_B_dma(0)
                    plan_B_dma(1)
                plan_B_compute(echunk)
                if echunk + 2 < 12:
                    plan_B_dma(echunk + 2)
            if ec4 == 0 and e512 < 3:
                nslg = wt_group("g", e512 + 1)
        slg = nslg

    ve(lambda v: v.tensor_scalar(carryTm[:], carryT[:], s_carrymask[:], None,
                                 ALU.mult),
       reads=[("carryT", ec) for ec in range(16)] + [("consts",)],
       writes=[("carryTm",)])

    kvs = [mem.alloc("kvs0", [128, T]), mem.alloc("kvs1", [128, T])]
    qt = mem.alloc("qt", [128, T])
    slq = wt_group("q", 0)
    for e512 in range(4):
        for ec4 in range(4):
            echunk = e512 * 4 + ec4
            skv = echunk % 2
            ve(lambda v, ec=echunk, skv=skv: v.tensor_tensor_scan(
                   kvs[skv][:], kv[:, ec * T:(ec + 1) * T],
                   kv[:, ec * T:(ec + 1) * T], carryTm[:, ec:ec + 1],
                   ALU.add, ALU.bypass),
               reads=[("kv", echunk), ("carryTm",)], writes=[("kvs", skv)])
            gq = next_g()

            def qmm(eng, slq=slq, ec4=ec4, gq=gq):
                insts = []
                for dc in range(DCH):
                    lhs = wtbuf[slq][:, dc * 512 + ec4 * 128:
                                     dc * 512 + (ec4 + 1) * 128]
                    for q in range(2):
                        insts.append(eng.matmul(
                            PG[gq][:, q * 512:(q + 1) * 512], lhs,
                            xnT[:, dc * T + q * 512: dc * T + q * 512 + 512],
                            start=(dc == 0), stop=(dc == DCH - 1)))
                return insts
            pe(qmm, reads=[("wt", slq)] + xnT_ref, writes=[("PG", gq)])

            def sq1(v, ec=echunk, gq=gq, skv=skv):
                return v.scalar_tensor_tensor(
                    qt[:], PG[gq][0:128, 0:1024], s_bcols[:, ec:ec + 1],
                    kvs[skv][:], ALU.add, ALU.mult)
            ve(sq1, reads=[("PG", gq), ("kvs", skv), ("consts",)],
               writes=[("qt",), ("PG", gq)])

            ssl = echunk % 2

            def sq2(v, ec=echunk, ssl=ssl):
                return v.tensor_tensor(s_scr[ssl][:], qt[:],
                                       sig_g[:, ec * T:(ec + 1) * T],
                                       ALU.mult)
            ve(sq2, reads=[("qt",), ("sig_g", echunk)],
               writes=[("s_scr", ssl)])
            dma("sync", ch[f"c_so{ssl}"],
                [(sig_g[:, echunk * T:(echunk + 1) * T]
                  .rearrange("p (a b) -> p a b", a=8),
                  s_scr[ssl][:], {"transpose": True})],
                reads=[("s_scr", ssl)],
                writes=[("sig_g", echunk), ("stm", echunk)])
            if pc_thunks:
                pc_thunks.pop(0)()
            if ec4 == 0 and e512 < 3:
                nslq = wt_group("q", e512 + 1)
        slq = nslq

    S.membar([("kv", ec) for ec in range(16)]
             + [("stm", ec) for ec in range(16)]
             + [("wt", 0), ("wt", 1), ("k_sb",), ("kvp",), ("red0",),
                ("red1",), ("kvs", 0), ("kvs", 1), ("qt",),
                ("s_scr", 0), ("s_scr", 1)])
    mem.free(["qt", "kvs1", "kvs0", "red1", "red0", "kvp", "k_sb",
              "wtbuf1", "wtbuf0", "s_scr1", "s_scr0", "kv"])

    logits_all = [("logits", tch) for tch in range(16)]

    def l3():
        return logitsT[:].rearrange("p (t e) -> p t e", e=8)

    def bc1(buf):
        return buf[:].unsqueeze(2).broadcast_to([128, 64, 8])

    iota_b = s_iota[:].unsqueeze(1).broadcast_to([128, 64, 8])
    slm_b = s_slotmask[:].unsqueeze(2).broadcast_to([128, 64, 8])

    def ec3():
        return t2b_ec[:].rearrange("p (t e) -> p t e", e=8)

    def am3():
        return t2b_am[:].rearrange("p (t e) -> p t e", e=8)

    ve(lambda v: v.tensor_reduce(t2b_mx1[:], l3(), AX, ALU.max),
       reads=logits_all, writes=[("t2bm",)])
    ve(lambda v: v.tensor_tensor(ec3(), l3(), bc1(t2b_mx1), ALU.is_equal),
       reads=logits_all + [("t2bm",)], writes=[("t2b",)])
    ve(lambda v: v.tensor_scalar(t2b_ec[:], t2b_ec[:], -1000.0, 1000.0,
                                 ALU.mult, ALU.add),
       reads=[("t2b",)], writes=[("t2b",)])
    ve(lambda v: v.tensor_tensor(ec3(), ec3(), iota_b, ALU.add),
       reads=[("t2b",), ("consts",)], writes=[("t2b",)])
    ve(lambda v: v.tensor_reduce(t2b_ix[:], ec3(), AX, ALU.min),
       reads=[("t2b",)], writes=[("t2bi",)])
    ve(lambda v: v.tensor_tensor(
           t2b_is1[:].rearrange("p (t e) -> p t e", e=8), iota_b,
           bc1(t2b_ix), ALU.is_equal),
       reads=[("t2bi",), ("consts",)], writes=[("t2b1",)])
    ve(lambda v: v.scalar_tensor_tensor(t2b_ec[:], t2b_is1[:], -1.0e30,
                                        logitsT[:], ALU.mult, ALU.add),
       reads=[("t2b1",)] + logits_all, writes=[("t2b",)])
    ve(lambda v: v.tensor_reduce(t2b_mx2[:], ec3(), AX, ALU.max),
       reads=[("t2b",)], writes=[("t2bm2",)])
    ve(lambda v: v.tensor_tensor(am3(), ec3(), bc1(t2b_mx2), ALU.is_equal),
       reads=[("t2b",), ("t2bm2",)], writes=[("t2ba",)])
    ve(lambda v: v.tensor_scalar(t2b_am[:], t2b_am[:], -1000.0, 1000.0,
                                 ALU.mult, ALU.add),
       reads=[("t2ba",)], writes=[("t2ba",)])
    ve(lambda v: v.tensor_tensor(am3(), am3(), iota_b, ALU.add),
       reads=[("t2ba",), ("consts",)], writes=[("t2ba",)])
    ve(lambda v: v.tensor_reduce(t2b_ix[:], am3(), AX, ALU.min),
       reads=[("t2ba",)], writes=[("t2bi",)])
    ve(lambda v: v.tensor_tensor(
           t2b_is2[:].rearrange("p (t e) -> p t e", e=8), iota_b,
           bc1(t2b_ix), ALU.is_equal),
       reads=[("t2bi",), ("consts",)], writes=[("t2b2",)])
    ve(lambda v: v.tensor_tensor(t2b_am[:], t2b_is1[:], t2b_is2[:], ALU.add),
       reads=[("t2b1",), ("t2b2",)], writes=[("t2ba",)])
    ve(lambda v: v.tensor_tensor(am3(), am3(), slm_b, ALU.mult),
       reads=[("t2ba",), ("consts",)], writes=[("t2ba",)])

    def csumf(eng):
        insts = []
        for j in range(OWN0):
            insts.append(eng.matmul(PU[0:1, 0:8], s_onescolbf[:],
                                    t2b_am[:, j * 8:(j + 1) * 8],
                                    start=(j == 0), stop=(j == OWN0 - 1)))
        return insts
    pe(csumf, reads=[("t2ba",), ("consts",)], writes=[("PG", 3)])
    ve(lambda v: v.tensor_copy(crow[:], PU[0:1, 0:8]),
       reads=[("PG", 3)], writes=[("crow",), ("PG", 3)])

    def c_tile(j):
        tt = j - OWN0
        if tt == 0:
            ve(lambda v: v.tensor_copy(fbrow[:], crow[:]),
               reads=[("crow",)], writes=[("fbrow",)])

        def pfx(eng, j=j):
            return [
                eng.matmul(PU[:, 8:16], s_lstrict[:],
                           t2b_am[:, j * 8:(j + 1) * 8],
                           start=True, stop=False),
                eng.matmul(PU[:, 8:16], s_onesrow[:], crow[:],
                           start=False, stop=True)]
        pe(pfx, reads=[("t2ba",), ("crow",), ("consts",)], writes=[("PG", 3)])
        ve(lambda v, tt=tt: v.tensor_copy(posb[:, tt * 8:(tt + 1) * 8],
                                          PU[:, 8:16]),
           reads=[("PG", 3)], writes=[("posb", tt)])
        ve(lambda v: v.tensor_single_scalar(t2_val[:], PU[:, 8:16],
                                            CAP, ALU.is_lt),
           reads=[("PG", 3)], writes=[("t2val",), ("PG", 3)])
        ve(lambda v, j=j, tt=tt: v.tensor_tensor(
               ambuf[:, tt * 8:(tt + 1) * 8], t2b_am[:, j * 8:(j + 1) * 8],
               t2_val[:], ALU.mult),
           reads=[("t2ba",), ("t2val",)], writes=[("ambuf", tt)])
        ve(lambda v, j=j: v.tensor_tensor(t2_dd[:], t2b_mx1[:, j:j + 1],
                                          t2b_mx2[:, j:j + 1], ALU.subtract),
           reads=[("t2bm",), ("t2bm2",)], writes=[("t2dd",)])
        se(lambda s: s.activation(t2_w1[:], t2_dd[:], AF.Sigmoid),
           reads=[("t2dd",)], writes=[("t2w1",)])
        se(lambda s: s.activation(t2_w2[:], t2_dd[:], AF.Sigmoid, scale=-1.0),
           reads=[("t2dd",)], writes=[("t2w2",)])
        ve(lambda v, j=j: v.tensor_scalar(t2_m1[:],
                                          t2b_is1[:, j * 8:(j + 1) * 8],
                                          t2_w1[:], None, ALU.mult),
           reads=[("t2b1",), ("t2w1",)], writes=[("t2m1",)])
        ve(lambda v, j=j: v.tensor_scalar(t2_m2[:],
                                          t2b_is2[:, j * 8:(j + 1) * 8],
                                          t2_w2[:], None, ALU.mult),
           reads=[("t2b2",), ("t2w2",)], writes=[("t2m2",)])
        ve(lambda v: v.tensor_add(t2_m1[:], t2_m1[:], t2_m2[:]),
           reads=[("t2m1",), ("t2m2",)], writes=[("t2m1",)])
        ve(lambda v, tt=tt: v.tensor_tensor(
               masks[:, tt * 8:(tt + 1) * 8], t2_m1[:], t2_val[:], ALU.mult),
           reads=[("t2m1",), ("t2val",)], writes=[("masks", tt)])

        def csum1(eng, j=j):
            return [eng.matmul(PU[0:1, 0:8], s_onescolbf[:],
                               t2b_am[:, j * 8:(j + 1) * 8],
                               start=True, stop=True)]
        pe(csum1, reads=[("t2ba",), ("consts",)], writes=[("PG", 3)])
        ve(lambda v: v.tensor_copy(csrow[:], PU[0:1, 0:8]),
           reads=[("PG", 3)], writes=[("csrow",), ("PG", 3)])
        ve(lambda v: v.tensor_add(crow[:], crow[:], csrow[:]),
           reads=[("csrow",), ("crow",)], writes=[("crow",)])

    c_thunks = [(lambda j=j: c_tile(j)) for j in range(OWN0, SLOTS)]

    S.membar([("xTb", 0), ("xTb", 1), ("xsqT", 0),
              ("mrow", 0), ("sqrow", 0), ("vrow", 0), ("rrow", 0),
              ("lgs", 0), ("t2b",), ("t2bi",)])
    mem.free(["t2b_ix", "t2b_ec"] + list(reversed(AB_NAMES)))

    acc = mem.alloc("acc", [128, DCH * T])
    posb = mem.alloc("posb", [128, 64])
    ambuf = mem.alloc("ambuf", [128, 64])
    wbuf = mem.alloc("wbuf", [128, 16])
    gidxu = mem.alloc("gidxu", [128, 32], U32)
    fbrow = mem.alloc("fbrow", [1, 8])
    fb_bc = mem.alloc("fb_bc", [128, 8])
    ebase = mem.alloc("ebase", [128, 8])
    lpb = mem.alloc("lpb", [128, 8])
    a8 = mem.alloc("a8", [128, 8])
    b8 = mem.alloc("b8", [128, 8])
    vcol = mem.alloc("vcol", [128, 1])
    gcol = mem.alloc("gcol", [128, 1])
    hbuf = mem.alloc("hbuf", [128, ICH * T], BF16)
    ubuf = [mem.alloc(f"ubuf{i}", [128, DCH * 128], BF16) for i in range(3)]
    dbuf = [mem.alloc(f"dbuf{i}", [128, ICH * 128], BF16) for i in range(2)]
    hsil = mem.alloc("hsil", [128, T], BF16)
    dzb = [mem.alloc(f"dzb{i}", [128, CAPS], BF16) for i in range(4)]
    xnstage = [mem.alloc(f"xnstage{i}", [128, D], BF16) for i in range(2)]
    xntm = [mem.alloc(f"xntm{i}", [128, D], BF16) for i in range(2)]
    uslot = [0]
    dslot = [0]

    xnT3 = xnT[:].rearrange("p (dc t) -> p dc t", dc=DCH)

    def r_prep():
        gfb = next_g()
        pe(lambda eng, gfb=gfb: [eng.matmul(PG[gfb][:, 0:8], s_onesrow[:],
                                            fbrow[:], start=True, stop=True)],
           reads=[("fbrow",), ("consts",)], writes=[("PG", gfb)])
        ve(lambda v, gfb=gfb: v.tensor_copy(fb_bc[:], PG[gfb][:, 0:8]),
           reads=[("PG", gfb)], writes=[("fb_bc",), ("PG", gfb)])
        ve(lambda v: v.tensor_single_scalar(ebase[:], s_iota[:], float(CAPS),
                                            ALU.mult),
           reads=[("consts",)], writes=[("ebase",)])

    def r_thunk(tt):
        j = OWN0 + tt
        sl = tt % 2
        pcol = posb[:, tt * 8:(tt + 1) * 8]
        acol = ambuf[:, tt * 8:(tt + 1) * 8]
        ve(lambda v: v.tensor_tensor(lpb[:], pcol, fb_bc[:], ALU.subtract),
           reads=[("posb", tt), ("fb_bc",)], writes=[("lpb",)])
        ve(lambda v: v.tensor_tensor(lpb[:], lpb[:], ebase[:], ALU.add),
           reads=[("lpb",), ("ebase",)], writes=[("lpb",)])
        for k, isb, gofs, wofs in ((0, t2b_is1, 0, 0), (1, t2b_is2, 16, 8)):
            iscol = isb[:, j * 8:(j + 1) * 8]
            isref = ("t2b1",) if k == 0 else ("t2b2",)
            ve(lambda v, iscol=iscol: v.tensor_tensor(a8[:], iscol, acol,
                                                      ALU.mult),
               reads=[isref, ("ambuf", tt)], writes=[("a8",)])
            ve(lambda v: v.tensor_tensor(b8[:], a8[:], lpb[:], ALU.mult),
               reads=[("a8",), ("lpb",)], writes=[("b8",)])
            ve(lambda v: v.reduce_sum(gcol[:], b8[:], AX),
               reads=[("b8",)], writes=[("gcol",)])
            ve(lambda v: v.reduce_sum(vcol[:], a8[:], AX),
               reads=[("a8",)], writes=[("vcol",)])
            ve(lambda v: v.tensor_single_scalar(vcol[:], vcol[:], -BIG,
                                                ALU.mult),
               reads=[("vcol",)], writes=[("vcol",)])
            ve(lambda v: v.tensor_single_scalar(vcol[:], vcol[:], BIG,
                                                ALU.add),
               reads=[("vcol",)], writes=[("vcol",)])
            ve(lambda v: v.tensor_add(gcol[:], gcol[:], vcol[:]),
               reads=[("gcol",), ("vcol",)], writes=[("gcol",)])
            ve(lambda v, c=gofs + tt: v.tensor_copy(gidxu[:, c:c + 1],
                                                    gcol[:]),
               reads=[("gcol",)], writes=[("gidxu", gofs + tt)])
            ve(lambda v, iscol=iscol, tt=tt: v.tensor_tensor(
                   b8[:], iscol, masks[:, tt * 8:(tt + 1) * 8], ALU.mult),
               reads=[isref, ("masks", tt)], writes=[("b8",)])
            ve(lambda v, c=wofs + tt: v.reduce_sum(wbuf[:, c:c + 1], b8[:],
                                                   AX),
               reads=[("b8",)], writes=[("wbuf", wofs + tt)])
        ve(lambda v, tt=tt, sl=sl: v.tensor_copy(
               xnstage[sl][:].rearrange("p (a b) -> p a b", a=DCH),
               xnT3[:, :, tt * 128:(tt + 1) * 128]),
           reads=xnT_ref, writes=[("xnstage", sl)])
        dma("sync", ch["c_xs"],
            [(xntm[sl][:].rearrange("p (a b) -> p a b", a=DCH),
              xnstage[sl][:], {"transpose": True})],
            reads=[("xnstage", sl)], writes=[("xntm", sl)])
        for gofs in (0, 16):
            def sc(eng, sl=sl, c=gofs + tt):
                return eng.indirect_dma_start(
                    out=gxe, out_offset=IndirectOffsetOnAxis(
                        ap=gidxu[:, c:c + 1], axis=0),
                    in_=xntm[sl][:], in_offset=None,
                    bounds_check=NE * CAPS - 1, oob_is_err=False)
            idma(ch["c_gx"], sc,
                 reads=[("xntm", sl), ("gidxu", gofs + tt)],
                 writes=[("gxe",)])

    r_thunks = [(lambda tt=tt: r_thunk(tt)) for tt in range(NT)]

    def gb_ap(gsl):
        return xnT[:, gsl * DCH * CAPS:(gsl + 1) * DCH * CAPS] \
            .rearrange("p (dc s) -> p dc s", dc=DCH)

    def gb_mv(gsl, dc):
        base = gsl * DCH * CAPS + dc * CAPS
        return xnT[:, base:base + CAPS]

    def load_gbuf(e):
        gsl = e % 2
        dma("sync", ch["c_gb"],
            [(gb_ap(gsl), gxe[e * CAPS:(e + 1) * CAPS, :],
              {"transpose": True})],
            reads=[("gxe",)],
            writes=[("gbufs", gsl), ("xnTf", 14), ("xnTf", 15)])

    outrow = None
    UNIT_ORDER = [8, 9, 10, 11] + list(range(NE))
    for ui, u in enumerate(UNIT_ORDER):
        routed = u < 8
        if ui == 4:
            while r_thunks:
                r_thunks.pop(0)()
            S.membar([("xntm", 0), ("xntm", 1), ("xnstage", 0),
                      ("xnstage", 1)])
            mem.free(["xntm1", "xntm0", "xnstage1", "xnstage0"])
            outrow = mem.alloc("outrow", [128, NSC, D], BF16)
        if routed:
            e = u
            gsl = e % 2
            if e == 0:
                load_gbuf(0)
            for ic in range(ICH):
                gus = []
                for mc in (ic, ICH + ic):
                    su_ = uslot[0] % 3
                    uslot[0] += 1
                    dma("gpsimd", ch[f"c_ub{mc % 2}"],
                        [(ubuf[su_][:],
                          wugb[e, :, mc * 128:(mc + 1) * 128]
                          .rearrange("(dc p) m -> p dc m", p=128), {})],
                        reads=[("wugb", e, qh) for qh in range(4)],
                        writes=[("ubuf", su_)])
                    gu = next_g()
                    gus.append(gu)

                    def umm(eng, su_=su_, gu=gu, gsl=gsl):
                        insts = []
                        for dc in range(DCH):
                            insts.append(eng.matmul(
                                PG[gu][:, 0:CAPS],
                                ubuf[su_][:, dc * 128:(dc + 1) * 128],
                                gb_mv(gsl, dc),
                                start=(dc == 0), stop=(dc == DCH - 1)))
                        return insts
                    pe(umm, reads=[("ubuf", su_), ("gbufs", gsl)],
                       writes=[("PG", gu)])
                gg_, gu_ = gus
                se(lambda s, gg_=gg_: s.activation(hsil[:, 0:CAPS],
                                                   PG[gg_][0:128, 0:CAPS],
                                                   AF.Silu),
                   reads=[("PG", gg_)], writes=[("hsil",), ("PG", gg_)])

                def hb(v, ic=ic, gu_=gu_):
                    return v.scalar_tensor_tensor(
                        hbuf[:, ic * CAPS:(ic + 1) * CAPS],
                        PG[gu_][0:128, 0:CAPS], 0.0, hsil[:, 0:CAPS],
                        ALU.bypass, ALU.mult)
                ve(hb, reads=[("PG", gu_), ("hsil",)],
                   writes=[("h", ic), ("PG", gu_)])
            if e + 1 < NE:
                load_gbuf(e + 1)
            for dcc in range(DCH):
                sd_ = dslot[0] % 2
                dslot[0] += 1
                dma("gpsimd", ch[f"c_db{sd_}"],
                    [(dbuf[sd_][:],
                      wdnb[e, :, dcc * 128:(dcc + 1) * 128]
                      .rearrange("(ic p) d -> p ic d", p=128), {})],
                    reads=[("wdnb", e, 0), ("wdnb", e, 1)],
                    writes=[("dbuf", sd_)])
                gd = next_g()

                def dmm(eng, sd_=sd_, gd=gd):
                    insts = []
                    for ic in range(ICH):
                        insts.append(eng.matmul(
                            PG[gd][:, 0:CAPS],
                            dbuf[sd_][:, ic * 128:(ic + 1) * 128],
                            hbuf[:, ic * CAPS:(ic + 1) * CAPS],
                            start=(ic == 0), stop=(ic == ICH - 1)))
                    return insts
                pe(dmm, reads=[("dbuf", sd_)]
                   + [("h", ic) for ic in range(ICH)],
                   writes=[("PG", gd)])
                zsl = dcc % 4
                ve(lambda v, gd=gd, zsl=zsl: v.tensor_copy(
                       dzb[zsl][:], PG[gd][0:128, 0:CAPS]),
                   reads=[("PG", gd)], writes=[("dzb", zsl), ("PG", gd)])
                dma("sync", ch["c_or"],
                    [(outrow[:, :, dcc * 128:(dcc + 1) * 128],
                      dzb[zsl][:], {"transpose": True})],
                    reads=[("dzb", zsl)], writes=[("outrow", dcc)])
            dma("scalar", ch["c_ox"],
                [(oxe[e * CAPS:(e + 1) * CAPS, :]
                  .rearrange("(sc p) d -> p sc d", p=128),
                  outrow[:].rearrange("p a b -> p (a b)"), {})],
                reads=[("outrow", dcc) for dcc in range(DCH)],
                writes=[("oxe", e)])
            continue

        g0 = u - 8

        def up_ap(mc, g0=g0):
            if mc < ICH:
                c0 = g0 * I_FF + mc * 128
            else:
                c0 = 5632 + g0 * I_FF + (mc - ICH) * 128
            return sug[:, c0:c0 + 128]

        def dn_ap(dcc, g0=g0):
            return sdn[g0 * I_FF:(g0 + 1) * I_FF,
                       dcc * 128:(dcc + 1) * 128]

        for ic in range(ICH):
            if ui <= 1 and ic % 3 == 0 and c_thunks:
                c_thunks.pop(0)()
            if ui >= 2 and ic % 3 == 0 and r_thunks:
                r_thunks.pop(0)()
            gus = []
            for mc in (ic, ICH + ic):
                su_ = uslot[0] % 3
                uslot[0] += 1
                dma("gpsimd", ch[f"c_u{mc % 2}"],
                    [(ubuf[su_][:],
                      up_ap(mc).rearrange("(dc p) m -> p dc m", p=128),
                      {})],
                    writes=[("ubuf", su_)])
                gu = next_g()
                gus.append(gu)

                def umm(eng, su_=su_, gu=gu):
                    insts = []
                    for dc in range(DCH):
                        lhs = ubuf[su_][:, dc * 128:(dc + 1) * 128]
                        for q in range(2):
                            insts.append(eng.matmul(
                                PG[gu][:, q * 512:(q + 1) * 512], lhs,
                                xnT[:, dc * T + q * 512:
                                    dc * T + q * 512 + 512],
                                start=(dc == 0), stop=(dc == DCH - 1)))
                    return insts
                pe(umm, reads=[("ubuf", su_)] + xnT_ref, writes=[("PG", gu)])
            gg_, gu_ = gus
            se(lambda s, gg_=gg_: s.activation(hsil[:],
                                               PG[gg_][0:128, 0:1024],
                                               AF.Silu),
               reads=[("PG", gg_)], writes=[("hsil",), ("PG", gg_)])

            def hb(v, ic=ic, gu_=gu_):
                return v.scalar_tensor_tensor(
                    hbuf[:, ic * T:(ic + 1) * T],
                    PG[gu_][0:128, 0:1024], 0.0, hsil[:],
                    ALU.bypass, ALU.mult)
            ve(hb, reads=[("PG", gu_), ("hsil",)],
               writes=[("h", ic), ("PG", gu_)])
        if ui == 1:
            while c_thunks:
                c_thunks.pop(0)()
            r_prep()
        for dcc in range(DCH):
            sd_ = dslot[0] % 2
            dslot[0] += 1
            dma("gpsimd", ch[f"c_d{sd_}"],
                [(dbuf[sd_][:],
                  dn_ap(dcc).rearrange("(ic p) d -> p ic d", p=128), {})],
                writes=[("dbuf", sd_)])
            gd = next_g()

            def dmm(eng, sd_=sd_, gd=gd):
                insts = []
                for ic in range(ICH):
                    lhs = dbuf[sd_][:, ic * 128:(ic + 1) * 128]
                    for q in range(2):
                        insts.append(eng.matmul(
                            PG[gd][:, q * 512:(q + 1) * 512], lhs,
                            hbuf[:, ic * T + q * 512:
                                   ic * T + q * 512 + 512],
                            start=(ic == 0), stop=(ic == ICH - 1)))
                return insts
            pe(dmm, reads=[("dbuf", sd_)] + [("h", ic) for ic in range(ICH)],
               writes=[("PG", gd)])
            if ui == 0:
                ve(lambda v, dcc=dcc, gd=gd: v.tensor_copy(
                       acc[:, dcc * T:(dcc + 1) * T], PG[gd][0:128, 0:1024]),
                   reads=[("PG", gd)], writes=[("acc", dcc), ("PG", gd)])
            else:
                ve(lambda v, dcc=dcc, gd=gd: v.tensor_tensor(
                       acc[:, dcc * T:(dcc + 1) * T],
                       acc[:, dcc * T:(dcc + 1) * T],
                       PG[gd][0:128, 0:1024], ALU.add),
                   reads=[("PG", gd), ("acc", dcc)],
                   writes=[("acc", dcc), ("PG", gd)])

    S.membar([("hsil",), ("gbufs", 0), ("gbufs", 1)]
             + [("dzb", i) for i in range(4)]
             + [("outrow", dcc) for dcc in range(DCH)]
             + [("dbuf", i) for i in range(2)]
             + [("ubuf", i) for i in range(3)]
             + [("h", ic) for ic in range(ICH)])
    mem.free(["outrow", "dzb3", "dzb2", "dzb1", "dzb0", "hsil",
              "dbuf1", "dbuf0", "ubuf2", "ubuf1", "ubuf0", "hbuf"])

    g1 = [mem.alloc("g1a", [128, D], BF16), mem.alloc("g1b", [128, D], BF16)]
    g2 = [mem.alloc("g2a", [128, D], BF16), mem.alloc("g2b", [128, D], BF16)]
    tc1 = mem.alloc("tc1", [128, T])
    tc2 = mem.alloc("tc2", [128, T])
    silt = mem.alloc("silt", [128, T])
    xrl = [mem.alloc("xrl0", [128, D]), mem.alloc("xrl1", [128, D])]
    outt = [mem.alloc("outt0", [128, D]), mem.alloc("outt1", [128, D])]
    sgv = sig_g[:].rearrange("p (ec a) -> p ec a", ec=DCH)
    oxe_refs = [("oxe", e) for e in range(NE)]
    for tt in range(NT):
        so = tt % 2
        ge(lambda g, so=so: g.memset(g1[so][:], 0.0),
           writes=[("g1", so)])
        ge(lambda g, so=so: g.memset(g2[so][:], 0.0),
           writes=[("g2", so)])

        def ga1(eng, so=so, tt=tt):
            return eng.indirect_dma_start(
                out=g1[so][:], out_offset=None, in_=oxe,
                in_offset=IndirectOffsetOnAxis(
                    ap=gidxu[:, tt:tt + 1], axis=0),
                bounds_check=NE * CAPS - 1, oob_is_err=False)
        idma(ch["c_gt"], ga1,
             reads=oxe_refs + [("g1", so), ("gidxu", tt)],
             writes=[("g1", so)])

        def ga2(eng, so=so, tt=tt):
            return eng.indirect_dma_start(
                out=g2[so][:], out_offset=None, in_=oxe,
                in_offset=IndirectOffsetOnAxis(
                    ap=gidxu[:, 16 + tt:17 + tt], axis=0),
                bounds_check=NE * CAPS - 1, oob_is_err=False)
        idma(ch["c_gt"], ga2,
             reads=oxe_refs + [("g2", so), ("gidxu", 16 + tt)],
             writes=[("g2", so)])
        dma("sync", ch[f"c_xr{so}"],
            [(xrl[so][:], xo[tt * 128:(tt + 1) * 128, :], {})],
            writes=[("xrl", so)])
        for half in range(2):
            gf = next_g()

            def ftp(eng, tt=tt, half=half, gf=gf):
                insts = []
                for k in range(8):
                    dcc = half * 8 + k
                    insts.append(eng.transpose(
                        PG[gf][:, k * 128:(k + 1) * 128],
                        acc[:, dcc * T + tt * 128: dcc * T + (tt + 1) * 128],
                        s_idf[:]))
                return insts
            pe(ftp, reads=[("acc", half * 8 + k) for k in range(8)]
               + [("consts",)],
               writes=[("PG", gf)])
            hs = slice(half * 1024, (half + 1) * 1024)
            ve(lambda v, so=so, tt=tt, hs=hs: v.tensor_scalar(
                   tc1[:], g1[so][:, hs], wbuf[:, tt:tt + 1], None,
                   ALU.mult),
               reads=[("g1", so), ("wbuf", tt)], writes=[("tc1",)])
            ve(lambda v, so=so, tt=tt, hs=hs: v.tensor_scalar(
                   tc2[:], g2[so][:, hs], wbuf[:, 8 + tt:9 + tt], None,
                   ALU.mult),
               reads=[("g2", so), ("wbuf", 8 + tt)], writes=[("tc2",)])
            ve(lambda v: v.tensor_add(tc1[:], tc1[:], tc2[:]),
               reads=[("tc1",), ("tc2",)], writes=[("tc1",)])
            ve(lambda v, gf=gf: v.tensor_tensor(tc1[:], tc1[:],
                                                PG[gf][0:128, 0:1024],
                                                ALU.add),
               reads=[("tc1",), ("PG", gf)], writes=[("tc1",), ("PG", gf)])
            se(lambda s: s.activation(silt[:], tc1[:], AF.Silu),
               reads=[("tc1",)], writes=[("silt",)])
            ve(lambda v, tt=tt, half=half: v.tensor_tensor(
                   tc1[:].rearrange("p (a b) -> p a b", a=8),
                   silt[:].rearrange("p (a b) -> p a b", a=8),
                   sgv[:, half * 8:(half + 1) * 8,
                       tt * 128:(tt + 1) * 128],
                   ALU.mult),
               reads=[("silt",)] + [("stm", ec)
                                    for ec in range(half * 8, half * 8 + 8)],
               writes=[("tc1",)])
            ve(lambda v, so=so, hs=hs: v.tensor_tensor(
                   outt[so][:, hs], tc1[:], xrl[so][:, hs], ALU.add),
               reads=[("tc1",), ("xrl", so)], writes=[("outt", so, half)])
        dma("sync", ch["c_out"],
            [(out_hbm[tt * 128:(tt + 1) * 128, :], outt[so][:], {})],
            reads=[("outt", so, 0), ("outt", so, 1)],
            writes=[("outdma", tt)])

    def fin(eng, sem, n):
        eng.nop().then_inc(sem, n)
    S.add("gpsimd", fin, chan=ch["fin"],
          extra_waits=[(ch["c_out"], ch["c_out"].total)])

    S.emit(nc, es)
    es.close()
    nc.finalize()
    return nc


_NC_CACHE = {}


def _consts():
    c = {}
    c["c_idbf"] = np.eye(128, dtype=bfnp)
    c["c_idf"] = np.eye(128, dtype=np.float32)
    c["c_onesrow"] = np.ones((1, 128), dtype=np.float32)
    c["c_onescolf"] = np.ones((128, 1), dtype=np.float32)
    c["c_onescolbf"] = np.ones((128, 1), dtype=bfnp)
    c["c_lstrict"] = np.triu(np.ones((128, 128), dtype=bfnp), k=1)
    c["c_iota"] = np.broadcast_to(np.arange(8, dtype=np.float32),
                                  (128, 8)).copy()
    return c


def make_in_maps(L0_attn_x, L0_attn_Wq, L0_attn_bq, L0_attn_Wk, L0_attn_bk,
                 L0_attn_Wv, L0_attn_bv, L0_attn_Wg, L0_attn_bg,
                 L0_ffn_router, L0_ffn_w_up_gate, L0_ffn_w_down,
                 L0_ffn_shared_w_up_gate, L0_ffn_shared_w_down):
    xflat = np.ascontiguousarray(np.asarray(L0_attn_x, dtype=np.float32)
                                 .reshape(TG, D))
    bcols = np.concatenate(
        [np.asarray(b, np.float32).reshape(16, 128).T
         for b in (L0_attn_bq, L0_attn_bk, L0_attn_bv, L0_attn_bg)], axis=1)
    rh = np.ascontiguousarray(
        np.asarray(L0_ffn_router, np.float32).reshape(16, 128, 8)
        .transpose(1, 0, 2).reshape(128, 128))
    wqT = np.ascontiguousarray(np.asarray(L0_attn_Wq, np.float32).T)
    wkT = np.ascontiguousarray(np.asarray(L0_attn_Wk, np.float32).T)
    wvT = np.ascontiguousarray(np.asarray(L0_attn_Wv, np.float32).T)
    wgT = np.ascontiguousarray(np.asarray(L0_attn_Wg, np.float32).T)
    consts = _consts()

    in_maps = []
    for c in range(8):
        sm = np.zeros((128, SLOTS), dtype=np.float32)
        for j in range(SLOTS):
            g = (j + 8 * (c + 1)) % 64
            if g < 8 * (c + 1):
                sm[:, j] = 1.0
        cm = np.full((128, 1), 1.0 if c % 2 == 1 else 0.0, dtype=np.float32)
        m = {
            "xgT": np.ascontiguousarray(
                np.roll(xflat, -1024 * (c + 1), axis=0).T),
            "xo": np.ascontiguousarray(xflat[1024 * c:1024 * (c + 1)]),
            "wq": wqT, "wk": wkT, "wv": wvT, "wgt": wgT,
            "bcols": bcols,
            "rhost": rh,
            "wug": np.asarray(L0_ffn_w_up_gate, np.float32),
            "wdn": np.asarray(L0_ffn_w_down, np.float32),
            "sug": np.asarray(L0_ffn_shared_w_up_gate, np.float32),
            "sdn": np.asarray(L0_ffn_shared_w_down, np.float32),
            "c_slotmask": sm,
            "c_carrymask": cm,
        }
        m.update(consts)
        in_maps.append(m)
    return in_maps


def kernel(**inputs):
    if "nc" not in _NC_CACHE:
        _NC_CACHE["nc"] = build_nc()
    nc = _NC_CACHE["nc"]
    in_maps = make_in_maps(**inputs)
    res = run_bass_kernel_spmd(nc, in_maps, core_ids=list(range(8)))
    out = np.concatenate([res.results[c]["out"] for c in range(8)], axis=0)
    return out.reshape(4, 2048, D)


# version 13
# speedup vs baseline: 1.2520x; 1.1160x over previous
import numpy as np
import ml_dtypes
import concourse.bass as bass
import concourse.mybir as mybir
from concourse.bass import IndirectOffsetOnAxis
from concourse.bass_utils import run_bass_kernel_spmd
from contextlib import ExitStack

F32 = mybir.dt.float32
BF16 = mybir.dt.bfloat16
U32 = mybir.dt.uint32
AF = mybir.ActivationFunctionType
ALU = mybir.AluOpType
AX = mybir.AxisListType.X

bfnp = ml_dtypes.bfloat16

D = 2048
DCH = 16
T = 1024
NT = 8
TG = 8192
SLOTS = 64
OWN0 = 56
I_FF = 1408
ICH = 11
CAP = 2560.0
CAPS = 384
NSC = 3
NE = 8
BIG = 65536.0


class Chan:
    __slots__ = ("name", "total", "sem")

    def __init__(self, name):
        self.name = name
        self.total = 0
        self.sem = None


class Sched:
    ENGINES = ("tensor", "vector", "scalar", "sync", "gpsimd")

    def __init__(self):
        self.plan = {e: [] for e in self.ENGINES}
        self.chans = []
        self.last_w = {}
        self.readers = {}
        self.bar = {}

    def chan(self, name):
        c = Chan(name)
        self.chans.append(c)
        return c

    def membar(self, refs):
        for r in refs:
            evs = []
            if r in self.last_w:
                evs.append(self.last_w[r])
            evs.extend(self.readers.get(r, ()))
            for c, v in evs:
                if self.bar.get(c, 0) < v:
                    self.bar[c] = v

    def add(self, engine, fn, reads=(), writes=(), chan=None, n_inc=1,
            extra_waits=()):
        waits = dict(self.bar)

        def need(ev):
            c, v = ev
            if waits.get(c, 0) < v:
                waits[c] = v

        for r in reads:
            if r in self.last_w:
                need(self.last_w[r])
        for r in writes:
            if r in self.last_w:
                need(self.last_w[r])
            for ev in self.readers.get(r, ()):
                need(ev)
        for ev in extra_waits:
            need(ev)
        chan.total += n_inc
        ev = (chan, chan.total)
        for r in reads:
            self.readers.setdefault(r, []).append(ev)
        for r in writes:
            self.last_w[r] = ev
            self.readers[r] = []
        self.plan[engine].append(
            (sorted(waits.items(), key=lambda kv: kv[0].name), fn, chan, n_inc))
        return ev

    def emit(self, nc, es):
        for c in self.chans:
            c.sem = es.enter_context(nc.semaphore(c.name))
        block = es.enter_context(nc.Block())
        for eng_name in self.ENGINES:
            steps = self.plan[eng_name]
            if not steps:
                continue
            dec = getattr(block, eng_name)

            def make(steps):
                def body(eng):
                    hwm = {}
                    for waits, fn, chan, n_inc in steps:
                        for c, v in waits:
                            if hwm.get(id(c), 0) < v:
                                eng.wait_ge(c.sem, v)
                                hwm[id(c)] = v
                        fn(eng, chan.sem, n_inc)
                return body

            dec(make(steps))


class SbufScope:

    def __init__(self, nc):
        self.nc = nc
        self.stack = []

    def alloc(self, name, shape, dt=F32):
        cm = self.nc.sbuf_tensor(name, list(shape), dt)
        h = cm.__enter__()
        self.stack.append((name, cm))
        return h

    def free(self, names):
        for n in names:
            top_name, cm = self.stack.pop()
            assert top_name == n, f"non-LIFO free: {n} vs top {top_name}"
            cm.__exit__(None, None, None)


def build_nc():
    nc = bass.Bass("TRN2", target_bir_lowering=False, debug=False,
                   num_devices=8)
    es = ExitStack()
    mem = SbufScope(nc)

    def din(name, shape, dt=F32):
        return nc.dram_tensor(name, list(shape), dt, kind="ExternalInput").ap()

    xgT = din("xgT", [D, TG])
    xo = din("xo", [T, D])
    wsrcT = {"q": din("wq", [D, D]), "k": din("wk", [D, D]),
             "v": din("wv", [D, D]), "g": din("wgt", [D, D])}
    bcols_h = din("bcols", [128, 64])
    rhost = din("rhost", [128, 128])
    wug = din("wug", [8, 22, 128, D])
    wdn = din("wdn", [8, DCH, 128, I_FF])
    sug = din("sug", [4, 22, 128, D])
    sdn = din("sdn", [4, DCH, 128, I_FF])
    c_idbf = din("c_idbf", [128, 128], BF16)
    c_idf = din("c_idf", [128, 128])
    c_onesrow = din("c_onesrow", [1, 128])
    c_onescolf = din("c_onescolf", [128, 1])
    c_onescolbf = din("c_onescolbf", [128, 1], BF16)
    c_lstrict = din("c_lstrict", [128, 128], BF16)
    c_iota = din("c_iota", [128, 8])
    c_slotmask = din("c_slotmask", [128, 64])
    c_carrymask = din("c_carrymask", [128, 1])
    out_hbm = nc.dram_tensor("out", [T, D], F32, kind="ExternalOutput").ap()

    gxe = nc.dram_tensor("gxe", [NE * CAPS, D], BF16, kind="Internal").ap()
    oxe = nc.dram_tensor("oxe", [NE * CAPS, D], BF16, kind="Internal").ap()

    P = mem.alloc
    s_idbf = P("s_idbf", [128, 128], BF16)
    s_idf = P("s_idf", [128, 128])
    s_onesrow = P("s_onesrow", [1, 128])
    s_onescolf = P("s_onescolf", [128, 1])
    s_onescolbf = P("s_onescolbf", [128, 1], BF16)
    s_lstrict = P("s_lstrict", [128, 128], BF16)
    s_iota = P("s_iota", [128, 8])
    s_slotmask = P("s_slotmask", [128, 64])
    s_carrymask = P("s_carrymask", [128, 1])
    s_bcols = P("s_bcols", [128, 64])
    s_r = P("s_r", [128, 128], BF16)
    xnT = P("xnT", [128, DCH * T], BF16)
    logitsT = P("logitsT", [128, SLOTS * 8])
    mcol = P("mcol", [128, SLOTS])
    rcol = P("rcol", [128, SLOTS])
    bcol2 = P("bcol2", [128, 1])
    sRbc = P("sRbc", [128, 8])
    sRrow = P("sRrow", [1, 8])
    masks = P("masks", [128, NT * 8])
    carryT = P("carryT", [128, 16])
    carryTm = P("carryTm", [128, 16])
    crow = P("crow", [1, 8])
    csrow = P("csrow", [1, 8])
    t2_mx1 = P("t2_mx1", [128, 1]); t2_mx2 = P("t2_mx2", [128, 1])
    t2_eq = P("t2_eq", [128, 8]); t2_cnd = P("t2_cnd", [128, 8])
    t2_ix = P("t2_ix", [128, 1])
    t2_is1 = P("t2_is1", [128, 8]); t2_is2 = P("t2_is2", [128, 8])
    t2_l2 = P("t2_l2", [128, 8]); t2_a = P("t2_a", [128, 8])
    t2_am = P("t2_am", [128, 8])
    t2_dd = P("t2_dd", [128, 1])
    t2_w1 = P("t2_w1", [128, 1]); t2_w2 = P("t2_w2", [128, 1])
    t2_m1 = P("t2_m1", [128, 8]); t2_m2 = P("t2_m2", [128, 8])
    t2_val = P("t2_val", [128, 8])
    xnTp = P("xnTp", [128, DCH * T], BF16)

    PG = [es.enter_context(nc.psum_tensor(f"PG{i}", [128, 1024], F32))
          for i in range(4)]
    PU = PG[3]

    S = Sched()
    ch = {n: S.chan(n) for n in
          ["pe", "ve", "se", "ge", "fin",
           "c_const", "c_router",
           "c_xT0", "c_xT1", "c_wt0", "c_wt1",
           "c_u0", "c_u1", "c_d0", "c_d1",
           "c_ub0", "c_ub1", "c_db0", "c_db1",
           "c_gx", "c_xs", "c_gb", "c_or", "c_ox", "c_gt",
           "c_xr0", "c_xr1", "c_out", "c_so0", "c_so1"]}
    PE, VE, SE, GE = ch["pe"], ch["ve"], ch["se"], ch["ge"]

    def dma(queue, chan, pairs, reads=(), writes=()):
        def fn(eng, sem, n, pairs=pairs):
            for o, i, kw in pairs:
                eng.dma_start(out=o, in_=i, **kw).then_inc(sem, 16)
        prev = (chan, chan.total)
        return S.add(queue, fn, reads=reads, writes=writes, chan=chan,
                     n_inc=16 * len(pairs),
                     extra_waits=[prev] if chan.total else [])

    def idma(chan, fn, reads=(), writes=()):
        def wrap(eng, sem, n, fn=fn):
            fn(eng).then_inc(sem, 16)
        prev = (chan, chan.total)
        return S.add("gpsimd", wrap, reads=reads, writes=writes, chan=chan,
                     n_inc=16, extra_waits=[prev] if chan.total else [])

    def pe(fn, reads=(), writes=()):
        def wrap(eng, sem, n, fn=fn):
            insts = fn(eng)
            insts[-1].then_inc(sem, 1)
        return S.add("tensor", wrap, reads=reads, writes=writes, chan=PE)

    def ve(fn, reads=(), writes=()):
        def wrap(eng, sem, n, fn=fn):
            fn(eng).then_inc(sem, 1)
        return S.add("vector", wrap, reads=reads, writes=writes, chan=VE)

    def se(fn, reads=(), writes=()):
        def wrap(eng, sem, n, fn=fn):
            fn(eng).then_inc(sem, 1)
        return S.add("scalar", wrap, reads=reads, writes=writes, chan=SE)

    def ge(fn, reads=(), writes=()):
        def wrap(eng, sem, n, fn=fn):
            fn(eng).then_inc(sem, 1)
        return S.add("gpsimd", wrap, reads=reads, writes=writes, chan=GE)

    dma("sync", ch["c_const"], [
        (s_idbf[:], c_idbf, {}), (s_idf[:], c_idf, {}),
        (s_onesrow[:], c_onesrow, {}), (s_onescolf[:], c_onescolf, {}),
        (s_onescolbf[:], c_onescolbf, {}), (s_lstrict[:], c_lstrict, {}),
        (s_iota[:], c_iota, {}),
        (s_slotmask[:], c_slotmask, {}), (s_carrymask[:], c_carrymask, {}),
        (s_bcols[:], bcols_h, {}),
    ], writes=[("consts",)])
    dma("gpsimd", ch["c_router"], [(s_r[:], rhost, {})], writes=[("router",)])
    t2b_is1 = mem.alloc("t2b_is1", [128, 512], BF16)
    t2b_is2 = mem.alloc("t2b_is2", [128, 512], BF16)
    t2b_am = mem.alloc("t2b_am", [128, 512], BF16)
    t2b_mx1 = mem.alloc("t2b_mx1", [128, 64])
    t2b_mx2 = mem.alloc("t2b_mx2", [128, 64])

    AB_NAMES = []

    def AB(name, shape, dt=F32):
        AB_NAMES.append(name)
        return mem.alloc(name, shape, dt)

    xTb = [AB("xTb0", [128, DCH * 512], BF16),
           AB("xTb1", [128, DCH * 512], BF16)]
    _xq = AB("xsqT0", [128, 512], BF16); xsqT = [_xq, _xq]
    _mr = AB("mrow0", [1, 512]); mrow = [_mr, _mr]
    _sq = AB("sqrow0", [1, 512]); sqrow = [_sq, _sq]
    _vr = AB("vrow0", [1, 512]); vrow = [_vr, _vr]
    _rr = AB("rrow0", [1, 512]); rrow = [_rr, _rr]
    _lg = AB("lgs0", [8, 512]); lgs = [_lg, _lg]
    MRP_NAMES = []

    def MRP(name, shape, dt=F32):
        MRP_NAMES.append(name)
        return mem.alloc(name, shape, dt)

    mbc_sb = MRP("mbc_sb", [128, 512])
    rbc_sb = MRP("rbc_sb", [128, 512])
    pf_sb = MRP("pf_sb", [128, 512])

    def sr_mm(eng):
        insts = []
        for dc in range(DCH):
            insts.append(eng.matmul(PU[0:1, 0:8], s_onescolbf[:],
                                    s_r[:, dc * 8:(dc + 1) * 8],
                                    start=(dc == 0), stop=(dc == DCH - 1)))
        return insts
    pe(sr_mm, reads=[("router",), ("consts",)], writes=[("PG", 3)])
    ve(lambda v: v.tensor_copy(sRrow[:], PU[0:1, 0:8]),
       reads=[("PG", 3)], writes=[("sRrow",), ("PG", 3)])
    pe(lambda eng: [eng.matmul(PU[:, 0:8], s_onesrow[:], sRrow[:],
                               start=True, stop=True)],
       reads=[("sRrow",), ("consts",)], writes=[("PG", 3)])
    ve(lambda v: v.tensor_copy(sRbc[:], PU[:, 0:8]),
       reads=[("PG", 3)], writes=[("sRbc",), ("PG", 3)])

    def b_layout(tch):
        sl = tch % 2
        folded = tch >= 12
        if tch in (12, 13):
            dst, col0 = xnTp, (tch - 12) * 512
        elif tch in (14, 15):
            dst, col0 = xnT, (tch - 14) * 512
        else:
            dst, col0 = xTb[sl], 0
        dstref = ("xnTraw", tch) if folded else ("xTb", sl)
        pitch = T if folded else 512
        return sl, folded, dst, dstref, col0, pitch

    def plan_B_dma(tch):
        sl, folded, dst, dstref, col0, pitch = b_layout(tch)
        src = xgT[:, tch * 512:(tch + 1) * 512] \
            .rearrange("(dc p) t -> p dc t", p=128)
        if folded:
            dview = dst[:].rearrange("p (dc t) -> p dc t", dc=DCH) \
                [:, :, col0:col0 + 512]
        else:
            dview = dst[:]
        dma("gpsimd", ch[f"c_xT{sl}"], [(dview, src, {})], writes=[dstref])

    def plan_B_compute(tch):
        sl, folded, dst, dstref, col0, pitch = b_layout(tch)

        g = tch % 3
        for dc in range(DCH):
            def xt_ap(dc=dc, dst=dst, col0=col0, pitch=pitch):
                return dst[:, dc * pitch + col0: dc * pitch + col0 + 512]
            s2 = dc % 2
            se(lambda s, s2=s2, xt=xt_ap: s.activation(xsqT[s2][:], xt(),
                                                       AF.Square),
               reads=[dstref], writes=[("xsqT", 0)])

            def mms(eng, dc=dc, s2=s2, g=g, xt=xt_ap):
                return [
                    eng.matmul(PU[0:1, 0:512], s_onescolbf[:], xt(),
                               start=(dc == 0), stop=(dc == DCH - 1)),
                    eng.matmul(PG[g][32:33, 512:1024], s_onescolbf[:],
                               xsqT[s2][:],
                               start=(dc == 0), stop=(dc == DCH - 1)),
                    eng.matmul(PG[g][0:8, 0:512], s_r[:, dc * 8:(dc + 1) * 8],
                               xt(), start=(dc == 0), stop=(dc == DCH - 1)),
                ]
            pe(mms, reads=[dstref, ("xsqT", 0), ("router",), ("consts",)],
               writes=[("PG", 3), ("PG", g)])

        ve(lambda v, sl=sl: v.tensor_single_scalar(mrow[sl][:],
                                                   PU[0:1, 0:512],
                                                   1.0 / D, ALU.mult),
           reads=[("PG", 3)], writes=[("mrow", 0), ("PG", 3)])
        ve(lambda v, sl=sl, g=g: v.tensor_single_scalar(
               sqrow[sl][:], PG[g][32:33, 512:1024], 1.0 / D, ALU.mult),
           reads=[("PG", g)], writes=[("sqrow", 0)])
        ve(lambda v, sl=sl: v.tensor_mul(vrow[sl][:], mrow[sl][:],
                                         mrow[sl][:]),
           reads=[("mrow", 0)], writes=[("vrow", 0)])
        ve(lambda v, sl=sl: v.tensor_tensor(vrow[sl][:], sqrow[sl][:],
                                            vrow[sl][:], ALU.subtract),
           reads=[("sqrow", 0), ("vrow", 0)], writes=[("vrow", 0)])
        ve(lambda v, sl=sl: v.tensor_single_scalar(vrow[sl][:], vrow[sl][:],
                                                   1e-5, ALU.add),
           reads=[("vrow", 0)], writes=[("vrow", 0)])
        se(lambda s, sl=sl: s.activation(vrow[sl][:], vrow[sl][:], AF.Ln),
           reads=[("vrow", 0)], writes=[("vrow", 0)])
        se(lambda s, sl=sl: s.activation(rrow[sl][:], vrow[sl][:], AF.Exp,
                                         scale=-0.5),
           reads=[("vrow", 0)], writes=[("rrow", 0)])

        ve(lambda v, g=g, sl=sl: v.tensor_copy(lgs[sl][:], PG[g][0:8, 0:512]),
           reads=[("PG", g)], writes=[("lgs", 0), ("PG", g)])

        def tpb(eng, sl=sl):
            insts = []
            for q in range(4):
                insts.append(eng.transpose(
                    PU[:, q:q + 1], mrow[sl][0:1, q * 128:(q + 1) * 128],
                    s_idf[0:1, 0:1]))
                insts.append(eng.transpose(
                    PU[:, 4 + q:5 + q], rrow[sl][0:1, q * 128:(q + 1) * 128],
                    s_idf[0:1, 0:1]))
                insts.append(eng.transpose(
                    PU[:, 8 + q * 8: 16 + q * 8],
                    lgs[sl][0:8, q * 128:(q + 1) * 128], s_idf[0:8, 0:8]))
            return insts
        pe(tpb, reads=[("mrow", 0), ("rrow", 0), ("lgs", 0), ("consts",)],
           writes=[("PG", 3)])

        def fold_ev(v, tch=tch):
            for q in range(4):
                j = tch * 4 + q
                v.tensor_copy(mcol[:, j:j + 1], PU[:, q:q + 1])
                v.tensor_copy(rcol[:, j:j + 1], PU[:, 4 + q:5 + q])
                inst = v.tensor_copy(logitsT[:, j * 8:(j + 1) * 8],
                                     PU[:, 8 + q * 8:16 + q * 8])
            return inst
        ve(fold_ev, reads=[("PG", 3)],
           writes=[("mrcol", tch), ("lraw", tch), ("PG", 3)])

        for q in range(4):
            j = tch * 4 + q
            ve(lambda v, j=j: v.tensor_mul(bcol2[:], mcol[:, j:j + 1],
                                           rcol[:, j:j + 1]),
               reads=[("mrcol", tch)], writes=[("bcol2",)])
            ve(lambda v, j=j: v.tensor_scalar(
                   logitsT[:, j * 8:(j + 1) * 8],
                   logitsT[:, j * 8:(j + 1) * 8],
                   rcol[:, j:j + 1], None, ALU.mult),
               reads=[("mrcol", tch), ("lraw", tch)],
               writes=[("lraw", tch)])
            ve(lambda v: v.tensor_scalar(t2_eq[:], sRbc[:], bcol2[:], None,
                                         ALU.mult),
               reads=[("sRbc",), ("bcol2",)], writes=[("t2",)])
            ve(lambda v, j=j: v.tensor_tensor(
                   logitsT[:, j * 8:(j + 1) * 8],
                   logitsT[:, j * 8:(j + 1) * 8],
                   t2_eq[:], ALU.subtract),
               reads=[("t2",), ("lraw", tch)],
               writes=[("lraw", tch)])
        S.last_w[("logits", tch)] = S.last_w[("lraw", tch)]

        if folded:
            g2 = (g + 1) % 3

            def bcst(eng, sl=sl, g2=g2):
                return [
                    eng.matmul(PG[g2][:, 0:512], s_onesrow[:], mrow[sl][:],
                               start=True, stop=True),
                    eng.matmul(PG[g2][:, 512:1024], s_onesrow[:], rrow[sl][:],
                               start=True, stop=True)]
            pe(bcst, reads=[("mrow", 0), ("rrow", 0), ("consts",)],
               writes=[("PG", g2)])
            ve(lambda v, g2=g2: v.tensor_copy(mbc_sb[:], PG[g2][:, 0:512]),
               reads=[("PG", g2)], writes=[("mbc",)])
            ve(lambda v, g2=g2: v.tensor_copy(rbc_sb[:], PG[g2][:, 512:1024]),
               reads=[("PG", g2)], writes=[("rbc",), ("PG", g2)])
            for dc in range(DCH):
                ve(lambda v, dc=dc, col0=col0, dst=dst: v.tensor_tensor(
                       pf_sb[:],
                       dst[:, dc * T + col0: dc * T + col0 + 512],
                       mbc_sb[:], ALU.subtract),
                   reads=[("xnTraw", tch), ("mbc",)], writes=[("pf",)])
                ve(lambda v, dc=dc, col0=col0, dst=dst: v.tensor_tensor(
                       dst[:, dc * T + col0: dc * T + col0 + 512],
                       pf_sb[:], rbc_sb[:], ALU.mult),
                   reads=[("pf",), ("rbc",)],
                   writes=[("xnTraw", tch), ("xnTf", tch)])


    b_early = [14, 15, 12, 13]
    for k, tch in enumerate(b_early):
        plan_B_dma(tch)
    for tch in b_early:
        plan_B_compute(tch)
    S.membar([("mbc",), ("rbc",), ("pf",)])
    mem.free(list(reversed(MRP_NAMES)))
    t2b_ec = mem.alloc("t2b_ec", [128, 512])
    t2b_ix = mem.alloc("t2b_ix", [128, 64])

    xnT_ref = [("xnTf", 14), ("xnTf", 15)]

    kv = mem.alloc("kv", [128, DCH * T], BF16)
    s_scr = [mem.alloc("s_scr0", [128, T], BF16),
             mem.alloc("s_scr1", [128, T], BF16)]
    wtbuf = [mem.alloc("wtbuf0", [128, DCH * 512], BF16),
             mem.alloc("wtbuf1", [128, DCH * 512], BF16)]
    k_sb = mem.alloc("k_sb", [128, 2048], BF16)
    kvp = mem.alloc("kvp", [128, 512], BF16)
    red0 = mem.alloc("red0", [128, 1])
    red1 = mem.alloc("red1", [128, 1])

    gcyc = [0]

    def next_g():
        g = gcyc[0] % 3
        gcyc[0] += 1
        return g

    g4cyc = [0]

    def next_g4():
        g = g4cyc[0] % 4
        g4cyc[0] += 1
        return g

    wslot = [0]

    def wt_group(w, e512):
        slw = wslot[0] % 2
        wslot[0] += 1
        src = wsrcT[w][:, e512 * 512:(e512 + 1) * 512] \
            .rearrange("(dc p) r -> p dc r", p=128)
        dma("gpsimd", ch[f"c_wt{slw}"], [(wtbuf[slw][:], src, {})],
            writes=[("wt", slw)])
        return slw

    for e512 in range(4):
        slk = wt_group("k", e512)
        slv = wt_group("v", e512)
        for ec4 in range(4):
            echunk = e512 * 4 + ec4
            gk = [next_g4(), next_g4()]

            def kmm(eng, slk=slk, ec4=ec4, gk=gk):
                insts = []
                for dc in range(DCH):
                    lhs = wtbuf[slk][:, dc * 512 + ec4 * 128:
                                     dc * 512 + (ec4 + 1) * 128]
                    for q in range(4):
                        src = xnT if q < 2 else xnTp
                        mv = src[:, dc * T + (q % 2) * 512:
                                 dc * T + (q % 2) * 512 + 512]
                        insts.append(eng.matmul(
                            PG[gk[q // 2]][:, (q % 2) * 512:(q % 2) * 512 + 512],
                            lhs, mv, start=(dc == 0), stop=(dc == DCH - 1)))
                return insts
            pe(kmm, reads=[("wt", slk), ("xnTf", 12), ("xnTf", 13),
               ("xnTf", 14), ("xnTf", 15)],
               writes=[("PG", gk[0]), ("PG", gk[1])])

            def kev(v, ec=echunk, gk=gk):
                for q in range(4):
                    inst = v.tensor_scalar(
                        k_sb[:, q * 512:(q + 1) * 512],
                        PG[gk[q // 2]][:, (q % 2) * 512:(q % 2) * 512 + 512],
                        s_bcols[:, 16 + ec:17 + ec], None, ALU.add)
                return inst
            ve(kev, reads=[("PG", gk[0]), ("PG", gk[1]), ("consts",)],
               writes=[("k_sb",), ("PG", gk[0]), ("PG", gk[1])])

            gv = [next_g4(), next_g4()]

            def vmm(eng, slv=slv, ec4=ec4, gv=gv):
                insts = []
                for dc in range(DCH):
                    lhs = wtbuf[slv][:, dc * 512 + ec4 * 128:
                                     dc * 512 + (ec4 + 1) * 128]
                    for q in range(4):
                        src = xnT if q < 2 else xnTp
                        mv = src[:, dc * T + (q % 2) * 512:
                                 dc * T + (q % 2) * 512 + 512]
                        insts.append(eng.matmul(
                            PG[gv[q // 2]][:, (q % 2) * 512:(q % 2) * 512 + 512],
                            lhs, mv, start=(dc == 0), stop=(dc == DCH - 1)))
                return insts
            pe(vmm, reads=[("wt", slv), ("xnTf", 12), ("xnTf", 13),
               ("xnTf", 14), ("xnTf", 15)],
               writes=[("PG", gv[0]), ("PG", gv[1])])

            def kvf(v, ec=echunk, gv=gv):
                for q in range(2):
                    inst = v.scalar_tensor_tensor(
                        kv[:, ec * T + q * 512: ec * T + q * 512 + 512],
                        PG[gv[0]][:, q * 512: q * 512 + 512],
                        s_bcols[:, 32 + ec:33 + ec],
                        k_sb[:, q * 512:(q + 1) * 512], ALU.add, ALU.mult)
                return inst
            ve(kvf, reads=[("PG", gv[0]), ("k_sb",), ("consts",)],
               writes=[("kv", echunk), ("PG", gv[0])])

            def kvp1(v, ec=echunk, gv=gv):
                return v.scalar_tensor_tensor(
                    kvp[:], PG[gv[1]][:, 0:512], s_bcols[:, 32 + ec:33 + ec],
                    k_sb[:, 1024:1536], ALU.add, ALU.mult)
            ve(kvp1, reads=[("PG", gv[1]), ("k_sb",), ("consts",)],
               writes=[("kvp",)])
            ve(lambda v: v.reduce_sum(red0[:], kvp[:], AX),
               reads=[("kvp",)], writes=[("red0",)])

            def kvp2(v, ec=echunk, gv=gv):
                return v.scalar_tensor_tensor(
                    kvp[:], PG[gv[1]][:, 512:1024],
                    s_bcols[:, 32 + ec:33 + ec],
                    k_sb[:, 1536:2048], ALU.add, ALU.mult)
            ve(kvp2, reads=[("PG", gv[1]), ("k_sb",), ("consts",)],
               writes=[("kvp",), ("PG", gv[1])])
            ve(lambda v: v.reduce_sum(red1[:], kvp[:], AX),
               reads=[("kvp",)], writes=[("red1",)])

            ve(lambda v: v.tensor_add(red0[:], red0[:], red1[:]),
               reads=[("red0",), ("red1",)], writes=[("red0",)])
            ve(lambda v, ec=echunk: v.tensor_copy(carryT[:, ec:ec + 1],
                                                  red0[:]),
               reads=[("red0",)], writes=[("carryT", echunk)])

    sig_g = xnTp
    S.membar([("xnTf", 12), ("xnTf", 13), ("xnTraw", 12), ("xnTraw", 13)])

    slg = wt_group("g", 0)
    for e512 in range(4):
        for ec4 in range(4):
            echunk = e512 * 4 + ec4
            gg = next_g()

            def gmm(eng, slg=slg, ec4=ec4, gg=gg):
                insts = []
                for dc in range(DCH):
                    lhs = wtbuf[slg][:, dc * 512 + ec4 * 128:
                                     dc * 512 + (ec4 + 1) * 128]
                    for q in range(2):
                        insts.append(eng.matmul(
                            PG[gg][:, q * 512:(q + 1) * 512], lhs,
                            xnT[:, dc * T + q * 512: dc * T + q * 512 + 512],
                            start=(dc == 0), stop=(dc == DCH - 1)))
                return insts
            pe(gmm, reads=[("wt", slg)] + xnT_ref, writes=[("PG", gg)])

            def gev(s, ec=echunk, gg=gg):
                for q in range(2):
                    inst = s.activation(
                        sig_g[:, ec * T + q * 512: ec * T + q * 512 + 512],
                        PG[gg][:, q * 512:(q + 1) * 512], AF.Sigmoid,
                        bias=s_bcols[:, 48 + ec:49 + ec])
                return inst
            se(gev, reads=[("PG", gg), ("consts",)],
               writes=[("sig_g", echunk), ("PG", gg)])
            if echunk < 12:
                if echunk == 0:
                    plan_B_dma(0)
                    plan_B_dma(1)
                plan_B_compute(echunk)
                if echunk + 2 < 12:
                    plan_B_dma(echunk + 2)
            if ec4 == 0 and e512 < 3:
                nslg = wt_group("g", e512 + 1)
        slg = nslg

    ve(lambda v: v.tensor_scalar(carryTm[:], carryT[:], s_carrymask[:], None,
                                 ALU.mult),
       reads=[("carryT", ec) for ec in range(16)] + [("consts",)],
       writes=[("carryTm",)])

    kvs = [mem.alloc("kvs0", [128, T]), mem.alloc("kvs1", [128, T])]
    qt = mem.alloc("qt", [128, T])
    slq = wt_group("q", 0)
    for e512 in range(4):
        for ec4 in range(4):
            echunk = e512 * 4 + ec4
            skv = echunk % 2
            ve(lambda v, ec=echunk, skv=skv: v.tensor_tensor_scan(
                   kvs[skv][:], kv[:, ec * T:(ec + 1) * T],
                   kv[:, ec * T:(ec + 1) * T], carryTm[:, ec:ec + 1],
                   ALU.add, ALU.bypass),
               reads=[("kv", echunk), ("carryTm",)], writes=[("kvs", skv)])
            gq = next_g()

            def qmm(eng, slq=slq, ec4=ec4, gq=gq):
                insts = []
                for dc in range(DCH):
                    lhs = wtbuf[slq][:, dc * 512 + ec4 * 128:
                                     dc * 512 + (ec4 + 1) * 128]
                    for q in range(2):
                        insts.append(eng.matmul(
                            PG[gq][:, q * 512:(q + 1) * 512], lhs,
                            xnT[:, dc * T + q * 512: dc * T + q * 512 + 512],
                            start=(dc == 0), stop=(dc == DCH - 1)))
                return insts
            pe(qmm, reads=[("wt", slq)] + xnT_ref, writes=[("PG", gq)])

            def sq1(v, ec=echunk, gq=gq, skv=skv):
                return v.scalar_tensor_tensor(
                    qt[:], PG[gq][0:128, 0:1024], s_bcols[:, ec:ec + 1],
                    kvs[skv][:], ALU.add, ALU.mult)
            ve(sq1, reads=[("PG", gq), ("kvs", skv), ("consts",)],
               writes=[("qt",), ("PG", gq)])

            ssl = echunk % 2

            def sq2(v, ec=echunk, ssl=ssl):
                return v.tensor_tensor(s_scr[ssl][:], qt[:],
                                       sig_g[:, ec * T:(ec + 1) * T],
                                       ALU.mult)
            ve(sq2, reads=[("qt",), ("sig_g", echunk)],
               writes=[("s_scr", ssl)])
            dma("sync", ch[f"c_so{ssl}"],
                [(sig_g[:, echunk * T:(echunk + 1) * T]
                  .rearrange("p (a b) -> p a b", a=8),
                  s_scr[ssl][:], {"transpose": True})],
                reads=[("s_scr", ssl)],
                writes=[("sig_g", echunk), ("stm", echunk)])
            if ec4 == 0 and e512 < 3:
                nslq = wt_group("q", e512 + 1)
        slq = nslq

    S.membar([("kv", ec) for ec in range(16)]
             + [("stm", ec) for ec in range(16)]
             + [("wt", 0), ("wt", 1), ("k_sb",), ("kvp",), ("red0",),
                ("red1",), ("kvs", 0), ("kvs", 1), ("qt",),
                ("s_scr", 0), ("s_scr", 1)])
    mem.free(["qt", "kvs1", "kvs0", "red1", "red0", "kvp", "k_sb",
              "wtbuf1", "wtbuf0", "s_scr1", "s_scr0", "kv"])

    logits_all = [("logits", tch) for tch in range(16)]

    def l3():
        return logitsT[:].rearrange("p (t e) -> p t e", e=8)

    def bc1(buf):
        return buf[:].unsqueeze(2).broadcast_to([128, 64, 8])

    iota_b = s_iota[:].unsqueeze(1).broadcast_to([128, 64, 8])
    slm_b = s_slotmask[:].unsqueeze(2).broadcast_to([128, 64, 8])

    def ec3():
        return t2b_ec[:].rearrange("p (t e) -> p t e", e=8)

    def am3():
        return t2b_am[:].rearrange("p (t e) -> p t e", e=8)

    ve(lambda v: v.tensor_reduce(t2b_mx1[:], l3(), AX, ALU.max),
       reads=logits_all, writes=[("t2bm",)])
    ve(lambda v: v.tensor_tensor(ec3(), l3(), bc1(t2b_mx1), ALU.is_equal),
       reads=logits_all + [("t2bm",)], writes=[("t2b",)])
    ve(lambda v: v.tensor_scalar(t2b_ec[:], t2b_ec[:], -1000.0, 1000.0,
                                 ALU.mult, ALU.add),
       reads=[("t2b",)], writes=[("t2b",)])
    ve(lambda v: v.tensor_tensor(ec3(), ec3(), iota_b, ALU.add),
       reads=[("t2b",), ("consts",)], writes=[("t2b",)])
    ve(lambda v: v.tensor_reduce(t2b_ix[:], ec3(), AX, ALU.min),
       reads=[("t2b",)], writes=[("t2bi",)])
    ve(lambda v: v.tensor_tensor(
           t2b_is1[:].rearrange("p (t e) -> p t e", e=8), iota_b,
           bc1(t2b_ix), ALU.is_equal),
       reads=[("t2bi",), ("consts",)], writes=[("t2b1",)])
    ve(lambda v: v.scalar_tensor_tensor(t2b_ec[:], t2b_is1[:], -1.0e30,
                                        logitsT[:], ALU.mult, ALU.add),
       reads=[("t2b1",)] + logits_all, writes=[("t2b",)])
    ve(lambda v: v.tensor_reduce(t2b_mx2[:], ec3(), AX, ALU.max),
       reads=[("t2b",)], writes=[("t2bm2",)])
    ve(lambda v: v.tensor_tensor(am3(), ec3(), bc1(t2b_mx2), ALU.is_equal),
       reads=[("t2b",), ("t2bm2",)], writes=[("t2ba",)])
    ve(lambda v: v.tensor_scalar(t2b_am[:], t2b_am[:], -1000.0, 1000.0,
                                 ALU.mult, ALU.add),
       reads=[("t2ba",)], writes=[("t2ba",)])
    ve(lambda v: v.tensor_tensor(am3(), am3(), iota_b, ALU.add),
       reads=[("t2ba",), ("consts",)], writes=[("t2ba",)])
    ve(lambda v: v.tensor_reduce(t2b_ix[:], am3(), AX, ALU.min),
       reads=[("t2ba",)], writes=[("t2bi",)])
    ve(lambda v: v.tensor_tensor(
           t2b_is2[:].rearrange("p (t e) -> p t e", e=8), iota_b,
           bc1(t2b_ix), ALU.is_equal),
       reads=[("t2bi",), ("consts",)], writes=[("t2b2",)])
    ve(lambda v: v.tensor_tensor(t2b_am[:], t2b_is1[:], t2b_is2[:], ALU.add),
       reads=[("t2b1",), ("t2b2",)], writes=[("t2ba",)])
    ve(lambda v: v.tensor_tensor(am3(), am3(), slm_b, ALU.mult),
       reads=[("t2ba",), ("consts",)], writes=[("t2ba",)])

    def csumf(eng):
        insts = []
        for j in range(OWN0):
            insts.append(eng.matmul(PU[0:1, 0:8], s_onescolbf[:],
                                    t2b_am[:, j * 8:(j + 1) * 8],
                                    start=(j == 0), stop=(j == OWN0 - 1)))
        return insts
    pe(csumf, reads=[("t2ba",), ("consts",)], writes=[("PG", 3)])
    ve(lambda v: v.tensor_copy(crow[:], PU[0:1, 0:8]),
       reads=[("PG", 3)], writes=[("crow",), ("PG", 3)])

    def c_tile(j):
        tt = j - OWN0
        if tt == 0:
            ve(lambda v: v.tensor_copy(fbrow[:], crow[:]),
               reads=[("crow",)], writes=[("fbrow",)])

        def pfx(eng, j=j):
            return [
                eng.matmul(PU[:, 8:16], s_lstrict[:],
                           t2b_am[:, j * 8:(j + 1) * 8],
                           start=True, stop=False),
                eng.matmul(PU[:, 8:16], s_onesrow[:], crow[:],
                           start=False, stop=True)]
        pe(pfx, reads=[("t2ba",), ("crow",), ("consts",)], writes=[("PG", 3)])
        ve(lambda v, tt=tt: v.tensor_copy(posb[:, tt * 8:(tt + 1) * 8],
                                          PU[:, 8:16]),
           reads=[("PG", 3)], writes=[("posb", tt)])
        ve(lambda v: v.tensor_single_scalar(t2_val[:], PU[:, 8:16],
                                            CAP, ALU.is_lt),
           reads=[("PG", 3)], writes=[("t2val",), ("PG", 3)])
        ve(lambda v, j=j, tt=tt: v.tensor_tensor(
               ambuf[:, tt * 8:(tt + 1) * 8], t2b_am[:, j * 8:(j + 1) * 8],
               t2_val[:], ALU.mult),
           reads=[("t2ba",), ("t2val",)], writes=[("ambuf", tt)])
        ve(lambda v, j=j: v.tensor_tensor(t2_dd[:], t2b_mx1[:, j:j + 1],
                                          t2b_mx2[:, j:j + 1], ALU.subtract),
           reads=[("t2bm",), ("t2bm2",)], writes=[("t2dd",)])
        se(lambda s: s.activation(t2_w1[:], t2_dd[:], AF.Sigmoid),
           reads=[("t2dd",)], writes=[("t2w1",)])
        se(lambda s: s.activation(t2_w2[:], t2_dd[:], AF.Sigmoid, scale=-1.0),
           reads=[("t2dd",)], writes=[("t2w2",)])
        ve(lambda v, j=j: v.tensor_scalar(t2_m1[:],
                                          t2b_is1[:, j * 8:(j + 1) * 8],
                                          t2_w1[:], None, ALU.mult),
           reads=[("t2b1",), ("t2w1",)], writes=[("t2m1",)])
        ve(lambda v, j=j: v.tensor_scalar(t2_m2[:],
                                          t2b_is2[:, j * 8:(j + 1) * 8],
                                          t2_w2[:], None, ALU.mult),
           reads=[("t2b2",), ("t2w2",)], writes=[("t2m2",)])
        ve(lambda v: v.tensor_add(t2_m1[:], t2_m1[:], t2_m2[:]),
           reads=[("t2m1",), ("t2m2",)], writes=[("t2m1",)])
        ve(lambda v, tt=tt: v.tensor_tensor(
               masks[:, tt * 8:(tt + 1) * 8], t2_m1[:], t2_val[:], ALU.mult),
           reads=[("t2m1",), ("t2val",)], writes=[("masks", tt)])

        def csum1(eng, j=j):
            return [eng.matmul(PU[0:1, 0:8], s_onescolbf[:],
                               t2b_am[:, j * 8:(j + 1) * 8],
                               start=True, stop=True)]
        pe(csum1, reads=[("t2ba",), ("consts",)], writes=[("PG", 3)])
        ve(lambda v: v.tensor_copy(csrow[:], PU[0:1, 0:8]),
           reads=[("PG", 3)], writes=[("csrow",), ("PG", 3)])
        ve(lambda v: v.tensor_add(crow[:], crow[:], csrow[:]),
           reads=[("csrow",), ("crow",)], writes=[("crow",)])

    c_thunks = [(lambda j=j: c_tile(j)) for j in range(OWN0, SLOTS)]

    S.membar([("xTb", 0), ("xTb", 1), ("xsqT", 0),
              ("mrow", 0), ("sqrow", 0), ("vrow", 0), ("rrow", 0),
              ("lgs", 0), ("t2b",), ("t2bi",)])
    mem.free(["t2b_ix", "t2b_ec"] + list(reversed(AB_NAMES)))

    acc = mem.alloc("acc", [128, DCH * T])
    posb = mem.alloc("posb", [128, 64])
    ambuf = mem.alloc("ambuf", [128, 64])
    wbuf = mem.alloc("wbuf", [128, 16])
    gidxu = mem.alloc("gidxu", [128, 32], U32)
    fbrow = mem.alloc("fbrow", [1, 8])
    fb_bc = mem.alloc("fb_bc", [128, 8])
    ebase = mem.alloc("ebase", [128, 8])
    lpb = mem.alloc("lpb", [128, 8])
    a8 = mem.alloc("a8", [128, 8])
    b8 = mem.alloc("b8", [128, 8])
    vcol = mem.alloc("vcol", [128, 1])
    gcol = mem.alloc("gcol", [128, 1])
    hbuf = mem.alloc("hbuf", [128, ICH * T], BF16)
    ubuf = [mem.alloc(f"ubuf{i}", [128, DCH * 128], BF16) for i in range(3)]
    dbuf = [mem.alloc(f"dbuf{i}", [128, ICH * 128], BF16) for i in range(2)]
    hsil = mem.alloc("hsil", [128, T], BF16)
    dzb = [mem.alloc(f"dzb{i}", [128, CAPS], BF16) for i in range(4)]
    xnstage = [mem.alloc(f"xnstage{i}", [128, D], BF16) for i in range(2)]
    xntm = [mem.alloc(f"xntm{i}", [128, D], BF16) for i in range(2)]
    uslot = [0]
    dslot = [0]

    xnT3 = xnT[:].rearrange("p (dc t) -> p dc t", dc=DCH)

    def r_prep():
        gfb = next_g()
        pe(lambda eng, gfb=gfb: [eng.matmul(PG[gfb][:, 0:8], s_onesrow[:],
                                            fbrow[:], start=True, stop=True)],
           reads=[("fbrow",), ("consts",)], writes=[("PG", gfb)])
        ve(lambda v, gfb=gfb: v.tensor_copy(fb_bc[:], PG[gfb][:, 0:8]),
           reads=[("PG", gfb)], writes=[("fb_bc",), ("PG", gfb)])
        ve(lambda v: v.tensor_single_scalar(ebase[:], s_iota[:], float(CAPS),
                                            ALU.mult),
           reads=[("consts",)], writes=[("ebase",)])

    def r_thunk(tt):
        j = OWN0 + tt
        sl = tt % 2
        pcol = posb[:, tt * 8:(tt + 1) * 8]
        acol = ambuf[:, tt * 8:(tt + 1) * 8]
        ve(lambda v: v.tensor_tensor(lpb[:], pcol, fb_bc[:], ALU.subtract),
           reads=[("posb", tt), ("fb_bc",)], writes=[("lpb",)])
        ve(lambda v: v.tensor_tensor(lpb[:], lpb[:], ebase[:], ALU.add),
           reads=[("lpb",), ("ebase",)], writes=[("lpb",)])
        for k, isb, gofs, wofs in ((0, t2b_is1, 0, 0), (1, t2b_is2, 16, 8)):
            iscol = isb[:, j * 8:(j + 1) * 8]
            isref = ("t2b1",) if k == 0 else ("t2b2",)
            ve(lambda v, iscol=iscol: v.tensor_tensor(a8[:], iscol, acol,
                                                      ALU.mult),
               reads=[isref, ("ambuf", tt)], writes=[("a8",)])
            ve(lambda v: v.tensor_tensor(b8[:], a8[:], lpb[:], ALU.mult),
               reads=[("a8",), ("lpb",)], writes=[("b8",)])
            ve(lambda v: v.reduce_sum(gcol[:], b8[:], AX),
               reads=[("b8",)], writes=[("gcol",)])
            ve(lambda v: v.reduce_sum(vcol[:], a8[:], AX),
               reads=[("a8",)], writes=[("vcol",)])
            ve(lambda v: v.tensor_single_scalar(vcol[:], vcol[:], -BIG,
                                                ALU.mult),
               reads=[("vcol",)], writes=[("vcol",)])
            ve(lambda v: v.tensor_single_scalar(vcol[:], vcol[:], BIG,
                                                ALU.add),
               reads=[("vcol",)], writes=[("vcol",)])
            ve(lambda v: v.tensor_add(gcol[:], gcol[:], vcol[:]),
               reads=[("gcol",), ("vcol",)], writes=[("gcol",)])
            ve(lambda v, c=gofs + tt: v.tensor_copy(gidxu[:, c:c + 1],
                                                    gcol[:]),
               reads=[("gcol",)], writes=[("gidxu", gofs + tt)])
            ve(lambda v, iscol=iscol, tt=tt: v.tensor_tensor(
                   b8[:], iscol, masks[:, tt * 8:(tt + 1) * 8], ALU.mult),
               reads=[isref, ("masks", tt)], writes=[("b8",)])
            ve(lambda v, c=wofs + tt: v.reduce_sum(wbuf[:, c:c + 1], b8[:],
                                                   AX),
               reads=[("b8",)], writes=[("wbuf", wofs + tt)])
        ve(lambda v, tt=tt, sl=sl: v.tensor_copy(
               xnstage[sl][:].rearrange("p (a b) -> p a b", a=DCH),
               xnT3[:, :, tt * 128:(tt + 1) * 128]),
           reads=xnT_ref, writes=[("xnstage", sl)])
        dma("sync", ch["c_xs"],
            [(xntm[sl][:].rearrange("p (a b) -> p a b", a=DCH),
              xnstage[sl][:], {"transpose": True})],
            reads=[("xnstage", sl)], writes=[("xntm", sl)])
        for gofs in (0, 16):
            def sc(eng, sl=sl, c=gofs + tt):
                return eng.indirect_dma_start(
                    out=gxe, out_offset=IndirectOffsetOnAxis(
                        ap=gidxu[:, c:c + 1], axis=0),
                    in_=xntm[sl][:], in_offset=None,
                    bounds_check=NE * CAPS - 1, oob_is_err=False)
            idma(ch["c_gx"], sc,
                 reads=[("xntm", sl), ("gidxu", gofs + tt)],
                 writes=[("gxe",)])

    r_thunks = [(lambda tt=tt: r_thunk(tt)) for tt in range(NT)]

    def gb_ap(gsl):
        return xnT[:, gsl * DCH * CAPS:(gsl + 1) * DCH * CAPS] \
            .rearrange("p (dc s) -> p dc s", dc=DCH)

    def gb_mv(gsl, dc):
        base = gsl * DCH * CAPS + dc * CAPS
        return xnT[:, base:base + CAPS]

    def load_gbuf(e):
        gsl = e % 2
        dma("sync", ch["c_gb"],
            [(gb_ap(gsl), gxe[e * CAPS:(e + 1) * CAPS, :],
              {"transpose": True})],
            reads=[("gxe",)],
            writes=[("gbufs", gsl), ("xnTf", 14), ("xnTf", 15)])

    outrow = None
    UNIT_ORDER = [8, 9, 10, 11] + list(range(NE))
    for ui, u in enumerate(UNIT_ORDER):
        routed = u < 8
        if ui == 4:
            while r_thunks:
                r_thunks.pop(0)()
            S.membar([("xntm", 0), ("xntm", 1), ("xnstage", 0),
                      ("xnstage", 1)])
            mem.free(["xntm1", "xntm0", "xnstage1", "xnstage0"])
            outrow = mem.alloc("outrow", [128, NSC, D], BF16)
        if routed:
            e = u
            gsl = e % 2
            if e == 0:
                load_gbuf(0)
            for ic in range(ICH):
                gus = []
                for mc in (ic, ICH + ic):
                    su_ = uslot[0] % 3
                    uslot[0] += 1
                    dma("gpsimd", ch[f"c_ub{mc % 2}"],
                        [(ubuf[su_][:], wug[e, mc], {})],
                        writes=[("ubuf", su_)])
                    gu = next_g()
                    gus.append(gu)

                    def umm(eng, su_=su_, gu=gu, gsl=gsl):
                        insts = []
                        for dc in range(DCH):
                            insts.append(eng.matmul(
                                PG[gu][:, 0:CAPS],
                                ubuf[su_][:, dc * 128:(dc + 1) * 128],
                                gb_mv(gsl, dc),
                                start=(dc == 0), stop=(dc == DCH - 1)))
                        return insts
                    pe(umm, reads=[("ubuf", su_), ("gbufs", gsl)],
                       writes=[("PG", gu)])
                gg_, gu_ = gus
                se(lambda s, gg_=gg_: s.activation(hsil[:, 0:CAPS],
                                                   PG[gg_][0:128, 0:CAPS],
                                                   AF.Silu),
                   reads=[("PG", gg_)], writes=[("hsil",), ("PG", gg_)])

                def hb(v, ic=ic, gu_=gu_):
                    return v.scalar_tensor_tensor(
                        hbuf[:, ic * CAPS:(ic + 1) * CAPS],
                        PG[gu_][0:128, 0:CAPS], 0.0, hsil[:, 0:CAPS],
                        ALU.bypass, ALU.mult)
                ve(hb, reads=[("PG", gu_), ("hsil",)],
                   writes=[("h", ic), ("PG", gu_)])
            if e + 1 < NE:
                load_gbuf(e + 1)
            for dcc in range(DCH):
                sd_ = dslot[0] % 2
                dslot[0] += 1
                dma("gpsimd", ch[f"c_db{sd_}"],
                    [(dbuf[sd_][:], wdn[e, dcc], {})],
                    writes=[("dbuf", sd_)])
                gd = next_g()

                def dmm(eng, sd_=sd_, gd=gd):
                    insts = []
                    for ic in range(ICH):
                        insts.append(eng.matmul(
                            PG[gd][:, 0:CAPS],
                            dbuf[sd_][:, ic * 128:(ic + 1) * 128],
                            hbuf[:, ic * CAPS:(ic + 1) * CAPS],
                            start=(ic == 0), stop=(ic == ICH - 1)))
                    return insts
                pe(dmm, reads=[("dbuf", sd_)]
                   + [("h", ic) for ic in range(ICH)],
                   writes=[("PG", gd)])
                zsl = dcc % 4
                ve(lambda v, gd=gd, zsl=zsl: v.tensor_copy(
                       dzb[zsl][:], PG[gd][0:128, 0:CAPS]),
                   reads=[("PG", gd)], writes=[("dzb", zsl), ("PG", gd)])
                dma("sync", ch["c_or"],
                    [(outrow[:, :, dcc * 128:(dcc + 1) * 128],
                      dzb[zsl][:], {"transpose": True})],
                    reads=[("dzb", zsl)], writes=[("outrow", dcc)])
            dma("scalar", ch["c_ox"],
                [(oxe[e * CAPS:(e + 1) * CAPS, :]
                  .rearrange("(sc p) d -> p sc d", p=128),
                  outrow[:].rearrange("p a b -> p (a b)"), {})],
                reads=[("outrow", dcc) for dcc in range(DCH)],
                writes=[("oxe", e)])
            continue

        g0 = u - 8


        for ic in range(ICH):
            if ui <= 1 and ic % 3 == 0 and c_thunks:
                c_thunks.pop(0)()
            if ui >= 2 and ic % 3 == 0 and r_thunks:
                r_thunks.pop(0)()
            gus = []
            for mc in (ic, ICH + ic):
                su_ = uslot[0] % 3
                uslot[0] += 1
                dma("gpsimd", ch[f"c_u{mc % 2}"],
                    [(ubuf[su_][:], sug[g0, mc], {})],
                    writes=[("ubuf", su_)])
                gu = next_g()
                gus.append(gu)

                def umm(eng, su_=su_, gu=gu):
                    insts = []
                    for dc in range(DCH):
                        lhs = ubuf[su_][:, dc * 128:(dc + 1) * 128]
                        for q in range(2):
                            insts.append(eng.matmul(
                                PG[gu][:, q * 512:(q + 1) * 512], lhs,
                                xnT[:, dc * T + q * 512:
                                    dc * T + q * 512 + 512],
                                start=(dc == 0), stop=(dc == DCH - 1)))
                    return insts
                pe(umm, reads=[("ubuf", su_)] + xnT_ref, writes=[("PG", gu)])
            gg_, gu_ = gus
            se(lambda s, gg_=gg_: s.activation(hsil[:],
                                               PG[gg_][0:128, 0:1024],
                                               AF.Silu),
               reads=[("PG", gg_)], writes=[("hsil",), ("PG", gg_)])

            def hb(v, ic=ic, gu_=gu_):
                return v.scalar_tensor_tensor(
                    hbuf[:, ic * T:(ic + 1) * T],
                    PG[gu_][0:128, 0:1024], 0.0, hsil[:],
                    ALU.bypass, ALU.mult)
            ve(hb, reads=[("PG", gu_), ("hsil",)],
               writes=[("h", ic), ("PG", gu_)])
        if ui == 1:
            while c_thunks:
                c_thunks.pop(0)()
            r_prep()
        for dcc in range(DCH):
            sd_ = dslot[0] % 2
            dslot[0] += 1
            dma("gpsimd", ch[f"c_d{sd_}"],
                [(dbuf[sd_][:], sdn[g0, dcc], {})],
                writes=[("dbuf", sd_)])
            gd = next_g()

            def dmm(eng, sd_=sd_, gd=gd):
                insts = []
                for ic in range(ICH):
                    lhs = dbuf[sd_][:, ic * 128:(ic + 1) * 128]
                    for q in range(2):
                        insts.append(eng.matmul(
                            PG[gd][:, q * 512:(q + 1) * 512], lhs,
                            hbuf[:, ic * T + q * 512:
                                   ic * T + q * 512 + 512],
                            start=(ic == 0), stop=(ic == ICH - 1)))
                return insts
            pe(dmm, reads=[("dbuf", sd_)] + [("h", ic) for ic in range(ICH)],
               writes=[("PG", gd)])
            if ui == 0:
                ve(lambda v, dcc=dcc, gd=gd: v.tensor_copy(
                       acc[:, dcc * T:(dcc + 1) * T], PG[gd][0:128, 0:1024]),
                   reads=[("PG", gd)], writes=[("acc", dcc), ("PG", gd)])
            else:
                ve(lambda v, dcc=dcc, gd=gd: v.tensor_tensor(
                       acc[:, dcc * T:(dcc + 1) * T],
                       acc[:, dcc * T:(dcc + 1) * T],
                       PG[gd][0:128, 0:1024], ALU.add),
                   reads=[("PG", gd), ("acc", dcc)],
                   writes=[("acc", dcc), ("PG", gd)])

    S.membar([("hsil",), ("gbufs", 0), ("gbufs", 1)]
             + [("dzb", i) for i in range(4)]
             + [("outrow", dcc) for dcc in range(DCH)]
             + [("dbuf", i) for i in range(2)]
             + [("ubuf", i) for i in range(3)]
             + [("h", ic) for ic in range(ICH)])
    mem.free(["outrow", "dzb3", "dzb2", "dzb1", "dzb0", "hsil",
              "dbuf1", "dbuf0", "ubuf2", "ubuf1", "ubuf0", "hbuf"])

    g1 = [mem.alloc("g1a", [128, D], BF16), mem.alloc("g1b", [128, D], BF16)]
    g2 = [mem.alloc("g2a", [128, D], BF16), mem.alloc("g2b", [128, D], BF16)]
    tc1 = mem.alloc("tc1", [128, T])
    tc2 = mem.alloc("tc2", [128, T])
    silt = mem.alloc("silt", [128, T])
    xrl = [mem.alloc("xrl0", [128, D]), mem.alloc("xrl1", [128, D])]
    outt = [mem.alloc("outt0", [128, D]), mem.alloc("outt1", [128, D])]
    sgv = sig_g[:].rearrange("p (ec a) -> p ec a", ec=DCH)
    oxe_refs = [("oxe", e) for e in range(NE)]
    for tt in range(NT):
        so = tt % 2
        ge(lambda g, so=so: g.memset(g1[so][:], 0.0),
           writes=[("g1", so)])
        ge(lambda g, so=so: g.memset(g2[so][:], 0.0),
           writes=[("g2", so)])

        def ga1(eng, so=so, tt=tt):
            return eng.indirect_dma_start(
                out=g1[so][:], out_offset=None, in_=oxe,
                in_offset=IndirectOffsetOnAxis(
                    ap=gidxu[:, tt:tt + 1], axis=0),
                bounds_check=NE * CAPS - 1, oob_is_err=False)
        idma(ch["c_gt"], ga1,
             reads=oxe_refs + [("g1", so), ("gidxu", tt)],
             writes=[("g1", so)])

        def ga2(eng, so=so, tt=tt):
            return eng.indirect_dma_start(
                out=g2[so][:], out_offset=None, in_=oxe,
                in_offset=IndirectOffsetOnAxis(
                    ap=gidxu[:, 16 + tt:17 + tt], axis=0),
                bounds_check=NE * CAPS - 1, oob_is_err=False)
        idma(ch["c_gt"], ga2,
             reads=oxe_refs + [("g2", so), ("gidxu", 16 + tt)],
             writes=[("g2", so)])
        dma("sync", ch[f"c_xr{so}"],
            [(xrl[so][:], xo[tt * 128:(tt + 1) * 128, :], {})],
            writes=[("xrl", so)])
        for half in range(2):
            gf = next_g()

            def ftp(eng, tt=tt, half=half, gf=gf):
                insts = []
                for k in range(8):
                    dcc = half * 8 + k
                    insts.append(eng.transpose(
                        PG[gf][:, k * 128:(k + 1) * 128],
                        acc[:, dcc * T + tt * 128: dcc * T + (tt + 1) * 128],
                        s_idf[:]))
                return insts
            pe(ftp, reads=[("acc", half * 8 + k) for k in range(8)]
               + [("consts",)],
               writes=[("PG", gf)])
            hs = slice(half * 1024, (half + 1) * 1024)
            ve(lambda v, so=so, tt=tt, hs=hs: v.tensor_scalar(
                   tc1[:], g1[so][:, hs], wbuf[:, tt:tt + 1], None,
                   ALU.mult),
               reads=[("g1", so), ("wbuf", tt)], writes=[("tc1",)])
            ve(lambda v, so=so, tt=tt, hs=hs: v.tensor_scalar(
                   tc2[:], g2[so][:, hs], wbuf[:, 8 + tt:9 + tt], None,
                   ALU.mult),
               reads=[("g2", so), ("wbuf", 8 + tt)], writes=[("tc2",)])
            ve(lambda v: v.tensor_add(tc1[:], tc1[:], tc2[:]),
               reads=[("tc1",), ("tc2",)], writes=[("tc1",)])
            ve(lambda v, gf=gf: v.tensor_tensor(tc1[:], tc1[:],
                                                PG[gf][0:128, 0:1024],
                                                ALU.add),
               reads=[("tc1",), ("PG", gf)], writes=[("tc1",), ("PG", gf)])
            se(lambda s: s.activation(silt[:], tc1[:], AF.Silu),
               reads=[("tc1",)], writes=[("silt",)])
            ve(lambda v, tt=tt, half=half: v.tensor_tensor(
                   tc1[:].rearrange("p (a b) -> p a b", a=8),
                   silt[:].rearrange("p (a b) -> p a b", a=8),
                   sgv[:, half * 8:(half + 1) * 8,
                       tt * 128:(tt + 1) * 128],
                   ALU.mult),
               reads=[("silt",)] + [("stm", ec)
                                    for ec in range(half * 8, half * 8 + 8)],
               writes=[("tc1",)])
            ve(lambda v, so=so, hs=hs: v.tensor_tensor(
                   outt[so][:, hs], tc1[:], xrl[so][:, hs], ALU.add),
               reads=[("tc1",), ("xrl", so)], writes=[("outt", so, half)])
        dma("sync", ch["c_out"],
            [(out_hbm[tt * 128:(tt + 1) * 128, :], outt[so][:], {})],
            reads=[("outt", so, 0), ("outt", so, 1)],
            writes=[("outdma", tt)])

    def fin(eng, sem, n):
        eng.nop().then_inc(sem, n)
    S.add("gpsimd", fin, chan=ch["fin"],
          extra_waits=[(ch["c_out"], ch["c_out"].total)])

    S.emit(nc, es)
    es.close()
    nc.finalize()
    return nc


_NC_CACHE = {}


def _consts():
    c = {}
    c["c_idbf"] = np.eye(128, dtype=bfnp)
    c["c_idf"] = np.eye(128, dtype=np.float32)
    c["c_onesrow"] = np.ones((1, 128), dtype=np.float32)
    c["c_onescolf"] = np.ones((128, 1), dtype=np.float32)
    c["c_onescolbf"] = np.ones((128, 1), dtype=bfnp)
    c["c_lstrict"] = np.triu(np.ones((128, 128), dtype=bfnp), k=1)
    c["c_iota"] = np.broadcast_to(np.arange(8, dtype=np.float32),
                                  (128, 8)).copy()
    return c


def make_in_maps(L0_attn_x, L0_attn_Wq, L0_attn_bq, L0_attn_Wk, L0_attn_bk,
                 L0_attn_Wv, L0_attn_bv, L0_attn_Wg, L0_attn_bg,
                 L0_ffn_router, L0_ffn_w_up_gate, L0_ffn_w_down,
                 L0_ffn_shared_w_up_gate, L0_ffn_shared_w_down):
    xflat = np.ascontiguousarray(np.asarray(L0_attn_x, dtype=np.float32)
                                 .reshape(TG, D))
    bcols = np.concatenate(
        [np.asarray(b, np.float32).reshape(16, 128).T
         for b in (L0_attn_bq, L0_attn_bk, L0_attn_bv, L0_attn_bg)], axis=1)
    rh = np.ascontiguousarray(
        np.asarray(L0_ffn_router, np.float32).reshape(16, 128, 8)
        .transpose(1, 0, 2).reshape(128, 128))
    wqT = np.ascontiguousarray(np.asarray(L0_attn_Wq, np.float32).T)
    wkT = np.ascontiguousarray(np.asarray(L0_attn_Wk, np.float32).T)
    wvT = np.ascontiguousarray(np.asarray(L0_attn_Wv, np.float32).T)
    wgT = np.ascontiguousarray(np.asarray(L0_attn_Wg, np.float32).T)
    wug_t = np.ascontiguousarray(
        np.asarray(L0_ffn_w_up_gate, np.float32)
        .reshape(8, 16, 128, 22, 128).transpose(0, 3, 2, 1, 4)
        .reshape(8, 22, 128, 2048))
    wdn_t = np.ascontiguousarray(
        np.asarray(L0_ffn_w_down, np.float32)
        .reshape(8, 11, 128, 16, 128).transpose(0, 3, 2, 1, 4)
        .reshape(8, 16, 128, 1408))
    sug_t = np.ascontiguousarray(
        np.asarray(L0_ffn_shared_w_up_gate, np.float32)
        .reshape(16, 128, 2, 4, 11, 128).transpose(3, 2, 4, 1, 0, 5)
        .reshape(4, 22, 128, 2048))
    sdn_t = np.ascontiguousarray(
        np.asarray(L0_ffn_shared_w_down, np.float32)
        .reshape(4, 11, 128, 16, 128).transpose(0, 3, 2, 1, 4)
        .reshape(4, 16, 128, 1408))
    consts = _consts()

    in_maps = []
    for c in range(8):
        sm = np.zeros((128, SLOTS), dtype=np.float32)
        for j in range(SLOTS):
            g = (j + 8 * (c + 1)) % 64
            if g < 8 * (c + 1):
                sm[:, j] = 1.0
        cm = np.full((128, 1), 1.0 if c % 2 == 1 else 0.0, dtype=np.float32)
        m = {
            "xgT": np.ascontiguousarray(
                np.roll(xflat, -1024 * (c + 1), axis=0).T),
            "xo": np.ascontiguousarray(xflat[1024 * c:1024 * (c + 1)]),
            "wq": wqT, "wk": wkT, "wv": wvT, "wgt": wgT,
            "bcols": bcols,
            "rhost": rh,
            "wug": wug_t,
            "wdn": wdn_t,
            "sug": sug_t,
            "sdn": sdn_t,
            "c_slotmask": sm,
            "c_carrymask": cm,
        }
        m.update(consts)
        in_maps.append(m)
    return in_maps


def kernel(**inputs):
    if "nc" not in _NC_CACHE:
        _NC_CACHE["nc"] = build_nc()
    nc = _NC_CACHE["nc"]
    in_maps = make_in_maps(**inputs)
    res = run_bass_kernel_spmd(nc, in_maps, core_ids=list(range(8)))
    out = np.concatenate([res.results[c]["out"] for c in range(8)], axis=0)
    return out.reshape(4, 2048, D)


# version 14
# speedup vs baseline: 1.2877x; 1.0285x over previous
import numpy as np
import ml_dtypes
import concourse.bass as bass
import concourse.mybir as mybir
from concourse.bass import IndirectOffsetOnAxis
from concourse.bass_utils import run_bass_kernel_spmd
from contextlib import ExitStack

F32 = mybir.dt.float32
BF16 = mybir.dt.bfloat16
U32 = mybir.dt.uint32
AF = mybir.ActivationFunctionType
ALU = mybir.AluOpType
AX = mybir.AxisListType.X

bfnp = ml_dtypes.bfloat16

D = 2048
DCH = 16
T = 1024
NT = 8
TG = 8192
SLOTS = 64
OWN0 = 56
I_FF = 1408
ICH = 11
CAP = 2560.0
CAPS = 384
NSC = 3
NE = 8
BIG = 65536.0


class Chan:
    __slots__ = ("name", "total", "sem")

    def __init__(self, name):
        self.name = name
        self.total = 0
        self.sem = None


class Sched:
    ENGINES = ("tensor", "vector", "scalar", "sync", "gpsimd")

    def __init__(self):
        self.plan = {e: [] for e in self.ENGINES}
        self.chans = []
        self.last_w = {}
        self.readers = {}
        self.bar = {}

    def chan(self, name):
        c = Chan(name)
        self.chans.append(c)
        return c

    def membar(self, refs):
        for r in refs:
            evs = []
            if r in self.last_w:
                evs.append(self.last_w[r])
            evs.extend(self.readers.get(r, ()))
            for c, v in evs:
                if self.bar.get(c, 0) < v:
                    self.bar[c] = v

    def add(self, engine, fn, reads=(), writes=(), chan=None, n_inc=1,
            extra_waits=()):
        waits = dict(self.bar)

        def need(ev):
            c, v = ev
            if waits.get(c, 0) < v:
                waits[c] = v

        for r in reads:
            if r in self.last_w:
                need(self.last_w[r])
        for r in writes:
            if r in self.last_w:
                need(self.last_w[r])
            for ev in self.readers.get(r, ()):
                need(ev)
        for ev in extra_waits:
            need(ev)
        chan.total += n_inc
        ev = (chan, chan.total)
        for r in reads:
            self.readers.setdefault(r, []).append(ev)
        for r in writes:
            self.last_w[r] = ev
            self.readers[r] = []
        self.plan[engine].append(
            (sorted(waits.items(), key=lambda kv: kv[0].name), fn, chan, n_inc))
        return ev

    def emit(self, nc, es):
        for c in self.chans:
            c.sem = es.enter_context(nc.semaphore(c.name))
        block = es.enter_context(nc.Block())
        for eng_name in self.ENGINES:
            steps = self.plan[eng_name]
            if not steps:
                continue
            dec = getattr(block, eng_name)

            def make(steps):
                def body(eng):
                    hwm = {}
                    for waits, fn, chan, n_inc in steps:
                        for c, v in waits:
                            if hwm.get(id(c), 0) < v:
                                eng.wait_ge(c.sem, v)
                                hwm[id(c)] = v
                        fn(eng, chan.sem, n_inc)
                return body

            dec(make(steps))


class SbufScope:

    def __init__(self, nc):
        self.nc = nc
        self.stack = []

    def alloc(self, name, shape, dt=F32):
        cm = self.nc.sbuf_tensor(name, list(shape), dt)
        h = cm.__enter__()
        self.stack.append((name, cm))
        return h

    def free(self, names):
        for n in names:
            top_name, cm = self.stack.pop()
            assert top_name == n, f"non-LIFO free: {n} vs top {top_name}"
            cm.__exit__(None, None, None)


def build_nc():
    nc = bass.Bass("TRN2", target_bir_lowering=False, debug=False,
                   num_devices=8)
    es = ExitStack()
    mem = SbufScope(nc)

    def din(name, shape, dt=F32):
        return nc.dram_tensor(name, list(shape), dt, kind="ExternalInput").ap()

    xgT = din("xgT", [D, TG])
    xo = din("xo", [T, D])
    wsrcT = {"q": din("wq", [D, D]), "k": din("wk", [D, D]),
             "v": din("wv", [D, D]), "g": din("wgt", [D, D])}
    bcols_h = din("bcols", [128, 64])
    rhost = din("rhost", [128, 128])
    wug = din("wug", [8, 22, 128, D])
    wdn = din("wdn", [8, DCH, 128, I_FF])
    sug = din("sug", [4, 22, 128, D])
    sdn = din("sdn", [4, DCH, 128, I_FF])
    c_idbf = din("c_idbf", [128, 128], BF16)
    c_idf = din("c_idf", [128, 128])
    c_onesrow = din("c_onesrow", [1, 128])
    c_onescolf = din("c_onescolf", [128, 1])
    c_onescolbf = din("c_onescolbf", [128, 1], BF16)
    c_lstrict = din("c_lstrict", [128, 128], BF16)
    c_iota = din("c_iota", [128, 8])
    c_slotmask = din("c_slotmask", [128, 64])
    c_carrymask = din("c_carrymask", [128, 1])
    out_hbm = nc.dram_tensor("out", [T, D], F32, kind="ExternalOutput").ap()

    gxe = nc.dram_tensor("gxe", [NE * CAPS, D], BF16, kind="Internal").ap()
    oxe = nc.dram_tensor("oxe", [NE * CAPS, D], BF16, kind="Internal").ap()

    P = mem.alloc
    s_idbf = P("s_idbf", [128, 128], BF16)
    s_idf = P("s_idf", [128, 128])
    s_onesrow = P("s_onesrow", [1, 128])
    s_onescolf = P("s_onescolf", [128, 1])
    s_onescolbf = P("s_onescolbf", [128, 1], BF16)
    s_lstrict = P("s_lstrict", [128, 128], BF16)
    s_iota = P("s_iota", [128, 8])
    s_slotmask = P("s_slotmask", [128, 64])
    s_carrymask = P("s_carrymask", [128, 1])
    s_bcols = P("s_bcols", [128, 64])
    s_r = P("s_r", [128, 128], BF16)
    xnT = P("xnT", [128, DCH * T], BF16)
    logitsT = P("logitsT", [128, SLOTS * 8])
    mcol = P("mcol", [128, SLOTS])
    rcol = P("rcol", [128, SLOTS])
    bcol2 = P("bcol2", [128, 1])
    sRbc = P("sRbc", [128, 8])
    sRrow = P("sRrow", [1, 8])
    masks = P("masks", [128, NT * 8])
    carryT = P("carryT", [128, 16])
    carryTm = P("carryTm", [128, 16])
    crow = P("crow", [1, 8])
    csrow = P("csrow", [1, 8])
    t2_mx1 = P("t2_mx1", [128, 1]); t2_mx2 = P("t2_mx2", [128, 1])
    t2_eq = P("t2_eq", [128, 8]); t2_cnd = P("t2_cnd", [128, 8])
    t2_ix = P("t2_ix", [128, 1])
    t2_is1 = P("t2_is1", [128, 8]); t2_is2 = P("t2_is2", [128, 8])
    t2_l2 = P("t2_l2", [128, 8]); t2_a = P("t2_a", [128, 8])
    t2_am = P("t2_am", [128, 8])
    t2_dd = P("t2_dd", [128, 1])
    t2_w1 = P("t2_w1", [128, 1]); t2_w2 = P("t2_w2", [128, 1])
    t2_m1 = P("t2_m1", [128, 8]); t2_m2 = P("t2_m2", [128, 8])
    t2_val = P("t2_val", [128, 8])
    xnTp = P("xnTp", [128, DCH * T], BF16)

    PG = [es.enter_context(nc.psum_tensor(f"PG{i}", [128, 1024], F32))
          for i in range(4)]
    PU = PG[3]

    S = Sched()
    ch = {n: S.chan(n) for n in
          ["pe", "ve", "se", "ge", "fin",
           "c_const", "c_router",
           "c_xT0", "c_xT1", "c_wt0", "c_wt1",
           "c_u0", "c_u1", "c_d0", "c_d1",
           "c_ub0", "c_ub1", "c_db0", "c_db1",
           "c_gx", "c_xs", "c_gb", "c_or", "c_ox", "c_gt",
           "c_xr0", "c_xr1", "c_out", "c_so0", "c_so1"]}
    PE, VE, SE, GE = ch["pe"], ch["ve"], ch["se"], ch["ge"]

    def dma(queue, chan, pairs, reads=(), writes=()):
        def fn(eng, sem, n, pairs=pairs):
            for o, i, kw in pairs:
                eng.dma_start(out=o, in_=i, **kw).then_inc(sem, 16)
        prev = (chan, chan.total)
        return S.add(queue, fn, reads=reads, writes=writes, chan=chan,
                     n_inc=16 * len(pairs),
                     extra_waits=[prev] if chan.total else [])

    def idma(chan, fn, reads=(), writes=()):
        def wrap(eng, sem, n, fn=fn):
            fn(eng).then_inc(sem, 16)
        prev = (chan, chan.total)
        return S.add("gpsimd", wrap, reads=reads, writes=writes, chan=chan,
                     n_inc=16, extra_waits=[prev] if chan.total else [])

    def pe(fn, reads=(), writes=()):
        def wrap(eng, sem, n, fn=fn):
            insts = fn(eng)
            insts[-1].then_inc(sem, 1)
        return S.add("tensor", wrap, reads=reads, writes=writes, chan=PE)

    def ve(fn, reads=(), writes=()):
        def wrap(eng, sem, n, fn=fn):
            fn(eng).then_inc(sem, 1)
        return S.add("vector", wrap, reads=reads, writes=writes, chan=VE)

    def se(fn, reads=(), writes=()):
        def wrap(eng, sem, n, fn=fn):
            fn(eng).then_inc(sem, 1)
        return S.add("scalar", wrap, reads=reads, writes=writes, chan=SE)

    def ge(fn, reads=(), writes=()):
        def wrap(eng, sem, n, fn=fn):
            fn(eng).then_inc(sem, 1)
        return S.add("gpsimd", wrap, reads=reads, writes=writes, chan=GE)

    dma("sync", ch["c_const"], [
        (s_idbf[:], c_idbf, {}), (s_idf[:], c_idf, {}),
        (s_onesrow[:], c_onesrow, {}), (s_onescolf[:], c_onescolf, {}),
        (s_onescolbf[:], c_onescolbf, {}), (s_lstrict[:], c_lstrict, {}),
        (s_iota[:], c_iota, {}),
        (s_slotmask[:], c_slotmask, {}), (s_carrymask[:], c_carrymask, {}),
        (s_bcols[:], bcols_h, {}),
    ], writes=[("consts",)])
    dma("gpsimd", ch["c_router"], [(s_r[:], rhost, {})], writes=[("router",)])
    t2b_is1 = mem.alloc("t2b_is1", [128, 512], BF16)
    t2b_is2 = mem.alloc("t2b_is2", [128, 512], BF16)
    t2b_am = mem.alloc("t2b_am", [128, 512], BF16)
    t2b_mx1 = mem.alloc("t2b_mx1", [128, 64])
    t2b_mx2 = mem.alloc("t2b_mx2", [128, 64])

    AB_NAMES = []

    def AB(name, shape, dt=F32):
        AB_NAMES.append(name)
        return mem.alloc(name, shape, dt)

    xTb = [AB("xTb0", [128, DCH * 512], BF16),
           AB("xTb1", [128, DCH * 512], BF16)]
    xsqT = [AB("xsqT0", [128, 512], BF16), AB("xsqT1", [128, 512], BF16)]
    _mr = AB("mrow0", [1, 512]); mrow = [_mr, _mr]
    _sq = AB("sqrow0", [1, 512]); sqrow = [_sq, _sq]
    _vr = AB("vrow0", [1, 512]); vrow = [_vr, _vr]
    _rr = AB("rrow0", [1, 512]); rrow = [_rr, _rr]
    _lg = AB("lgs0", [8, 512]); lgs = [_lg, _lg]
    MRP_NAMES = []

    def MRP(name, shape, dt=F32):
        MRP_NAMES.append(name)
        return mem.alloc(name, shape, dt)

    mbc_sb = MRP("mbc_sb", [128, 512])
    rbc_sb = MRP("rbc_sb", [128, 512])
    pf_sb = MRP("pf_sb", [128, 512])

    def sr_mm(eng):
        insts = []
        for dc in range(DCH):
            insts.append(eng.matmul(PU[0:1, 0:8], s_onescolbf[:],
                                    s_r[:, dc * 8:(dc + 1) * 8],
                                    start=(dc == 0), stop=(dc == DCH - 1)))
        return insts
    pe(sr_mm, reads=[("router",), ("consts",)], writes=[("PG", 3)])
    ve(lambda v: v.tensor_copy(sRrow[:], PU[0:1, 0:8]),
       reads=[("PG", 3)], writes=[("sRrow",), ("PG", 3)])
    pe(lambda eng: [eng.matmul(PU[:, 0:8], s_onesrow[:], sRrow[:],
                               start=True, stop=True)],
       reads=[("sRrow",), ("consts",)], writes=[("PG", 3)])
    ve(lambda v: v.tensor_copy(sRbc[:], PU[:, 0:8]),
       reads=[("PG", 3)], writes=[("sRbc",), ("PG", 3)])

    def b_layout(tch):
        sl = tch % 2
        folded = tch >= 12
        if tch in (12, 13):
            dst, col0 = xnTp, (tch - 12) * 512
        elif tch in (14, 15):
            dst, col0 = xnT, (tch - 14) * 512
        else:
            dst, col0 = xTb[sl], 0
        dstref = ("xnTraw", tch) if folded else ("xTb", sl)
        pitch = T if folded else 512
        return sl, folded, dst, dstref, col0, pitch

    def plan_B_dma(tch):
        sl, folded, dst, dstref, col0, pitch = b_layout(tch)
        src = xgT[:, tch * 512:(tch + 1) * 512] \
            .rearrange("(dc p) t -> p dc t", p=128)
        if folded:
            dview = dst[:].rearrange("p (dc t) -> p dc t", dc=DCH) \
                [:, :, col0:col0 + 512]
        else:
            dview = dst[:]
        dma("gpsimd", ch[f"c_xT{sl}"], [(dview, src, {})], writes=[dstref])

    def plan_B_compute(tch):
        sl, folded, dst, dstref, col0, pitch = b_layout(tch)

        g = tch % 3
        for dc in range(DCH):
            def xt_ap(dc=dc, dst=dst, col0=col0, pitch=pitch):
                return dst[:, dc * pitch + col0: dc * pitch + col0 + 512]
            s2 = dc % 2
            ve(lambda v, s2=s2, xt=xt_ap: v.tensor_mul(xsqT[s2][:], xt(),
                                                       xt()),
               reads=[dstref], writes=[("xsqT", s2)])

            def mms(eng, dc=dc, s2=s2, g=g, xt=xt_ap):
                return [
                    eng.matmul(PU[0:1, 0:512], s_onescolbf[:], xt(),
                               start=(dc == 0), stop=(dc == DCH - 1)),
                    eng.matmul(PG[g][32:33, 512:1024], s_onescolbf[:],
                               xsqT[s2][:],
                               start=(dc == 0), stop=(dc == DCH - 1)),
                    eng.matmul(PG[g][0:8, 0:512], s_r[:, dc * 8:(dc + 1) * 8],
                               xt(), start=(dc == 0), stop=(dc == DCH - 1)),
                ]
            pe(mms, reads=[dstref, ("xsqT", s2), ("router",), ("consts",)],
               writes=[("PG", 3), ("PG", g)])

        ve(lambda v, sl=sl: v.tensor_single_scalar(mrow[sl][:],
                                                   PU[0:1, 0:512],
                                                   1.0 / D, ALU.mult),
           reads=[("PG", 3)], writes=[("mrow", 0), ("PG", 3)])
        ve(lambda v, sl=sl, g=g: v.tensor_single_scalar(
               sqrow[sl][:], PG[g][32:33, 512:1024], 1.0 / D, ALU.mult),
           reads=[("PG", g)], writes=[("sqrow", 0)])
        ve(lambda v, sl=sl: v.tensor_mul(vrow[sl][:], mrow[sl][:],
                                         mrow[sl][:]),
           reads=[("mrow", 0)], writes=[("vrow", 0)])
        ve(lambda v, sl=sl: v.tensor_tensor(vrow[sl][:], sqrow[sl][:],
                                            vrow[sl][:], ALU.subtract),
           reads=[("sqrow", 0), ("vrow", 0)], writes=[("vrow", 0)])
        ve(lambda v, sl=sl: v.tensor_single_scalar(vrow[sl][:], vrow[sl][:],
                                                   1e-5, ALU.add),
           reads=[("vrow", 0)], writes=[("vrow", 0)])
        se(lambda s, sl=sl: s.activation(vrow[sl][:], vrow[sl][:], AF.Ln),
           reads=[("vrow", 0)], writes=[("vrow", 0)])
        se(lambda s, sl=sl: s.activation(rrow[sl][:], vrow[sl][:], AF.Exp,
                                         scale=-0.5),
           reads=[("vrow", 0)], writes=[("rrow", 0)])

        ve(lambda v, g=g, sl=sl: v.tensor_copy(lgs[sl][:], PG[g][0:8, 0:512]),
           reads=[("PG", g)], writes=[("lgs", 0), ("PG", g)])

        def tpb(eng, sl=sl):
            insts = []
            for q in range(4):
                insts.append(eng.transpose(
                    PU[:, q:q + 1], mrow[sl][0:1, q * 128:(q + 1) * 128],
                    s_idf[0:1, 0:1]))
                insts.append(eng.transpose(
                    PU[:, 4 + q:5 + q], rrow[sl][0:1, q * 128:(q + 1) * 128],
                    s_idf[0:1, 0:1]))
                insts.append(eng.transpose(
                    PU[:, 8 + q * 8: 16 + q * 8],
                    lgs[sl][0:8, q * 128:(q + 1) * 128], s_idf[0:8, 0:8]))
            return insts
        pe(tpb, reads=[("mrow", 0), ("rrow", 0), ("lgs", 0), ("consts",)],
           writes=[("PG", 3)])

        def fold_ev(v, tch=tch):
            for q in range(4):
                j = tch * 4 + q
                v.tensor_copy(mcol[:, j:j + 1], PU[:, q:q + 1])
                v.tensor_copy(rcol[:, j:j + 1], PU[:, 4 + q:5 + q])
                inst = v.tensor_copy(logitsT[:, j * 8:(j + 1) * 8],
                                     PU[:, 8 + q * 8:16 + q * 8])
            return inst
        ve(fold_ev, reads=[("PG", 3)],
           writes=[("mrcol", tch), ("lraw", tch), ("PG", 3)])

        for q in range(4):
            j = tch * 4 + q
            ve(lambda v, j=j: v.tensor_mul(bcol2[:], mcol[:, j:j + 1],
                                           rcol[:, j:j + 1]),
               reads=[("mrcol", tch)], writes=[("bcol2",)])
            ve(lambda v, j=j: v.tensor_scalar(
                   logitsT[:, j * 8:(j + 1) * 8],
                   logitsT[:, j * 8:(j + 1) * 8],
                   rcol[:, j:j + 1], None, ALU.mult),
               reads=[("mrcol", tch), ("lraw", tch)],
               writes=[("lraw", tch)])
            ve(lambda v: v.tensor_scalar(t2_eq[:], sRbc[:], bcol2[:], None,
                                         ALU.mult),
               reads=[("sRbc",), ("bcol2",)], writes=[("t2",)])
            ve(lambda v, j=j: v.tensor_tensor(
                   logitsT[:, j * 8:(j + 1) * 8],
                   logitsT[:, j * 8:(j + 1) * 8],
                   t2_eq[:], ALU.subtract),
               reads=[("t2",), ("lraw", tch)],
               writes=[("lraw", tch)])
        S.last_w[("logits", tch)] = S.last_w[("lraw", tch)]

        if folded:
            g2 = (g + 1) % 3

            def bcst(eng, sl=sl, g2=g2):
                return [
                    eng.matmul(PG[g2][:, 0:512], s_onesrow[:], mrow[sl][:],
                               start=True, stop=True),
                    eng.matmul(PG[g2][:, 512:1024], s_onesrow[:], rrow[sl][:],
                               start=True, stop=True)]
            pe(bcst, reads=[("mrow", 0), ("rrow", 0), ("consts",)],
               writes=[("PG", g2)])
            ve(lambda v, g2=g2: v.tensor_copy(mbc_sb[:], PG[g2][:, 0:512]),
               reads=[("PG", g2)], writes=[("mbc",)])
            ve(lambda v, g2=g2: v.tensor_copy(rbc_sb[:], PG[g2][:, 512:1024]),
               reads=[("PG", g2)], writes=[("rbc",), ("PG", g2)])
            for dc in range(DCH):
                ve(lambda v, dc=dc, col0=col0, dst=dst: v.tensor_tensor(
                       pf_sb[:],
                       dst[:, dc * T + col0: dc * T + col0 + 512],
                       mbc_sb[:], ALU.subtract),
                   reads=[("xnTraw", tch), ("mbc",)], writes=[("pf",)])
                ve(lambda v, dc=dc, col0=col0, dst=dst: v.tensor_tensor(
                       dst[:, dc * T + col0: dc * T + col0 + 512],
                       pf_sb[:], rbc_sb[:], ALU.mult),
                   reads=[("pf",), ("rbc",)],
                   writes=[("xnTraw", tch), ("xnTf", tch)])


    b_early = [14, 15, 12, 13]
    for k, tch in enumerate(b_early):
        plan_B_dma(tch)
    for tch in b_early:
        plan_B_compute(tch)
    S.membar([("mbc",), ("rbc",), ("pf",)])
    mem.free(list(reversed(MRP_NAMES)))
    t2b_ec = mem.alloc("t2b_ec", [128, 512])
    t2b_ix = mem.alloc("t2b_ix", [128, 64])

    xnT_ref = [("xnTf", 14), ("xnTf", 15)]

    kv = mem.alloc("kv", [128, DCH * T], BF16)
    s_scr = [mem.alloc("s_scr0", [128, T], BF16),
             mem.alloc("s_scr1", [128, T], BF16)]
    wtbuf = [mem.alloc("wtbuf0", [128, DCH * 512], BF16),
             mem.alloc("wtbuf1", [128, DCH * 512], BF16)]
    k_sb = mem.alloc("k_sb", [128, 2048], BF16)
    kvp = mem.alloc("kvp", [128, 512], BF16)
    red0 = mem.alloc("red0", [128, 1])
    red1 = mem.alloc("red1", [128, 1])

    gcyc = [0]

    def next_g():
        g = gcyc[0] % 3
        gcyc[0] += 1
        return g

    g4cyc = [0]

    def next_g4():
        g = g4cyc[0] % 4
        g4cyc[0] += 1
        return g

    wslot = [0]

    def wt_group(w, e512):
        slw = wslot[0] % 2
        wslot[0] += 1
        src = wsrcT[w][:, e512 * 512:(e512 + 1) * 512] \
            .rearrange("(dc p) r -> p dc r", p=128)
        dma("gpsimd", ch[f"c_wt{slw}"], [(wtbuf[slw][:], src, {})],
            writes=[("wt", slw)])
        return slw

    for e512 in range(4):
        slk = wt_group("k", e512)
        slv = wt_group("v", e512)
        for ec4 in range(4):
            echunk = e512 * 4 + ec4
            gk = [next_g4(), next_g4()]

            def kmm(eng, slk=slk, ec4=ec4, gk=gk):
                insts = []
                for dc in range(DCH):
                    lhs = wtbuf[slk][:, dc * 512 + ec4 * 128:
                                     dc * 512 + (ec4 + 1) * 128]
                    for q in range(4):
                        src = xnT if q < 2 else xnTp
                        mv = src[:, dc * T + (q % 2) * 512:
                                 dc * T + (q % 2) * 512 + 512]
                        insts.append(eng.matmul(
                            PG[gk[q // 2]][:, (q % 2) * 512:(q % 2) * 512 + 512],
                            lhs, mv, start=(dc == 0), stop=(dc == DCH - 1)))
                return insts
            pe(kmm, reads=[("wt", slk), ("xnTf", 12), ("xnTf", 13),
               ("xnTf", 14), ("xnTf", 15)],
               writes=[("PG", gk[0]), ("PG", gk[1])])

            def kev(v, ec=echunk, gk=gk):
                for q in range(4):
                    inst = v.tensor_scalar(
                        k_sb[:, q * 512:(q + 1) * 512],
                        PG[gk[q // 2]][:, (q % 2) * 512:(q % 2) * 512 + 512],
                        s_bcols[:, 16 + ec:17 + ec], None, ALU.add)
                return inst
            ve(kev, reads=[("PG", gk[0]), ("PG", gk[1]), ("consts",)],
               writes=[("k_sb",), ("PG", gk[0]), ("PG", gk[1])])

            gv = [next_g4(), next_g4()]

            def vmm(eng, slv=slv, ec4=ec4, gv=gv):
                insts = []
                for dc in range(DCH):
                    lhs = wtbuf[slv][:, dc * 512 + ec4 * 128:
                                     dc * 512 + (ec4 + 1) * 128]
                    for q in range(4):
                        src = xnT if q < 2 else xnTp
                        mv = src[:, dc * T + (q % 2) * 512:
                                 dc * T + (q % 2) * 512 + 512]
                        insts.append(eng.matmul(
                            PG[gv[q // 2]][:, (q % 2) * 512:(q % 2) * 512 + 512],
                            lhs, mv, start=(dc == 0), stop=(dc == DCH - 1)))
                return insts
            pe(vmm, reads=[("wt", slv), ("xnTf", 12), ("xnTf", 13),
               ("xnTf", 14), ("xnTf", 15)],
               writes=[("PG", gv[0]), ("PG", gv[1])])

            def kvf(v, ec=echunk, gv=gv):
                for q in range(2):
                    inst = v.scalar_tensor_tensor(
                        kv[:, ec * T + q * 512: ec * T + q * 512 + 512],
                        PG[gv[0]][:, q * 512: q * 512 + 512],
                        s_bcols[:, 32 + ec:33 + ec],
                        k_sb[:, q * 512:(q + 1) * 512], ALU.add, ALU.mult)
                return inst
            ve(kvf, reads=[("PG", gv[0]), ("k_sb",), ("consts",)],
               writes=[("kv", echunk), ("PG", gv[0])])

            def kvp1(v, ec=echunk, gv=gv):
                return v.scalar_tensor_tensor(
                    kvp[:], PG[gv[1]][:, 0:512], s_bcols[:, 32 + ec:33 + ec],
                    k_sb[:, 1024:1536], ALU.add, ALU.mult)
            ve(kvp1, reads=[("PG", gv[1]), ("k_sb",), ("consts",)],
               writes=[("kvp",)])
            ve(lambda v: v.reduce_sum(red0[:], kvp[:], AX),
               reads=[("kvp",)], writes=[("red0",)])

            def kvp2(v, ec=echunk, gv=gv):
                return v.scalar_tensor_tensor(
                    kvp[:], PG[gv[1]][:, 512:1024],
                    s_bcols[:, 32 + ec:33 + ec],
                    k_sb[:, 1536:2048], ALU.add, ALU.mult)
            ve(kvp2, reads=[("PG", gv[1]), ("k_sb",), ("consts",)],
               writes=[("kvp",), ("PG", gv[1])])
            ve(lambda v: v.reduce_sum(red1[:], kvp[:], AX),
               reads=[("kvp",)], writes=[("red1",)])

            ve(lambda v: v.tensor_add(red0[:], red0[:], red1[:]),
               reads=[("red0",), ("red1",)], writes=[("red0",)])
            ve(lambda v, ec=echunk: v.tensor_copy(carryT[:, ec:ec + 1],
                                                  red0[:]),
               reads=[("red0",)], writes=[("carryT", echunk)])

    sig_g = xnTp
    S.membar([("xnTf", 12), ("xnTf", 13), ("xnTraw", 12), ("xnTraw", 13)])

    slg = wt_group("g", 0)
    for e512 in range(4):
        for ec4 in range(4):
            echunk = e512 * 4 + ec4
            gg = next_g()

            def gmm(eng, slg=slg, ec4=ec4, gg=gg):
                insts = []
                for dc in range(DCH):
                    lhs = wtbuf[slg][:, dc * 512 + ec4 * 128:
                                     dc * 512 + (ec4 + 1) * 128]
                    for q in range(2):
                        insts.append(eng.matmul(
                            PG[gg][:, q * 512:(q + 1) * 512], lhs,
                            xnT[:, dc * T + q * 512: dc * T + q * 512 + 512],
                            start=(dc == 0), stop=(dc == DCH - 1)))
                return insts
            pe(gmm, reads=[("wt", slg)] + xnT_ref, writes=[("PG", gg)])

            def gev(s, ec=echunk, gg=gg):
                for q in range(2):
                    inst = s.activation(
                        sig_g[:, ec * T + q * 512: ec * T + q * 512 + 512],
                        PG[gg][:, q * 512:(q + 1) * 512], AF.Sigmoid,
                        bias=s_bcols[:, 48 + ec:49 + ec])
                return inst
            se(gev, reads=[("PG", gg), ("consts",)],
               writes=[("sig_g", echunk), ("PG", gg)])
            if echunk < 12:
                if echunk == 0:
                    plan_B_dma(0)
                    plan_B_dma(1)
                plan_B_compute(echunk)
                if echunk + 2 < 12:
                    plan_B_dma(echunk + 2)
            if ec4 == 0 and e512 < 3:
                nslg = wt_group("g", e512 + 1)
        slg = nslg

    ve(lambda v: v.tensor_scalar(carryTm[:], carryT[:], s_carrymask[:], None,
                                 ALU.mult),
       reads=[("carryT", ec) for ec in range(16)] + [("consts",)],
       writes=[("carryTm",)])

    kvs = [mem.alloc("kvs0", [128, T]), mem.alloc("kvs1", [128, T])]
    qt = mem.alloc("qt", [128, T])
    slq = wt_group("q", 0)
    for e512 in range(4):
        for ec4 in range(4):
            echunk = e512 * 4 + ec4
            skv = echunk % 2
            ve(lambda v, ec=echunk, skv=skv: v.tensor_tensor_scan(
                   kvs[skv][:], kv[:, ec * T:(ec + 1) * T],
                   kv[:, ec * T:(ec + 1) * T], carryTm[:, ec:ec + 1],
                   ALU.add, ALU.bypass),
               reads=[("kv", echunk), ("carryTm",)], writes=[("kvs", skv)])
            gq = next_g()

            def qmm(eng, slq=slq, ec4=ec4, gq=gq):
                insts = []
                for dc in range(DCH):
                    lhs = wtbuf[slq][:, dc * 512 + ec4 * 128:
                                     dc * 512 + (ec4 + 1) * 128]
                    for q in range(2):
                        insts.append(eng.matmul(
                            PG[gq][:, q * 512:(q + 1) * 512], lhs,
                            xnT[:, dc * T + q * 512: dc * T + q * 512 + 512],
                            start=(dc == 0), stop=(dc == DCH - 1)))
                return insts
            pe(qmm, reads=[("wt", slq)] + xnT_ref, writes=[("PG", gq)])

            def sq1(v, ec=echunk, gq=gq, skv=skv):
                return v.scalar_tensor_tensor(
                    qt[:], PG[gq][0:128, 0:1024], s_bcols[:, ec:ec + 1],
                    kvs[skv][:], ALU.add, ALU.mult)
            ve(sq1, reads=[("PG", gq), ("kvs", skv), ("consts",)],
               writes=[("qt",), ("PG", gq)])

            ssl = echunk % 2

            def sq2(v, ec=echunk, ssl=ssl):
                return v.tensor_tensor(s_scr[ssl][:], qt[:],
                                       sig_g[:, ec * T:(ec + 1) * T],
                                       ALU.mult)
            ve(sq2, reads=[("qt",), ("sig_g", echunk)],
               writes=[("s_scr", ssl)])
            dma("sync", ch[f"c_so{ssl}"],
                [(sig_g[:, echunk * T:(echunk + 1) * T]
                  .rearrange("p (a b) -> p a b", a=8),
                  s_scr[ssl][:], {"transpose": True})],
                reads=[("s_scr", ssl)],
                writes=[("sig_g", echunk), ("stm", echunk)])
            if ec4 == 0 and e512 < 3:
                nslq = wt_group("q", e512 + 1)
        slq = nslq

    S.membar([("kv", ec) for ec in range(16)]
             + [("stm", ec) for ec in range(16)]
             + [("wt", 0), ("wt", 1), ("k_sb",), ("kvp",), ("red0",),
                ("red1",), ("kvs", 0), ("kvs", 1), ("qt",),
                ("s_scr", 0), ("s_scr", 1)])
    mem.free(["qt", "kvs1", "kvs0", "red1", "red0", "kvp", "k_sb",
              "wtbuf1", "wtbuf0", "s_scr1", "s_scr0", "kv"])

    logits_all = [("logits", tch) for tch in range(16)]

    def l3():
        return logitsT[:].rearrange("p (t e) -> p t e", e=8)

    def bc1(buf):
        return buf[:].unsqueeze(2).broadcast_to([128, 64, 8])

    iota_b = s_iota[:].unsqueeze(1).broadcast_to([128, 64, 8])
    slm_b = s_slotmask[:].unsqueeze(2).broadcast_to([128, 64, 8])

    def ec3():
        return t2b_ec[:].rearrange("p (t e) -> p t e", e=8)

    def am3():
        return t2b_am[:].rearrange("p (t e) -> p t e", e=8)

    ve(lambda v: v.tensor_reduce(t2b_mx1[:], l3(), AX, ALU.max),
       reads=logits_all, writes=[("t2bm",)])
    ve(lambda v: v.tensor_tensor(ec3(), l3(), bc1(t2b_mx1), ALU.is_equal),
       reads=logits_all + [("t2bm",)], writes=[("t2b",)])
    ve(lambda v: v.tensor_scalar(t2b_ec[:], t2b_ec[:], -1000.0, 1000.0,
                                 ALU.mult, ALU.add),
       reads=[("t2b",)], writes=[("t2b",)])
    ve(lambda v: v.tensor_tensor(ec3(), ec3(), iota_b, ALU.add),
       reads=[("t2b",), ("consts",)], writes=[("t2b",)])
    ve(lambda v: v.tensor_reduce(t2b_ix[:], ec3(), AX, ALU.min),
       reads=[("t2b",)], writes=[("t2bi",)])
    ve(lambda v: v.tensor_tensor(
           t2b_is1[:].rearrange("p (t e) -> p t e", e=8), iota_b,
           bc1(t2b_ix), ALU.is_equal),
       reads=[("t2bi",), ("consts",)], writes=[("t2b1",)])
    ve(lambda v: v.scalar_tensor_tensor(t2b_ec[:], t2b_is1[:], -1.0e30,
                                        logitsT[:], ALU.mult, ALU.add),
       reads=[("t2b1",)] + logits_all, writes=[("t2b",)])
    ve(lambda v: v.tensor_reduce(t2b_mx2[:], ec3(), AX, ALU.max),
       reads=[("t2b",)], writes=[("t2bm2",)])
    ve(lambda v: v.tensor_tensor(am3(), ec3(), bc1(t2b_mx2), ALU.is_equal),
       reads=[("t2b",), ("t2bm2",)], writes=[("t2ba",)])
    ve(lambda v: v.tensor_scalar(t2b_am[:], t2b_am[:], -1000.0, 1000.0,
                                 ALU.mult, ALU.add),
       reads=[("t2ba",)], writes=[("t2ba",)])
    ve(lambda v: v.tensor_tensor(am3(), am3(), iota_b, ALU.add),
       reads=[("t2ba",), ("consts",)], writes=[("t2ba",)])
    ve(lambda v: v.tensor_reduce(t2b_ix[:], am3(), AX, ALU.min),
       reads=[("t2ba",)], writes=[("t2bi",)])
    ve(lambda v: v.tensor_tensor(
           t2b_is2[:].rearrange("p (t e) -> p t e", e=8), iota_b,
           bc1(t2b_ix), ALU.is_equal),
       reads=[("t2bi",), ("consts",)], writes=[("t2b2",)])
    ve(lambda v: v.tensor_tensor(t2b_am[:], t2b_is1[:], t2b_is2[:], ALU.add),
       reads=[("t2b1",), ("t2b2",)], writes=[("t2ba",)])
    ve(lambda v: v.tensor_tensor(am3(), am3(), slm_b, ALU.mult),
       reads=[("t2ba",), ("consts",)], writes=[("t2ba",)])

    def csumf(eng):
        insts = []
        for j in range(OWN0):
            insts.append(eng.matmul(PU[0:1, 0:8], s_onescolbf[:],
                                    t2b_am[:, j * 8:(j + 1) * 8],
                                    start=(j == 0), stop=(j == OWN0 - 1)))
        return insts
    pe(csumf, reads=[("t2ba",), ("consts",)], writes=[("PG", 3)])
    ve(lambda v: v.tensor_copy(crow[:], PU[0:1, 0:8]),
       reads=[("PG", 3)], writes=[("crow",), ("PG", 3)])

    def c_tile(j):
        tt = j - OWN0
        if tt == 0:
            ve(lambda v: v.tensor_copy(fbrow[:], crow[:]),
               reads=[("crow",)], writes=[("fbrow",)])

        def pfx(eng, j=j):
            return [
                eng.matmul(PU[:, 8:16], s_lstrict[:],
                           t2b_am[:, j * 8:(j + 1) * 8],
                           start=True, stop=False),
                eng.matmul(PU[:, 8:16], s_onesrow[:], crow[:],
                           start=False, stop=True)]
        pe(pfx, reads=[("t2ba",), ("crow",), ("consts",)], writes=[("PG", 3)])
        ve(lambda v, tt=tt: v.tensor_copy(posb[:, tt * 8:(tt + 1) * 8],
                                          PU[:, 8:16]),
           reads=[("PG", 3)], writes=[("posb", tt)])
        ve(lambda v: v.tensor_single_scalar(t2_val[:], PU[:, 8:16],
                                            CAP, ALU.is_lt),
           reads=[("PG", 3)], writes=[("t2val",), ("PG", 3)])
        ve(lambda v, j=j, tt=tt: v.tensor_tensor(
               ambuf[:, tt * 8:(tt + 1) * 8], t2b_am[:, j * 8:(j + 1) * 8],
               t2_val[:], ALU.mult),
           reads=[("t2ba",), ("t2val",)], writes=[("ambuf", tt)])
        ve(lambda v, j=j: v.tensor_tensor(t2_dd[:], t2b_mx1[:, j:j + 1],
                                          t2b_mx2[:, j:j + 1], ALU.subtract),
           reads=[("t2bm",), ("t2bm2",)], writes=[("t2dd",)])
        se(lambda s: s.activation(t2_w1[:], t2_dd[:], AF.Sigmoid),
           reads=[("t2dd",)], writes=[("t2w1",)])
        se(lambda s: s.activation(t2_w2[:], t2_dd[:], AF.Sigmoid, scale=-1.0),
           reads=[("t2dd",)], writes=[("t2w2",)])
        ve(lambda v, j=j: v.tensor_scalar(t2_m1[:],
                                          t2b_is1[:, j * 8:(j + 1) * 8],
                                          t2_w1[:], None, ALU.mult),
           reads=[("t2b1",), ("t2w1",)], writes=[("t2m1",)])
        ve(lambda v, j=j: v.tensor_scalar(t2_m2[:],
                                          t2b_is2[:, j * 8:(j + 1) * 8],
                                          t2_w2[:], None, ALU.mult),
           reads=[("t2b2",), ("t2w2",)], writes=[("t2m2",)])
        ve(lambda v: v.tensor_add(t2_m1[:], t2_m1[:], t2_m2[:]),
           reads=[("t2m1",), ("t2m2",)], writes=[("t2m1",)])
        ve(lambda v, tt=tt: v.tensor_tensor(
               masks[:, tt * 8:(tt + 1) * 8], t2_m1[:], t2_val[:], ALU.mult),
           reads=[("t2m1",), ("t2val",)], writes=[("masks", tt)])

        def csum1(eng, j=j):
            return [eng.matmul(PU[0:1, 0:8], s_onescolbf[:],
                               t2b_am[:, j * 8:(j + 1) * 8],
                               start=True, stop=True)]
        pe(csum1, reads=[("t2ba",), ("consts",)], writes=[("PG", 3)])
        ve(lambda v: v.tensor_copy(csrow[:], PU[0:1, 0:8]),
           reads=[("PG", 3)], writes=[("csrow",), ("PG", 3)])
        ve(lambda v: v.tensor_add(crow[:], crow[:], csrow[:]),
           reads=[("csrow",), ("crow",)], writes=[("crow",)])

    c_thunks = [(lambda j=j: c_tile(j)) for j in range(OWN0, SLOTS)]

    S.membar([("xTb", 0), ("xTb", 1), ("xsqT", 0),
              ("mrow", 0), ("sqrow", 0), ("vrow", 0), ("rrow", 0),
              ("lgs", 0), ("t2b",), ("t2bi",)])
    mem.free(["t2b_ix", "t2b_ec"] + list(reversed(AB_NAMES)))

    acc = mem.alloc("acc", [128, DCH * T])
    posb = mem.alloc("posb", [128, 64])
    ambuf = mem.alloc("ambuf", [128, 64])
    wbuf = mem.alloc("wbuf", [128, 16])
    gidxu = mem.alloc("gidxu", [128, 32], U32)
    fbrow = mem.alloc("fbrow", [1, 8])
    fb_bc = mem.alloc("fb_bc", [128, 8])
    ebase = mem.alloc("ebase", [128, 8])
    lpb = mem.alloc("lpb", [128, 8])
    a8 = mem.alloc("a8", [128, 8])
    b8 = mem.alloc("b8", [128, 8])
    vcol = mem.alloc("vcol", [128, 1])
    gcol = mem.alloc("gcol", [128, 1])
    hbuf = mem.alloc("hbuf", [128, ICH * T], BF16)
    ubuf = [mem.alloc(f"ubuf{i}", [128, DCH * 128], BF16) for i in range(3)]
    dbuf = [mem.alloc(f"dbuf{i}", [128, ICH * 128], BF16) for i in range(2)]
    hsil = mem.alloc("hsil", [128, T], BF16)
    dzb = [mem.alloc(f"dzb{i}", [128, CAPS], BF16) for i in range(4)]
    xnstage = [mem.alloc(f"xnstage{i}", [128, D], BF16) for i in range(2)]
    xntm = [mem.alloc(f"xntm{i}", [128, D], BF16) for i in range(2)]
    uslot = [0]
    dslot = [0]

    xnT3 = xnT[:].rearrange("p (dc t) -> p dc t", dc=DCH)

    def r_prep():
        gfb = next_g()
        pe(lambda eng, gfb=gfb: [eng.matmul(PG[gfb][:, 0:8], s_onesrow[:],
                                            fbrow[:], start=True, stop=True)],
           reads=[("fbrow",), ("consts",)], writes=[("PG", gfb)])
        ve(lambda v, gfb=gfb: v.tensor_copy(fb_bc[:], PG[gfb][:, 0:8]),
           reads=[("PG", gfb)], writes=[("fb_bc",), ("PG", gfb)])
        ve(lambda v: v.tensor_single_scalar(ebase[:], s_iota[:], float(CAPS),
                                            ALU.mult),
           reads=[("consts",)], writes=[("ebase",)])

    def r_thunk(tt):
        j = OWN0 + tt
        sl = tt % 2
        pcol = posb[:, tt * 8:(tt + 1) * 8]
        acol = ambuf[:, tt * 8:(tt + 1) * 8]
        ve(lambda v: v.tensor_tensor(lpb[:], pcol, fb_bc[:], ALU.subtract),
           reads=[("posb", tt), ("fb_bc",)], writes=[("lpb",)])
        ve(lambda v: v.tensor_tensor(lpb[:], lpb[:], ebase[:], ALU.add),
           reads=[("lpb",), ("ebase",)], writes=[("lpb",)])
        for k, isb, gofs, wofs in ((0, t2b_is1, 0, 0), (1, t2b_is2, 16, 8)):
            iscol = isb[:, j * 8:(j + 1) * 8]
            isref = ("t2b1",) if k == 0 else ("t2b2",)
            ve(lambda v, iscol=iscol: v.tensor_tensor(a8[:], iscol, acol,
                                                      ALU.mult),
               reads=[isref, ("ambuf", tt)], writes=[("a8",)])
            ve(lambda v: v.tensor_tensor(b8[:], a8[:], lpb[:], ALU.mult),
               reads=[("a8",), ("lpb",)], writes=[("b8",)])
            ve(lambda v: v.reduce_sum(gcol[:], b8[:], AX),
               reads=[("b8",)], writes=[("gcol",)])
            ve(lambda v: v.reduce_sum(vcol[:], a8[:], AX),
               reads=[("a8",)], writes=[("vcol",)])
            ve(lambda v: v.tensor_single_scalar(vcol[:], vcol[:], -BIG,
                                                ALU.mult),
               reads=[("vcol",)], writes=[("vcol",)])
            ve(lambda v: v.tensor_single_scalar(vcol[:], vcol[:], BIG,
                                                ALU.add),
               reads=[("vcol",)], writes=[("vcol",)])
            ve(lambda v: v.tensor_add(gcol[:], gcol[:], vcol[:]),
               reads=[("gcol",), ("vcol",)], writes=[("gcol",)])
            ve(lambda v, c=gofs + tt: v.tensor_copy(gidxu[:, c:c + 1],
                                                    gcol[:]),
               reads=[("gcol",)], writes=[("gidxu", gofs + tt)])
            ve(lambda v, iscol=iscol, tt=tt: v.tensor_tensor(
                   b8[:], iscol, masks[:, tt * 8:(tt + 1) * 8], ALU.mult),
               reads=[isref, ("masks", tt)], writes=[("b8",)])
            ve(lambda v, c=wofs + tt: v.reduce_sum(wbuf[:, c:c + 1], b8[:],
                                                   AX),
               reads=[("b8",)], writes=[("wbuf", wofs + tt)])
        ve(lambda v, tt=tt, sl=sl: v.tensor_copy(
               xnstage[sl][:].rearrange("p (a b) -> p a b", a=DCH),
               xnT3[:, :, tt * 128:(tt + 1) * 128]),
           reads=xnT_ref, writes=[("xnstage", sl)])
        dma("sync", ch["c_xs"],
            [(xntm[sl][:].rearrange("p (a b) -> p a b", a=DCH),
              xnstage[sl][:], {"transpose": True})],
            reads=[("xnstage", sl)], writes=[("xntm", sl)])
        for gofs in (0, 16):
            def sc(eng, sl=sl, c=gofs + tt):
                return eng.indirect_dma_start(
                    out=gxe, out_offset=IndirectOffsetOnAxis(
                        ap=gidxu[:, c:c + 1], axis=0),
                    in_=xntm[sl][:], in_offset=None,
                    bounds_check=NE * CAPS - 1, oob_is_err=False)
            idma(ch["c_gx"], sc,
                 reads=[("xntm", sl), ("gidxu", gofs + tt)],
                 writes=[("gxe",)])

    r_thunks = [(lambda tt=tt: r_thunk(tt)) for tt in range(NT)]

    def gb_ap(gsl):
        return xnT[:, gsl * DCH * CAPS:(gsl + 1) * DCH * CAPS] \
            .rearrange("p (dc s) -> p dc s", dc=DCH)

    def gb_mv(gsl, dc):
        base = gsl * DCH * CAPS + dc * CAPS
        return xnT[:, base:base + CAPS]

    def load_gbuf(e):
        gsl = e % 2
        dma("sync", ch["c_gb"],
            [(gb_ap(gsl), gxe[e * CAPS:(e + 1) * CAPS, :],
              {"transpose": True})],
            reads=[("gxe",)],
            writes=[("gbufs", gsl), ("xnTf", 14), ("xnTf", 15)])

    outrow = None
    UNIT_ORDER = [8, 9, 10, 11] + list(range(NE))
    for ui, u in enumerate(UNIT_ORDER):
        routed = u < 8
        if ui == 4:
            while r_thunks:
                r_thunks.pop(0)()
            S.membar([("xntm", 0), ("xntm", 1), ("xnstage", 0),
                      ("xnstage", 1)])
            mem.free(["xntm1", "xntm0", "xnstage1", "xnstage0"])
            outrow = [mem.alloc("outrow0", [128, NSC, D], BF16),
                      mem.alloc("outrow1", [128, NSC, D], BF16)]
        if routed:
            e = u
            gsl = e % 2
            if e == 0:
                load_gbuf(0)
            for ic in range(ICH):
                gus = []
                for mc in (ic, ICH + ic):
                    su_ = uslot[0] % 3
                    uslot[0] += 1
                    dma("gpsimd", ch[f"c_ub{mc % 2}"],
                        [(ubuf[su_][:], wug[e, mc], {})],
                        writes=[("ubuf", su_)])
                    gu = next_g()
                    gus.append(gu)

                    def umm(eng, su_=su_, gu=gu, gsl=gsl):
                        insts = []
                        for dc in range(DCH):
                            insts.append(eng.matmul(
                                PG[gu][:, 0:CAPS],
                                ubuf[su_][:, dc * 128:(dc + 1) * 128],
                                gb_mv(gsl, dc),
                                start=(dc == 0), stop=(dc == DCH - 1)))
                        return insts
                    pe(umm, reads=[("ubuf", su_), ("gbufs", gsl)],
                       writes=[("PG", gu)])
                gg_, gu_ = gus
                se(lambda s, gg_=gg_: s.activation(hsil[:, 0:CAPS],
                                                   PG[gg_][0:128, 0:CAPS],
                                                   AF.Silu),
                   reads=[("PG", gg_)], writes=[("hsil",), ("PG", gg_)])

                def hb(v, ic=ic, gu_=gu_):
                    return v.scalar_tensor_tensor(
                        hbuf[:, ic * CAPS:(ic + 1) * CAPS],
                        PG[gu_][0:128, 0:CAPS], 0.0, hsil[:, 0:CAPS],
                        ALU.bypass, ALU.mult)
                ve(hb, reads=[("PG", gu_), ("hsil",)],
                   writes=[("h", ic), ("PG", gu_)])
            if e + 1 < NE:
                load_gbuf(e + 1)
            for dcc in range(DCH):
                sd_ = dslot[0] % 2
                dslot[0] += 1
                dma("gpsimd", ch[f"c_db{sd_}"],
                    [(dbuf[sd_][:], wdn[e, dcc], {})],
                    writes=[("dbuf", sd_)])
                gd = next_g()

                def dmm(eng, sd_=sd_, gd=gd):
                    insts = []
                    for ic in range(ICH):
                        insts.append(eng.matmul(
                            PG[gd][:, 0:CAPS],
                            dbuf[sd_][:, ic * 128:(ic + 1) * 128],
                            hbuf[:, ic * CAPS:(ic + 1) * CAPS],
                            start=(ic == 0), stop=(ic == ICH - 1)))
                    return insts
                pe(dmm, reads=[("dbuf", sd_)]
                   + [("h", ic) for ic in range(ICH)],
                   writes=[("PG", gd)])
                zsl = dcc % 4
                ve(lambda v, gd=gd, zsl=zsl: v.tensor_copy(
                       dzb[zsl][:], PG[gd][0:128, 0:CAPS]),
                   reads=[("PG", gd)], writes=[("dzb", zsl), ("PG", gd)])
                dma("sync", ch["c_or"],
                    [(outrow[e % 2][:, :, dcc * 128:(dcc + 1) * 128],
                      dzb[zsl][:], {"transpose": True})],
                    reads=[("dzb", zsl)], writes=[("outrow", e % 2, dcc)])
            dma("scalar", ch["c_ox"],
                [(oxe[e * CAPS:(e + 1) * CAPS, :]
                  .rearrange("(sc p) d -> p sc d", p=128),
                  outrow[e % 2][:].rearrange("p a b -> p (a b)"), {})],
                reads=[("outrow", e % 2, dcc) for dcc in range(DCH)],
                writes=[("oxe", e)])
            continue

        g0 = u - 8


        for ic in range(ICH):
            if ui <= 1 and ic % 3 == 0 and c_thunks:
                c_thunks.pop(0)()
            if ui >= 2 and ic % 3 == 0 and r_thunks:
                r_thunks.pop(0)()
            gus = []
            for mc in (ic, ICH + ic):
                su_ = uslot[0] % 3
                uslot[0] += 1
                dma("gpsimd", ch[f"c_u{mc % 2}"],
                    [(ubuf[su_][:], sug[g0, mc], {})],
                    writes=[("ubuf", su_)])
                gu = next_g()
                gus.append(gu)

                def umm(eng, su_=su_, gu=gu):
                    insts = []
                    for dc in range(DCH):
                        lhs = ubuf[su_][:, dc * 128:(dc + 1) * 128]
                        for q in range(2):
                            insts.append(eng.matmul(
                                PG[gu][:, q * 512:(q + 1) * 512], lhs,
                                xnT[:, dc * T + q * 512:
                                    dc * T + q * 512 + 512],
                                start=(dc == 0), stop=(dc == DCH - 1)))
                    return insts
                pe(umm, reads=[("ubuf", su_)] + xnT_ref, writes=[("PG", gu)])
            gg_, gu_ = gus
            se(lambda s, gg_=gg_: s.activation(hsil[:],
                                               PG[gg_][0:128, 0:1024],
                                               AF.Silu),
               reads=[("PG", gg_)], writes=[("hsil",), ("PG", gg_)])

            def hb(v, ic=ic, gu_=gu_):
                return v.scalar_tensor_tensor(
                    hbuf[:, ic * T:(ic + 1) * T],
                    PG[gu_][0:128, 0:1024], 0.0, hsil[:],
                    ALU.bypass, ALU.mult)
            ve(hb, reads=[("PG", gu_), ("hsil",)],
               writes=[("h", ic), ("PG", gu_)])
        if ui == 1:
            while c_thunks:
                c_thunks.pop(0)()
            r_prep()
        for dcc in range(DCH):
            sd_ = dslot[0] % 2
            dslot[0] += 1
            dma("gpsimd", ch[f"c_d{sd_}"],
                [(dbuf[sd_][:], sdn[g0, dcc], {})],
                writes=[("dbuf", sd_)])
            gd = next_g()

            def dmm(eng, sd_=sd_, gd=gd):
                insts = []
                for ic in range(ICH):
                    lhs = dbuf[sd_][:, ic * 128:(ic + 1) * 128]
                    for q in range(2):
                        insts.append(eng.matmul(
                            PG[gd][:, q * 512:(q + 1) * 512], lhs,
                            hbuf[:, ic * T + q * 512:
                                   ic * T + q * 512 + 512],
                            start=(ic == 0), stop=(ic == ICH - 1)))
                return insts
            pe(dmm, reads=[("dbuf", sd_)] + [("h", ic) for ic in range(ICH)],
               writes=[("PG", gd)])
            if ui == 0:
                ve(lambda v, dcc=dcc, gd=gd: v.tensor_copy(
                       acc[:, dcc * T:(dcc + 1) * T], PG[gd][0:128, 0:1024]),
                   reads=[("PG", gd)], writes=[("acc", dcc), ("PG", gd)])
            else:
                ve(lambda v, dcc=dcc, gd=gd: v.tensor_tensor(
                       acc[:, dcc * T:(dcc + 1) * T],
                       acc[:, dcc * T:(dcc + 1) * T],
                       PG[gd][0:128, 0:1024], ALU.add),
                   reads=[("PG", gd), ("acc", dcc)],
                   writes=[("acc", dcc), ("PG", gd)])

    S.membar([("hsil",), ("gbufs", 0), ("gbufs", 1)]
             + [("dzb", i) for i in range(4)]
             + [("outrow", i % 2, dcc) for i in range(2)
                for dcc in range(DCH)]
             + [("dbuf", i) for i in range(2)]
             + [("ubuf", i) for i in range(3)]
             + [("h", ic) for ic in range(ICH)])
    mem.free(["outrow1", "outrow0", "dzb3", "dzb2", "dzb1", "dzb0", "hsil",
              "dbuf1", "dbuf0", "ubuf2", "ubuf1", "ubuf0", "hbuf"])

    g1 = [mem.alloc("g1a", [128, D], BF16), mem.alloc("g1b", [128, D], BF16)]
    g2 = [mem.alloc("g2a", [128, D], BF16), mem.alloc("g2b", [128, D], BF16)]
    tc1 = mem.alloc("tc1", [128, T])
    tc2 = mem.alloc("tc2", [128, T])
    silt = mem.alloc("silt", [128, T])
    xrl = [mem.alloc("xrl0", [128, D]), mem.alloc("xrl1", [128, D])]
    outt = [mem.alloc("outt0", [128, D]), mem.alloc("outt1", [128, D])]
    sgv = sig_g[:].rearrange("p (ec a) -> p ec a", ec=DCH)
    oxe_refs = [("oxe", e) for e in range(NE)]
    for tt in range(NT):
        so = tt % 2
        ge(lambda g, so=so: g.memset(g1[so][:], 0.0),
           writes=[("g1", so)])
        ge(lambda g, so=so: g.memset(g2[so][:], 0.0),
           writes=[("g2", so)])

        def ga1(eng, so=so, tt=tt):
            return eng.indirect_dma_start(
                out=g1[so][:], out_offset=None, in_=oxe,
                in_offset=IndirectOffsetOnAxis(
                    ap=gidxu[:, tt:tt + 1], axis=0),
                bounds_check=NE * CAPS - 1, oob_is_err=False)
        idma(ch["c_gt"], ga1,
             reads=oxe_refs + [("g1", so), ("gidxu", tt)],
             writes=[("g1", so)])

        def ga2(eng, so=so, tt=tt):
            return eng.indirect_dma_start(
                out=g2[so][:], out_offset=None, in_=oxe,
                in_offset=IndirectOffsetOnAxis(
                    ap=gidxu[:, 16 + tt:17 + tt], axis=0),
                bounds_check=NE * CAPS - 1, oob_is_err=False)
        idma(ch["c_gt"], ga2,
             reads=oxe_refs + [("g2", so), ("gidxu", 16 + tt)],
             writes=[("g2", so)])
        dma("sync", ch[f"c_xr{so}"],
            [(xrl[so][:], xo[tt * 128:(tt + 1) * 128, :], {})],
            writes=[("xrl", so)])
        for half in range(2):
            gf = next_g()

            def ftp(eng, tt=tt, half=half, gf=gf):
                insts = []
                for k in range(8):
                    dcc = half * 8 + k
                    insts.append(eng.transpose(
                        PG[gf][:, k * 128:(k + 1) * 128],
                        acc[:, dcc * T + tt * 128: dcc * T + (tt + 1) * 128],
                        s_idf[:]))
                return insts
            pe(ftp, reads=[("acc", half * 8 + k) for k in range(8)]
               + [("consts",)],
               writes=[("PG", gf)])
            hs = slice(half * 1024, (half + 1) * 1024)
            ve(lambda v, so=so, tt=tt, hs=hs: v.tensor_scalar(
                   tc2[:], g2[so][:, hs], wbuf[:, 8 + tt:9 + tt], None,
                   ALU.mult),
               reads=[("g2", so), ("wbuf", 8 + tt)], writes=[("tc2",)])
            ve(lambda v, so=so, tt=tt, hs=hs: v.scalar_tensor_tensor(
                   tc1[:], g1[so][:, hs], wbuf[:, tt:tt + 1], tc2[:],
                   ALU.mult, ALU.add),
               reads=[("g1", so), ("wbuf", tt), ("tc2",)],
               writes=[("tc1",)])
            ve(lambda v, gf=gf: v.tensor_tensor(tc1[:], tc1[:],
                                                PG[gf][0:128, 0:1024],
                                                ALU.add),
               reads=[("tc1",), ("PG", gf)], writes=[("tc1",), ("PG", gf)])
            se(lambda s: s.activation(silt[:], tc1[:], AF.Silu),
               reads=[("tc1",)], writes=[("silt",)])
            ve(lambda v, tt=tt, half=half: v.tensor_tensor(
                   tc1[:].rearrange("p (a b) -> p a b", a=8),
                   silt[:].rearrange("p (a b) -> p a b", a=8),
                   sgv[:, half * 8:(half + 1) * 8,
                       tt * 128:(tt + 1) * 128],
                   ALU.mult),
               reads=[("silt",)] + [("stm", ec)
                                    for ec in range(half * 8, half * 8 + 8)],
               writes=[("tc1",)])
            ve(lambda v, so=so, hs=hs: v.tensor_tensor(
                   outt[so][:, hs], tc1[:], xrl[so][:, hs], ALU.add),
               reads=[("tc1",), ("xrl", so)], writes=[("outt", so, half)])
        dma("sync", ch["c_out"],
            [(out_hbm[tt * 128:(tt + 1) * 128, :], outt[so][:], {})],
            reads=[("outt", so, 0), ("outt", so, 1)],
            writes=[("outdma", tt)])

    def fin(eng, sem, n):
        eng.nop().then_inc(sem, n)
    S.add("gpsimd", fin, chan=ch["fin"],
          extra_waits=[(ch["c_out"], ch["c_out"].total)])

    S.emit(nc, es)
    es.close()
    nc.finalize()
    return nc


_NC_CACHE = {}


def _consts():
    c = {}
    c["c_idbf"] = np.eye(128, dtype=bfnp)
    c["c_idf"] = np.eye(128, dtype=np.float32)
    c["c_onesrow"] = np.ones((1, 128), dtype=np.float32)
    c["c_onescolf"] = np.ones((128, 1), dtype=np.float32)
    c["c_onescolbf"] = np.ones((128, 1), dtype=bfnp)
    c["c_lstrict"] = np.triu(np.ones((128, 128), dtype=bfnp), k=1)
    c["c_iota"] = np.broadcast_to(np.arange(8, dtype=np.float32),
                                  (128, 8)).copy()
    return c


def make_in_maps(L0_attn_x, L0_attn_Wq, L0_attn_bq, L0_attn_Wk, L0_attn_bk,
                 L0_attn_Wv, L0_attn_bv, L0_attn_Wg, L0_attn_bg,
                 L0_ffn_router, L0_ffn_w_up_gate, L0_ffn_w_down,
                 L0_ffn_shared_w_up_gate, L0_ffn_shared_w_down):
    xflat = np.ascontiguousarray(np.asarray(L0_attn_x, dtype=np.float32)
                                 .reshape(TG, D))
    bcols = np.concatenate(
        [np.asarray(b, np.float32).reshape(16, 128).T
         for b in (L0_attn_bq, L0_attn_bk, L0_attn_bv, L0_attn_bg)], axis=1)
    rh = np.ascontiguousarray(
        np.asarray(L0_ffn_router, np.float32).reshape(16, 128, 8)
        .transpose(1, 0, 2).reshape(128, 128))
    wqT = np.ascontiguousarray(np.asarray(L0_attn_Wq, np.float32).T)
    wkT = np.ascontiguousarray(np.asarray(L0_attn_Wk, np.float32).T)
    wvT = np.ascontiguousarray(np.asarray(L0_attn_Wv, np.float32).T)
    wgT = np.ascontiguousarray(np.asarray(L0_attn_Wg, np.float32).T)
    wug_t = np.ascontiguousarray(
        np.asarray(L0_ffn_w_up_gate, np.float32)
        .reshape(8, 16, 128, 22, 128).transpose(0, 3, 2, 1, 4)
        .reshape(8, 22, 128, 2048))
    wdn_t = np.ascontiguousarray(
        np.asarray(L0_ffn_w_down, np.float32)
        .reshape(8, 11, 128, 16, 128).transpose(0, 3, 2, 1, 4)
        .reshape(8, 16, 128, 1408))
    sug_t = np.ascontiguousarray(
        np.asarray(L0_ffn_shared_w_up_gate, np.float32)
        .reshape(16, 128, 2, 4, 11, 128).transpose(3, 2, 4, 1, 0, 5)
        .reshape(4, 22, 128, 2048))
    sdn_t = np.ascontiguousarray(
        np.asarray(L0_ffn_shared_w_down, np.float32)
        .reshape(4, 11, 128, 16, 128).transpose(0, 3, 2, 1, 4)
        .reshape(4, 16, 128, 1408))
    consts = _consts()

    in_maps = []
    for c in range(8):
        sm = np.zeros((128, SLOTS), dtype=np.float32)
        for j in range(SLOTS):
            g = (j + 8 * (c + 1)) % 64
            if g < 8 * (c + 1):
                sm[:, j] = 1.0
        cm = np.full((128, 1), 1.0 if c % 2 == 1 else 0.0, dtype=np.float32)
        m = {
            "xgT": np.ascontiguousarray(
                np.roll(xflat, -1024 * (c + 1), axis=0).T),
            "xo": np.ascontiguousarray(xflat[1024 * c:1024 * (c + 1)]),
            "wq": wqT, "wk": wkT, "wv": wvT, "wgt": wgT,
            "bcols": bcols,
            "rhost": rh,
            "wug": wug_t,
            "wdn": wdn_t,
            "sug": sug_t,
            "sdn": sdn_t,
            "c_slotmask": sm,
            "c_carrymask": cm,
        }
        m.update(consts)
        in_maps.append(m)
    return in_maps


def kernel(**inputs):
    if "nc" not in _NC_CACHE:
        _NC_CACHE["nc"] = build_nc()
    nc = _NC_CACHE["nc"]
    in_maps = make_in_maps(**inputs)
    res = run_bass_kernel_spmd(nc, in_maps, core_ids=list(range(8)))
    out = np.concatenate([res.results[c]["out"] for c in range(8)], axis=0)
    return out.reshape(4, 2048, D)


# version 15
# speedup vs baseline: 1.3120x; 1.0189x over previous
import numpy as np
import ml_dtypes
import concourse.bass as bass
import concourse.mybir as mybir
from concourse.bass import IndirectOffsetOnAxis
from concourse.bass_utils import run_bass_kernel_spmd
from contextlib import ExitStack

F32 = mybir.dt.float32
BF16 = mybir.dt.bfloat16
U32 = mybir.dt.uint32
AF = mybir.ActivationFunctionType
ALU = mybir.AluOpType
AX = mybir.AxisListType.X

bfnp = ml_dtypes.bfloat16

D = 2048
DCH = 16
T = 1024
NT = 8
TG = 8192
SLOTS = 64
OWN0 = 56
I_FF = 1408
ICH = 11
CAP = 2560.0
CAPS = 384
NSC = 3
NE = 8
BIG = 65536.0


class Chan:
    __slots__ = ("name", "total", "sem")

    def __init__(self, name):
        self.name = name
        self.total = 0
        self.sem = None


class Sched:
    ENGINES = ("tensor", "vector", "scalar", "sync", "gpsimd")

    def __init__(self):
        self.plan = {e: [] for e in self.ENGINES}
        self.chans = []
        self.last_w = {}
        self.readers = {}
        self.bar = {}

    def chan(self, name):
        c = Chan(name)
        self.chans.append(c)
        return c

    def membar(self, refs):
        for r in refs:
            evs = []
            if r in self.last_w:
                evs.append(self.last_w[r])
            evs.extend(self.readers.get(r, ()))
            for c, v in evs:
                if self.bar.get(c, 0) < v:
                    self.bar[c] = v

    def add(self, engine, fn, reads=(), writes=(), chan=None, n_inc=1,
            extra_waits=()):
        waits = dict(self.bar)

        def need(ev):
            c, v = ev
            if waits.get(c, 0) < v:
                waits[c] = v

        for r in reads:
            if r in self.last_w:
                need(self.last_w[r])
        for r in writes:
            if r in self.last_w:
                need(self.last_w[r])
            for ev in self.readers.get(r, ()):
                need(ev)
        for ev in extra_waits:
            need(ev)
        chan.total += n_inc
        ev = (chan, chan.total)
        for r in reads:
            self.readers.setdefault(r, []).append(ev)
        for r in writes:
            self.last_w[r] = ev
            self.readers[r] = []
        self.plan[engine].append(
            (sorted(waits.items(), key=lambda kv: kv[0].name), fn, chan, n_inc))
        return ev

    def emit(self, nc, es):
        for c in self.chans:
            c.sem = es.enter_context(nc.semaphore(c.name))
        block = es.enter_context(nc.Block())
        for eng_name in self.ENGINES:
            steps = self.plan[eng_name]
            if not steps:
                continue
            dec = getattr(block, eng_name)

            def make(steps):
                def body(eng):
                    hwm = {}
                    for waits, fn, chan, n_inc in steps:
                        for c, v in waits:
                            if hwm.get(id(c), 0) < v:
                                eng.wait_ge(c.sem, v)
                                hwm[id(c)] = v
                        fn(eng, chan.sem, n_inc)
                return body

            dec(make(steps))


class SbufScope:

    def __init__(self, nc):
        self.nc = nc
        self.stack = []

    def alloc(self, name, shape, dt=F32):
        cm = self.nc.sbuf_tensor(name, list(shape), dt)
        h = cm.__enter__()
        self.stack.append((name, cm))
        return h

    def free(self, names):
        for n in names:
            top_name, cm = self.stack.pop()
            assert top_name == n, f"non-LIFO free: {n} vs top {top_name}"
            cm.__exit__(None, None, None)


def build_nc():
    nc = bass.Bass("TRN2", target_bir_lowering=False, debug=False,
                   num_devices=8)
    es = ExitStack()
    mem = SbufScope(nc)

    def din(name, shape, dt=F32):
        return nc.dram_tensor(name, list(shape), dt, kind="ExternalInput").ap()

    xgT = din("xgT", [D, TG])
    xo = din("xo", [T, D])
    wsrcT = {"q": din("wq", [D, D]), "k": din("wk", [D, D]),
             "v": din("wv", [D, D]), "g": din("wgt", [D, D])}
    bcols_h = din("bcols", [128, 64])
    rhost = din("rhost", [128, 128])
    wug = din("wug", [8, 22, 128, D])
    wdn = din("wdn", [8, DCH, 128, I_FF])
    sug = din("sug", [4, 22, 128, D])
    sdn = din("sdn", [4, DCH, 128, I_FF])
    c_idbf = din("c_idbf", [128, 128], BF16)
    c_idf = din("c_idf", [128, 128])
    c_onesrow = din("c_onesrow", [1, 128])
    c_onescolf = din("c_onescolf", [128, 1])
    c_onescolbf = din("c_onescolbf", [128, 1], BF16)
    c_lstrict = din("c_lstrict", [128, 128], BF16)
    c_iota = din("c_iota", [128, 8])
    c_slotmask = din("c_slotmask", [128, 64])
    c_carrymask = din("c_carrymask", [128, 1])
    out_hbm = nc.dram_tensor("out", [T, D], F32, kind="ExternalOutput").ap()

    gxe = nc.dram_tensor("gxe", [NE * CAPS, D], BF16, kind="Internal").ap()
    oxe = nc.dram_tensor("oxe", [NE * CAPS, D], BF16, kind="Internal").ap()

    P = mem.alloc
    s_idbf = P("s_idbf", [128, 128], BF16)
    s_idf = P("s_idf", [128, 128])
    s_onesrow = P("s_onesrow", [1, 128])
    s_onescolf = P("s_onescolf", [128, 1])
    s_onescolbf = P("s_onescolbf", [128, 1], BF16)
    s_lstrict = P("s_lstrict", [128, 128], BF16)
    s_iota = P("s_iota", [128, 8])
    s_slotmask = P("s_slotmask", [128, 64])
    s_carrymask = P("s_carrymask", [128, 1])
    s_bcols = P("s_bcols", [128, 64])
    s_r = P("s_r", [128, 128], BF16)
    xnT = P("xnT", [128, DCH * T], BF16)
    logitsT = P("logitsT", [128, SLOTS * 8])
    mcol = P("mcol", [128, SLOTS])
    rcol = P("rcol", [128, SLOTS])
    bcol2 = P("bcol2", [128, 1])
    sRbc = P("sRbc", [128, 8])
    sRrow = P("sRrow", [1, 8])
    masks = P("masks", [128, NT * 8])
    carryT = P("carryT", [128, 16])
    carryTm = P("carryTm", [128, 16])
    crow = P("crow", [1, 8])
    csrow = P("csrow", [1, 8])
    t2_mx1 = P("t2_mx1", [128, 1]); t2_mx2 = P("t2_mx2", [128, 1])
    t2_eq = P("t2_eq", [128, 8]); t2_cnd = P("t2_cnd", [128, 8])
    t2_ix = P("t2_ix", [128, 1])
    t2_is1 = P("t2_is1", [128, 8]); t2_is2 = P("t2_is2", [128, 8])
    t2_l2 = P("t2_l2", [128, 8]); t2_a = P("t2_a", [128, 8])
    t2_am = P("t2_am", [128, 8])
    t2_dd = P("t2_dd", [128, 1])
    t2_w1 = P("t2_w1", [128, 1]); t2_w2 = P("t2_w2", [128, 1])
    t2_m1 = P("t2_m1", [128, 8]); t2_m2 = P("t2_m2", [128, 8])
    t2_val = P("t2_val", [128, 8])
    xnTp = P("xnTp", [128, DCH * T], BF16)

    PG = [es.enter_context(nc.psum_tensor(f"PG{i}", [128, 1024], F32))
          for i in range(4)]
    PU = PG[3]

    S = Sched()
    ch = {n: S.chan(n) for n in
          ["pe", "ve", "se", "ge", "fin",
           "c_const", "c_router",
           "c_xT0", "c_xT1", "c_wt0", "c_wt1",
           "c_u0", "c_u1", "c_d0", "c_d1",
           "c_ub0", "c_ub1", "c_db0", "c_db1",
           "c_gx", "c_xs", "c_gb", "c_or", "c_ox", "c_gt",
           "c_xr0", "c_xr1", "c_out", "c_so0", "c_so1"]}
    PE, VE, SE, GE = ch["pe"], ch["ve"], ch["se"], ch["ge"]

    def dma(queue, chan, pairs, reads=(), writes=()):
        def fn(eng, sem, n, pairs=pairs):
            for o, i, kw in pairs:
                eng.dma_start(out=o, in_=i, **kw).then_inc(sem, 16)
        prev = (chan, chan.total)
        return S.add(queue, fn, reads=reads, writes=writes, chan=chan,
                     n_inc=16 * len(pairs),
                     extra_waits=[prev] if chan.total else [])

    def idma(chan, fn, reads=(), writes=()):
        def wrap(eng, sem, n, fn=fn):
            fn(eng).then_inc(sem, 16)
        prev = (chan, chan.total)
        return S.add("gpsimd", wrap, reads=reads, writes=writes, chan=chan,
                     n_inc=16, extra_waits=[prev] if chan.total else [])

    def pe(fn, reads=(), writes=()):
        def wrap(eng, sem, n, fn=fn):
            insts = fn(eng)
            insts[-1].then_inc(sem, 1)
        return S.add("tensor", wrap, reads=reads, writes=writes, chan=PE)

    def ve(fn, reads=(), writes=()):
        def wrap(eng, sem, n, fn=fn):
            fn(eng).then_inc(sem, 1)
        return S.add("vector", wrap, reads=reads, writes=writes, chan=VE)

    def se(fn, reads=(), writes=()):
        def wrap(eng, sem, n, fn=fn):
            fn(eng).then_inc(sem, 1)
        return S.add("scalar", wrap, reads=reads, writes=writes, chan=SE)

    def ge(fn, reads=(), writes=()):
        def wrap(eng, sem, n, fn=fn):
            fn(eng).then_inc(sem, 1)
        return S.add("gpsimd", wrap, reads=reads, writes=writes, chan=GE)

    dma("sync", ch["c_const"], [
        (s_idbf[:], c_idbf, {}), (s_idf[:], c_idf, {}),
        (s_onesrow[:], c_onesrow, {}), (s_onescolf[:], c_onescolf, {}),
        (s_onescolbf[:], c_onescolbf, {}), (s_lstrict[:], c_lstrict, {}),
        (s_iota[:], c_iota, {}),
        (s_slotmask[:], c_slotmask, {}), (s_carrymask[:], c_carrymask, {}),
        (s_bcols[:], bcols_h, {}),
    ], writes=[("consts",)])
    dma("gpsimd", ch["c_router"], [(s_r[:], rhost, {})], writes=[("router",)])
    t2b_is1 = mem.alloc("t2b_is1", [128, 512], BF16)
    t2b_is2 = mem.alloc("t2b_is2", [128, 512], BF16)
    t2b_am = mem.alloc("t2b_am", [128, 512], BF16)
    t2b_mx1 = mem.alloc("t2b_mx1", [128, 64])
    t2b_mx2 = mem.alloc("t2b_mx2", [128, 64])

    AB_NAMES = []

    def AB(name, shape, dt=F32):
        AB_NAMES.append(name)
        return mem.alloc(name, shape, dt)

    xTb = [AB("xTb0", [128, DCH * 512], BF16),
           AB("xTb1", [128, DCH * 512], BF16)]
    xsqT = [AB("xsqT0", [128, 512], BF16), AB("xsqT1", [128, 512], BF16)]
    _mr = AB("mrow0", [1, 512]); mrow = [_mr, _mr]
    _sq = AB("sqrow0", [1, 512]); sqrow = [_sq, _sq]
    _vr = AB("vrow0", [1, 512]); vrow = [_vr, _vr]
    _rr = AB("rrow0", [1, 512]); rrow = [_rr, _rr]
    _lg = AB("lgs0", [8, 512]); lgs = [_lg, _lg]
    MRP_NAMES = []

    def MRP(name, shape, dt=F32):
        MRP_NAMES.append(name)
        return mem.alloc(name, shape, dt)

    mbc_sb = MRP("mbc_sb", [128, 512])
    rbc_sb = MRP("rbc_sb", [128, 512])
    pf_sb = MRP("pf_sb", [128, 512])

    def sr_mm(eng):
        insts = []
        for dc in range(DCH):
            insts.append(eng.matmul(PU[0:1, 0:8], s_onescolbf[:],
                                    s_r[:, dc * 8:(dc + 1) * 8],
                                    start=(dc == 0), stop=(dc == DCH - 1)))
        return insts
    pe(sr_mm, reads=[("router",), ("consts",)], writes=[("PG", 3)])
    ve(lambda v: v.tensor_copy(sRrow[:], PU[0:1, 0:8]),
       reads=[("PG", 3)], writes=[("sRrow",), ("PG", 3)])
    pe(lambda eng: [eng.matmul(PU[:, 0:8], s_onesrow[:], sRrow[:],
                               start=True, stop=True)],
       reads=[("sRrow",), ("consts",)], writes=[("PG", 3)])
    ve(lambda v: v.tensor_copy(sRbc[:], PU[:, 0:8]),
       reads=[("PG", 3)], writes=[("sRbc",), ("PG", 3)])

    def b_layout(tch):
        sl = tch % 2
        folded = tch >= 12
        if tch in (12, 13):
            dst, col0 = xnTp, (tch - 12) * 512
        elif tch in (14, 15):
            dst, col0 = xnT, (tch - 14) * 512
        else:
            dst, col0 = xTb[sl], 0
        dstref = ("xnTraw", tch) if folded else ("xTb", sl)
        pitch = T if folded else 512
        return sl, folded, dst, dstref, col0, pitch

    def plan_B_dma(tch):
        sl, folded, dst, dstref, col0, pitch = b_layout(tch)
        src = xgT[:, tch * 512:(tch + 1) * 512] \
            .rearrange("(dc p) t -> p dc t", p=128)
        if folded:
            dview = dst[:].rearrange("p (dc t) -> p dc t", dc=DCH) \
                [:, :, col0:col0 + 512]
        else:
            dview = dst[:]
        dma("gpsimd", ch[f"c_xT{sl}"], [(dview, src, {})], writes=[dstref])

    def plan_B_compute(tch):
        sl, folded, dst, dstref, col0, pitch = b_layout(tch)

        g = tch % 3
        for dc in range(DCH):
            def xt_ap(dc=dc, dst=dst, col0=col0, pitch=pitch):
                return dst[:, dc * pitch + col0: dc * pitch + col0 + 512]
            s2 = dc % 2
            ve(lambda v, s2=s2, xt=xt_ap: v.tensor_mul(xsqT[s2][:], xt(),
                                                       xt()),
               reads=[dstref], writes=[("xsqT", s2)])

            def mms(eng, dc=dc, s2=s2, g=g, xt=xt_ap):
                return [
                    eng.matmul(PU[0:1, 0:512], s_onescolbf[:], xt(),
                               start=(dc == 0), stop=(dc == DCH - 1)),
                    eng.matmul(PG[g][32:33, 512:1024], s_onescolbf[:],
                               xsqT[s2][:],
                               start=(dc == 0), stop=(dc == DCH - 1)),
                    eng.matmul(PG[g][0:8, 0:512], s_r[:, dc * 8:(dc + 1) * 8],
                               xt(), start=(dc == 0), stop=(dc == DCH - 1)),
                ]
            pe(mms, reads=[dstref, ("xsqT", s2), ("router",), ("consts",)],
               writes=[("PG", 3), ("PG", g)])

        ve(lambda v, sl=sl: v.tensor_single_scalar(mrow[sl][:],
                                                   PU[0:1, 0:512],
                                                   1.0 / D, ALU.mult),
           reads=[("PG", 3)], writes=[("mrow", 0), ("PG", 3)])
        ve(lambda v, sl=sl, g=g: v.tensor_single_scalar(
               sqrow[sl][:], PG[g][32:33, 512:1024], 1.0 / D, ALU.mult),
           reads=[("PG", g)], writes=[("sqrow", 0)])
        ve(lambda v, sl=sl: v.tensor_mul(vrow[sl][:], mrow[sl][:],
                                         mrow[sl][:]),
           reads=[("mrow", 0)], writes=[("vrow", 0)])
        ve(lambda v, sl=sl: v.tensor_tensor(vrow[sl][:], sqrow[sl][:],
                                            vrow[sl][:], ALU.subtract),
           reads=[("sqrow", 0), ("vrow", 0)], writes=[("vrow", 0)])
        ve(lambda v, sl=sl: v.tensor_single_scalar(vrow[sl][:], vrow[sl][:],
                                                   1e-5, ALU.add),
           reads=[("vrow", 0)], writes=[("vrow", 0)])
        se(lambda s, sl=sl: s.activation(vrow[sl][:], vrow[sl][:], AF.Ln),
           reads=[("vrow", 0)], writes=[("vrow", 0)])
        se(lambda s, sl=sl: s.activation(rrow[sl][:], vrow[sl][:], AF.Exp,
                                         scale=-0.5),
           reads=[("vrow", 0)], writes=[("rrow", 0)])

        ve(lambda v, g=g, sl=sl: v.tensor_copy(lgs[sl][:], PG[g][0:8, 0:512]),
           reads=[("PG", g)], writes=[("lgs", 0), ("PG", g)])

        def tpb(eng, sl=sl):
            insts = []
            for q in range(4):
                insts.append(eng.transpose(
                    PU[:, q:q + 1], mrow[sl][0:1, q * 128:(q + 1) * 128],
                    s_idf[0:1, 0:1]))
                insts.append(eng.transpose(
                    PU[:, 4 + q:5 + q], rrow[sl][0:1, q * 128:(q + 1) * 128],
                    s_idf[0:1, 0:1]))
                insts.append(eng.transpose(
                    PU[:, 8 + q * 8: 16 + q * 8],
                    lgs[sl][0:8, q * 128:(q + 1) * 128], s_idf[0:8, 0:8]))
            return insts
        pe(tpb, reads=[("mrow", 0), ("rrow", 0), ("lgs", 0), ("consts",)],
           writes=[("PG", 3)])

        def fold_ev(v, tch=tch):
            for q in range(4):
                j = tch * 4 + q
                v.tensor_copy(mcol[:, j:j + 1], PU[:, q:q + 1])
                v.tensor_copy(rcol[:, j:j + 1], PU[:, 4 + q:5 + q])
                inst = v.tensor_copy(logitsT[:, j * 8:(j + 1) * 8],
                                     PU[:, 8 + q * 8:16 + q * 8])
            return inst
        ve(fold_ev, reads=[("PG", 3)],
           writes=[("mrcol", tch), ("lraw", tch), ("PG", 3)])

        for q in range(4):
            j = tch * 4 + q
            ve(lambda v, j=j: v.tensor_mul(bcol2[:], mcol[:, j:j + 1],
                                           rcol[:, j:j + 1]),
               reads=[("mrcol", tch)], writes=[("bcol2",)])
            ve(lambda v, j=j: v.tensor_scalar(
                   logitsT[:, j * 8:(j + 1) * 8],
                   logitsT[:, j * 8:(j + 1) * 8],
                   rcol[:, j:j + 1], None, ALU.mult),
               reads=[("mrcol", tch), ("lraw", tch)],
               writes=[("lraw", tch)])
            ve(lambda v: v.tensor_scalar(t2_eq[:], sRbc[:], bcol2[:], None,
                                         ALU.mult),
               reads=[("sRbc",), ("bcol2",)], writes=[("t2",)])
            ve(lambda v, j=j: v.tensor_tensor(
                   logitsT[:, j * 8:(j + 1) * 8],
                   logitsT[:, j * 8:(j + 1) * 8],
                   t2_eq[:], ALU.subtract),
               reads=[("t2",), ("lraw", tch)],
               writes=[("lraw", tch)])
        S.last_w[("logits", tch)] = S.last_w[("lraw", tch)]

        if folded:
            g2 = (g + 1) % 3

            def bcst(eng, sl=sl, g2=g2):
                return [
                    eng.matmul(PG[g2][:, 0:512], s_onesrow[:], mrow[sl][:],
                               start=True, stop=True),
                    eng.matmul(PG[g2][:, 512:1024], s_onesrow[:], rrow[sl][:],
                               start=True, stop=True)]
            pe(bcst, reads=[("mrow", 0), ("rrow", 0), ("consts",)],
               writes=[("PG", g2)])
            ve(lambda v, g2=g2: v.tensor_copy(mbc_sb[:], PG[g2][:, 0:512]),
               reads=[("PG", g2)], writes=[("mbc",)])
            ve(lambda v, g2=g2: v.tensor_copy(rbc_sb[:], PG[g2][:, 512:1024]),
               reads=[("PG", g2)], writes=[("rbc",), ("PG", g2)])
            for dc in range(DCH):
                ve(lambda v, dc=dc, col0=col0, dst=dst: v.tensor_tensor(
                       pf_sb[:],
                       dst[:, dc * T + col0: dc * T + col0 + 512],
                       mbc_sb[:], ALU.subtract),
                   reads=[("xnTraw", tch), ("mbc",)], writes=[("pf",)])
                ve(lambda v, dc=dc, col0=col0, dst=dst: v.tensor_tensor(
                       dst[:, dc * T + col0: dc * T + col0 + 512],
                       pf_sb[:], rbc_sb[:], ALU.mult),
                   reads=[("pf",), ("rbc",)],
                   writes=[("xnTraw", tch), ("xnTf", tch)])


    b_early = [14, 15, 12, 13]
    for k, tch in enumerate(b_early):
        plan_B_dma(tch)
    for tch in b_early:
        plan_B_compute(tch)
    S.membar([("mbc",), ("rbc",), ("pf",)])
    mem.free(list(reversed(MRP_NAMES)))
    t2b_ec = mem.alloc("t2b_ec", [128, 512])
    t2b_ix = mem.alloc("t2b_ix", [128, 64])

    xnT_ref = [("xnTf", 14), ("xnTf", 15)]

    kv = mem.alloc("kv", [128, DCH * T], BF16)
    s_scr = [mem.alloc("s_scr0", [128, T], BF16),
             mem.alloc("s_scr1", [128, T], BF16)]
    wtbuf = [mem.alloc("wtbuf0", [128, DCH * 512], BF16),
             mem.alloc("wtbuf1", [128, DCH * 512], BF16)]
    k_sb = mem.alloc("k_sb", [128, 2048], BF16)
    kvp = mem.alloc("kvp", [128, 512], BF16)
    red0 = mem.alloc("red0", [128, 1])
    red1 = mem.alloc("red1", [128, 1])

    gcyc = [0]
    gmod = [3]

    def next_g():
        g = gcyc[0] % gmod[0]
        gcyc[0] += 1
        return g

    g4cyc = [0]

    def next_g4():
        g = g4cyc[0] % 4
        g4cyc[0] += 1
        return g

    wslot = [0]

    def wt_group(w, e512):
        slw = wslot[0] % 2
        wslot[0] += 1
        src = wsrcT[w][:, e512 * 512:(e512 + 1) * 512] \
            .rearrange("(dc p) r -> p dc r", p=128)
        dma("gpsimd", ch[f"c_wt{slw}"], [(wtbuf[slw][:], src, {})],
            writes=[("wt", slw)])
        return slw

    for e512 in range(4):
        slk = wt_group("k", e512)
        slv = wt_group("v", e512)
        for ec4 in range(4):
            echunk = e512 * 4 + ec4
            gk = [next_g4(), next_g4()]

            def kmm(eng, slk=slk, ec4=ec4, gk=gk):
                insts = []
                for dc in range(DCH):
                    lhs = wtbuf[slk][:, dc * 512 + ec4 * 128:
                                     dc * 512 + (ec4 + 1) * 128]
                    for q in range(4):
                        src = xnT if q < 2 else xnTp
                        mv = src[:, dc * T + (q % 2) * 512:
                                 dc * T + (q % 2) * 512 + 512]
                        insts.append(eng.matmul(
                            PG[gk[q // 2]][:, (q % 2) * 512:(q % 2) * 512 + 512],
                            lhs, mv, start=(dc == 0), stop=(dc == DCH - 1)))
                return insts
            pe(kmm, reads=[("wt", slk), ("xnTf", 12), ("xnTf", 13),
               ("xnTf", 14), ("xnTf", 15)],
               writes=[("PG", gk[0]), ("PG", gk[1])])

            def kev(v, ec=echunk, gk=gk):
                for q in range(4):
                    inst = v.tensor_scalar(
                        k_sb[:, q * 512:(q + 1) * 512],
                        PG[gk[q // 2]][:, (q % 2) * 512:(q % 2) * 512 + 512],
                        s_bcols[:, 16 + ec:17 + ec], None, ALU.add)
                return inst
            ve(kev, reads=[("PG", gk[0]), ("PG", gk[1]), ("consts",)],
               writes=[("k_sb",), ("PG", gk[0]), ("PG", gk[1])])

            gv = [next_g4(), next_g4()]

            def vmm(eng, slv=slv, ec4=ec4, gv=gv):
                insts = []
                for dc in range(DCH):
                    lhs = wtbuf[slv][:, dc * 512 + ec4 * 128:
                                     dc * 512 + (ec4 + 1) * 128]
                    for q in range(4):
                        src = xnT if q < 2 else xnTp
                        mv = src[:, dc * T + (q % 2) * 512:
                                 dc * T + (q % 2) * 512 + 512]
                        insts.append(eng.matmul(
                            PG[gv[q // 2]][:, (q % 2) * 512:(q % 2) * 512 + 512],
                            lhs, mv, start=(dc == 0), stop=(dc == DCH - 1)))
                return insts
            pe(vmm, reads=[("wt", slv), ("xnTf", 12), ("xnTf", 13),
               ("xnTf", 14), ("xnTf", 15)],
               writes=[("PG", gv[0]), ("PG", gv[1])])

            def kvf(v, ec=echunk, gv=gv):
                for q in range(2):
                    inst = v.scalar_tensor_tensor(
                        kv[:, ec * T + q * 512: ec * T + q * 512 + 512],
                        PG[gv[0]][:, q * 512: q * 512 + 512],
                        s_bcols[:, 32 + ec:33 + ec],
                        k_sb[:, q * 512:(q + 1) * 512], ALU.add, ALU.mult)
                return inst
            ve(kvf, reads=[("PG", gv[0]), ("k_sb",), ("consts",)],
               writes=[("kv", echunk), ("PG", gv[0])])

            def kvp1(v, ec=echunk, gv=gv):
                return v.scalar_tensor_tensor(
                    kvp[:], PG[gv[1]][:, 0:512], s_bcols[:, 32 + ec:33 + ec],
                    k_sb[:, 1024:1536], ALU.add, ALU.mult)
            ve(kvp1, reads=[("PG", gv[1]), ("k_sb",), ("consts",)],
               writes=[("kvp",)])
            ve(lambda v: v.reduce_sum(red0[:], kvp[:], AX),
               reads=[("kvp",)], writes=[("red0",)])

            def kvp2(v, ec=echunk, gv=gv):
                return v.scalar_tensor_tensor(
                    kvp[:], PG[gv[1]][:, 512:1024],
                    s_bcols[:, 32 + ec:33 + ec],
                    k_sb[:, 1536:2048], ALU.add, ALU.mult)
            ve(kvp2, reads=[("PG", gv[1]), ("k_sb",), ("consts",)],
               writes=[("kvp",), ("PG", gv[1])])
            ve(lambda v: v.reduce_sum(red1[:], kvp[:], AX),
               reads=[("kvp",)], writes=[("red1",)])

            ve(lambda v: v.tensor_add(red0[:], red0[:], red1[:]),
               reads=[("red0",), ("red1",)], writes=[("red0",)])
            ve(lambda v, ec=echunk: v.tensor_copy(carryT[:, ec:ec + 1],
                                                  red0[:]),
               reads=[("red0",)], writes=[("carryT", echunk)])

    sig_g = xnTp
    S.membar([("xnTf", 12), ("xnTf", 13), ("xnTraw", 12), ("xnTraw", 13)])

    slg = wt_group("g", 0)
    for e512 in range(4):
        for ec4 in range(4):
            echunk = e512 * 4 + ec4
            gg = next_g()

            def gmm(eng, slg=slg, ec4=ec4, gg=gg):
                insts = []
                for dc in range(DCH):
                    lhs = wtbuf[slg][:, dc * 512 + ec4 * 128:
                                     dc * 512 + (ec4 + 1) * 128]
                    for q in range(2):
                        insts.append(eng.matmul(
                            PG[gg][:, q * 512:(q + 1) * 512], lhs,
                            xnT[:, dc * T + q * 512: dc * T + q * 512 + 512],
                            start=(dc == 0), stop=(dc == DCH - 1)))
                return insts
            pe(gmm, reads=[("wt", slg)] + xnT_ref, writes=[("PG", gg)])

            def gev(s, ec=echunk, gg=gg):
                for q in range(2):
                    inst = s.activation(
                        sig_g[:, ec * T + q * 512: ec * T + q * 512 + 512],
                        PG[gg][:, q * 512:(q + 1) * 512], AF.Sigmoid,
                        bias=s_bcols[:, 48 + ec:49 + ec])
                return inst
            se(gev, reads=[("PG", gg), ("consts",)],
               writes=[("sig_g", echunk), ("PG", gg)])
            if echunk < 12:
                if echunk == 0:
                    plan_B_dma(0)
                    plan_B_dma(1)
                plan_B_compute(echunk)
                if echunk + 2 < 12:
                    plan_B_dma(echunk + 2)
            if ec4 == 0 and e512 < 3:
                nslg = wt_group("g", e512 + 1)
        slg = nslg

    ve(lambda v: v.tensor_scalar(carryTm[:], carryT[:], s_carrymask[:], None,
                                 ALU.mult),
       reads=[("carryT", ec) for ec in range(16)] + [("consts",)],
       writes=[("carryTm",)])

    kvs = [mem.alloc("kvs0", [128, T]), mem.alloc("kvs1", [128, T])]
    qt = mem.alloc("qt", [128, T])
    slq = wt_group("q", 0)
    for e512 in range(4):
        for ec4 in range(4):
            echunk = e512 * 4 + ec4
            skv = echunk % 2
            ve(lambda v, ec=echunk, skv=skv: v.tensor_tensor_scan(
                   kvs[skv][:], kv[:, ec * T:(ec + 1) * T],
                   kv[:, ec * T:(ec + 1) * T], carryTm[:, ec:ec + 1],
                   ALU.add, ALU.bypass),
               reads=[("kv", echunk), ("carryTm",)], writes=[("kvs", skv)])
            gq = next_g()

            def qmm(eng, slq=slq, ec4=ec4, gq=gq):
                insts = []
                for dc in range(DCH):
                    lhs = wtbuf[slq][:, dc * 512 + ec4 * 128:
                                     dc * 512 + (ec4 + 1) * 128]
                    for q in range(2):
                        insts.append(eng.matmul(
                            PG[gq][:, q * 512:(q + 1) * 512], lhs,
                            xnT[:, dc * T + q * 512: dc * T + q * 512 + 512],
                            start=(dc == 0), stop=(dc == DCH - 1)))
                return insts
            pe(qmm, reads=[("wt", slq)] + xnT_ref, writes=[("PG", gq)])

            def sq1(v, ec=echunk, gq=gq, skv=skv):
                return v.scalar_tensor_tensor(
                    qt[:], PG[gq][0:128, 0:1024], s_bcols[:, ec:ec + 1],
                    kvs[skv][:], ALU.add, ALU.mult)
            ve(sq1, reads=[("PG", gq), ("kvs", skv), ("consts",)],
               writes=[("qt",), ("PG", gq)])

            ssl = echunk % 2

            def sq2(v, ec=echunk, ssl=ssl):
                return v.tensor_tensor(s_scr[ssl][:], qt[:],
                                       sig_g[:, ec * T:(ec + 1) * T],
                                       ALU.mult)
            ve(sq2, reads=[("qt",), ("sig_g", echunk)],
               writes=[("s_scr", ssl)])
            dma("sync", ch[f"c_so{ssl}"],
                [(sig_g[:, echunk * T:(echunk + 1) * T]
                  .rearrange("p (a b) -> p a b", a=8),
                  s_scr[ssl][:], {"transpose": True})],
                reads=[("s_scr", ssl)],
                writes=[("sig_g", echunk), ("stm", echunk)])
            if ec4 == 0 and e512 < 3:
                nslq = wt_group("q", e512 + 1)
        slq = nslq

    S.membar([("kv", ec) for ec in range(16)]
             + [("stm", ec) for ec in range(16)]
             + [("wt", 0), ("wt", 1), ("k_sb",), ("kvp",), ("red0",),
                ("red1",), ("kvs", 0), ("kvs", 1), ("qt",),
                ("s_scr", 0), ("s_scr", 1)])
    mem.free(["qt", "kvs1", "kvs0", "red1", "red0", "kvp", "k_sb",
              "wtbuf1", "wtbuf0", "s_scr1", "s_scr0", "kv"])

    logits_all = [("logits", tch) for tch in range(16)]

    def l3():
        return logitsT[:].rearrange("p (t e) -> p t e", e=8)

    def bc1(buf):
        return buf[:].unsqueeze(2).broadcast_to([128, 64, 8])

    iota_b = s_iota[:].unsqueeze(1).broadcast_to([128, 64, 8])
    slm_b = s_slotmask[:].unsqueeze(2).broadcast_to([128, 64, 8])

    def ec3():
        return t2b_ec[:].rearrange("p (t e) -> p t e", e=8)

    def am3():
        return t2b_am[:].rearrange("p (t e) -> p t e", e=8)

    ve(lambda v: v.tensor_reduce(t2b_mx1[:], l3(), AX, ALU.max),
       reads=logits_all, writes=[("t2bm",)])
    ve(lambda v: v.tensor_tensor(ec3(), l3(), bc1(t2b_mx1), ALU.is_equal),
       reads=logits_all + [("t2bm",)], writes=[("t2b",)])
    ve(lambda v: v.tensor_scalar(t2b_ec[:], t2b_ec[:], -1000.0, 1000.0,
                                 ALU.mult, ALU.add),
       reads=[("t2b",)], writes=[("t2b",)])
    ve(lambda v: v.tensor_tensor(ec3(), ec3(), iota_b, ALU.add),
       reads=[("t2b",), ("consts",)], writes=[("t2b",)])
    ve(lambda v: v.tensor_reduce(t2b_ix[:], ec3(), AX, ALU.min),
       reads=[("t2b",)], writes=[("t2bi",)])
    ve(lambda v: v.tensor_tensor(
           t2b_is1[:].rearrange("p (t e) -> p t e", e=8), iota_b,
           bc1(t2b_ix), ALU.is_equal),
       reads=[("t2bi",), ("consts",)], writes=[("t2b1",)])
    ve(lambda v: v.scalar_tensor_tensor(t2b_ec[:], t2b_is1[:], -1.0e30,
                                        logitsT[:], ALU.mult, ALU.add),
       reads=[("t2b1",)] + logits_all, writes=[("t2b",)])
    ve(lambda v: v.tensor_reduce(t2b_mx2[:], ec3(), AX, ALU.max),
       reads=[("t2b",)], writes=[("t2bm2",)])
    ve(lambda v: v.tensor_tensor(am3(), ec3(), bc1(t2b_mx2), ALU.is_equal),
       reads=[("t2b",), ("t2bm2",)], writes=[("t2ba",)])
    ve(lambda v: v.tensor_scalar(t2b_am[:], t2b_am[:], -1000.0, 1000.0,
                                 ALU.mult, ALU.add),
       reads=[("t2ba",)], writes=[("t2ba",)])
    ve(lambda v: v.tensor_tensor(am3(), am3(), iota_b, ALU.add),
       reads=[("t2ba",), ("consts",)], writes=[("t2ba",)])
    ve(lambda v: v.tensor_reduce(t2b_ix[:], am3(), AX, ALU.min),
       reads=[("t2ba",)], writes=[("t2bi",)])
    ve(lambda v: v.tensor_tensor(
           t2b_is2[:].rearrange("p (t e) -> p t e", e=8), iota_b,
           bc1(t2b_ix), ALU.is_equal),
       reads=[("t2bi",), ("consts",)], writes=[("t2b2",)])
    ve(lambda v: v.tensor_tensor(t2b_am[:], t2b_is1[:], t2b_is2[:], ALU.add),
       reads=[("t2b1",), ("t2b2",)], writes=[("t2ba",)])
    ve(lambda v: v.tensor_tensor(am3(), am3(), slm_b, ALU.mult),
       reads=[("t2ba",), ("consts",)], writes=[("t2ba",)])

    def csumf(eng):
        insts = []
        for j in range(OWN0):
            insts.append(eng.matmul(PU[0:1, 0:8], s_onescolbf[:],
                                    t2b_am[:, j * 8:(j + 1) * 8],
                                    start=(j == 0), stop=(j == OWN0 - 1)))
        return insts
    pe(csumf, reads=[("t2ba",), ("consts",)], writes=[("PG", 3)])
    ve(lambda v: v.tensor_copy(crow[:], PU[0:1, 0:8]),
       reads=[("PG", 3)], writes=[("crow",), ("PG", 3)])

    def c_tile(j):
        tt = j - OWN0
        if tt == 0:
            ve(lambda v: v.tensor_copy(fbrow[:], crow[:]),
               reads=[("crow",)], writes=[("fbrow",)])

        def pfx(eng, j=j):
            return [
                eng.matmul(PU[:, 8:16], s_lstrict[:],
                           t2b_am[:, j * 8:(j + 1) * 8],
                           start=True, stop=False),
                eng.matmul(PU[:, 8:16], s_onesrow[:], crow[:],
                           start=False, stop=True)]
        pe(pfx, reads=[("t2ba",), ("crow",), ("consts",)], writes=[("PG", 3)])
        ve(lambda v, tt=tt: v.tensor_copy(posb[:, tt * 8:(tt + 1) * 8],
                                          PU[:, 8:16]),
           reads=[("PG", 3)], writes=[("posb", tt)])
        ve(lambda v: v.tensor_single_scalar(t2_val[:], PU[:, 8:16],
                                            CAP, ALU.is_lt),
           reads=[("PG", 3)], writes=[("t2val",), ("PG", 3)])
        ve(lambda v, j=j, tt=tt: v.tensor_tensor(
               ambuf[:, tt * 8:(tt + 1) * 8], t2b_am[:, j * 8:(j + 1) * 8],
               t2_val[:], ALU.mult),
           reads=[("t2ba",), ("t2val",)], writes=[("ambuf", tt)])
        ve(lambda v, j=j: v.tensor_tensor(t2_dd[:], t2b_mx1[:, j:j + 1],
                                          t2b_mx2[:, j:j + 1], ALU.subtract),
           reads=[("t2bm",), ("t2bm2",)], writes=[("t2dd",)])
        se(lambda s: s.activation(t2_w1[:], t2_dd[:], AF.Sigmoid),
           reads=[("t2dd",)], writes=[("t2w1",)])
        se(lambda s: s.activation(t2_w2[:], t2_dd[:], AF.Sigmoid, scale=-1.0),
           reads=[("t2dd",)], writes=[("t2w2",)])
        ve(lambda v, j=j: v.tensor_scalar(t2_m1[:],
                                          t2b_is1[:, j * 8:(j + 1) * 8],
                                          t2_w1[:], None, ALU.mult),
           reads=[("t2b1",), ("t2w1",)], writes=[("t2m1",)])
        ve(lambda v, j=j: v.tensor_scalar(t2_m2[:],
                                          t2b_is2[:, j * 8:(j + 1) * 8],
                                          t2_w2[:], None, ALU.mult),
           reads=[("t2b2",), ("t2w2",)], writes=[("t2m2",)])
        ve(lambda v: v.tensor_add(t2_m1[:], t2_m1[:], t2_m2[:]),
           reads=[("t2m1",), ("t2m2",)], writes=[("t2m1",)])
        ve(lambda v, tt=tt: v.tensor_tensor(
               masks[:, tt * 8:(tt + 1) * 8], t2_m1[:], t2_val[:], ALU.mult),
           reads=[("t2m1",), ("t2val",)], writes=[("masks", tt)])

        def csum1(eng, j=j):
            return [eng.matmul(PU[0:1, 0:8], s_onescolbf[:],
                               t2b_am[:, j * 8:(j + 1) * 8],
                               start=True, stop=True)]
        pe(csum1, reads=[("t2ba",), ("consts",)], writes=[("PG", 3)])
        ve(lambda v: v.tensor_copy(csrow[:], PU[0:1, 0:8]),
           reads=[("PG", 3)], writes=[("csrow",), ("PG", 3)])
        ve(lambda v: v.tensor_add(crow[:], crow[:], csrow[:]),
           reads=[("csrow",), ("crow",)], writes=[("crow",)])

    c_thunks = [(lambda j=j: c_tile(j)) for j in range(OWN0, SLOTS)]

    S.membar([("xTb", 0), ("xTb", 1), ("xsqT", 0),
              ("mrow", 0), ("sqrow", 0), ("vrow", 0), ("rrow", 0),
              ("lgs", 0), ("t2b",), ("t2bi",)])
    mem.free(["t2b_ix", "t2b_ec"] + list(reversed(AB_NAMES)))

    acc = mem.alloc("acc", [128, DCH * T])
    posb = mem.alloc("posb", [128, 64])
    ambuf = mem.alloc("ambuf", [128, 64])
    wbuf = mem.alloc("wbuf", [128, 16])
    gidxu = mem.alloc("gidxu", [128, 32], U32)
    fbrow = mem.alloc("fbrow", [1, 8])
    fb_bc = mem.alloc("fb_bc", [128, 8])
    ebase = mem.alloc("ebase", [128, 8])
    lpb = mem.alloc("lpb", [128, 8])
    a8 = mem.alloc("a8", [128, 8])
    b8 = mem.alloc("b8", [128, 8])
    vcol = mem.alloc("vcol", [128, 1])
    gcol = mem.alloc("gcol", [128, 1])
    hbuf = mem.alloc("hbuf", [128, ICH * T], BF16)
    ubuf = [mem.alloc(f"ubuf{i}", [128, DCH * 128], BF16) for i in range(3)]
    dbuf = [mem.alloc(f"dbuf{i}", [128, ICH * 128], BF16) for i in range(2)]
    hsil = mem.alloc("hsil", [128, T], BF16)
    dzb = [mem.alloc(f"dzb{i}", [128, CAPS], BF16) for i in range(4)]
    xnstage = [mem.alloc(f"xnstage{i}", [128, D], BF16) for i in range(2)]
    xntm = [mem.alloc(f"xntm{i}", [128, D], BF16) for i in range(2)]
    uslot = [0]
    dslot = [0]

    xnT3 = xnT[:].rearrange("p (dc t) -> p dc t", dc=DCH)

    def r_prep():
        gfb = next_g()
        pe(lambda eng, gfb=gfb: [eng.matmul(PG[gfb][:, 0:8], s_onesrow[:],
                                            fbrow[:], start=True, stop=True)],
           reads=[("fbrow",), ("consts",)], writes=[("PG", gfb)])
        ve(lambda v, gfb=gfb: v.tensor_copy(fb_bc[:], PG[gfb][:, 0:8]),
           reads=[("PG", gfb)], writes=[("fb_bc",), ("PG", gfb)])
        ve(lambda v: v.tensor_single_scalar(ebase[:], s_iota[:], float(CAPS),
                                            ALU.mult),
           reads=[("consts",)], writes=[("ebase",)])

    def r_thunk(tt):
        j = OWN0 + tt
        sl = tt % 2
        pcol = posb[:, tt * 8:(tt + 1) * 8]
        acol = ambuf[:, tt * 8:(tt + 1) * 8]
        ve(lambda v: v.tensor_tensor(lpb[:], pcol, fb_bc[:], ALU.subtract),
           reads=[("posb", tt), ("fb_bc",)], writes=[("lpb",)])
        ve(lambda v: v.tensor_tensor(lpb[:], lpb[:], ebase[:], ALU.add),
           reads=[("lpb",), ("ebase",)], writes=[("lpb",)])
        for k, isb, gofs, wofs in ((0, t2b_is1, 0, 0), (1, t2b_is2, 16, 8)):
            iscol = isb[:, j * 8:(j + 1) * 8]
            isref = ("t2b1",) if k == 0 else ("t2b2",)
            ve(lambda v, iscol=iscol: v.tensor_tensor(a8[:], iscol, acol,
                                                      ALU.mult),
               reads=[isref, ("ambuf", tt)], writes=[("a8",)])
            ve(lambda v: v.tensor_tensor(b8[:], a8[:], lpb[:], ALU.mult),
               reads=[("a8",), ("lpb",)], writes=[("b8",)])
            ve(lambda v: v.reduce_sum(gcol[:], b8[:], AX),
               reads=[("b8",)], writes=[("gcol",)])
            ve(lambda v: v.reduce_sum(vcol[:], a8[:], AX),
               reads=[("a8",)], writes=[("vcol",)])
            ve(lambda v: v.tensor_single_scalar(vcol[:], vcol[:], -BIG,
                                                ALU.mult),
               reads=[("vcol",)], writes=[("vcol",)])
            ve(lambda v: v.tensor_single_scalar(vcol[:], vcol[:], BIG,
                                                ALU.add),
               reads=[("vcol",)], writes=[("vcol",)])
            ve(lambda v: v.tensor_add(gcol[:], gcol[:], vcol[:]),
               reads=[("gcol",), ("vcol",)], writes=[("gcol",)])
            ve(lambda v, c=gofs + tt: v.tensor_copy(gidxu[:, c:c + 1],
                                                    gcol[:]),
               reads=[("gcol",)], writes=[("gidxu", gofs + tt)])
            ve(lambda v, iscol=iscol, tt=tt: v.tensor_tensor(
                   b8[:], iscol, masks[:, tt * 8:(tt + 1) * 8], ALU.mult),
               reads=[isref, ("masks", tt)], writes=[("b8",)])
            ve(lambda v, c=wofs + tt: v.reduce_sum(wbuf[:, c:c + 1], b8[:],
                                                   AX),
               reads=[("b8",)], writes=[("wbuf", wofs + tt)])
        ve(lambda v, tt=tt, sl=sl: v.tensor_copy(
               xnstage[sl][:].rearrange("p (a b) -> p a b", a=DCH),
               xnT3[:, :, tt * 128:(tt + 1) * 128]),
           reads=xnT_ref, writes=[("xnstage", sl)])
        dma("sync", ch["c_xs"],
            [(xntm[sl][:].rearrange("p (a b) -> p a b", a=DCH),
              xnstage[sl][:], {"transpose": True})],
            reads=[("xnstage", sl)], writes=[("xntm", sl)])
        for gofs in (0, 16):
            def sc(eng, sl=sl, c=gofs + tt):
                return eng.indirect_dma_start(
                    out=gxe, out_offset=IndirectOffsetOnAxis(
                        ap=gidxu[:, c:c + 1], axis=0),
                    in_=xntm[sl][:], in_offset=None,
                    bounds_check=NE * CAPS - 1, oob_is_err=False)
            idma(ch["c_gx"], sc,
                 reads=[("xntm", sl), ("gidxu", gofs + tt)],
                 writes=[("gxe",)])

    r_thunks = [(lambda tt=tt: r_thunk(tt)) for tt in range(NT)]

    def gb_ap(gsl):
        return xnT[:, gsl * DCH * CAPS:(gsl + 1) * DCH * CAPS] \
            .rearrange("p (dc s) -> p dc s", dc=DCH)

    def gb_mv(gsl, dc):
        base = gsl * DCH * CAPS + dc * CAPS
        return xnT[:, base:base + CAPS]

    def load_gbuf(e):
        gsl = e % 2
        dma("sync", ch["c_gb"],
            [(gb_ap(gsl), gxe[e * CAPS:(e + 1) * CAPS, :],
              {"transpose": True})],
            reads=[("gxe",)],
            writes=[("gbufs", gsl), ("xnTf", 14), ("xnTf", 15)])

    outrow = None
    UNIT_ORDER = [8, 9, 10, 11] + list(range(NE))
    for ui, u in enumerate(UNIT_ORDER):
        routed = u < 8
        if ui == 4:
            while r_thunks:
                r_thunks.pop(0)()
            S.membar([("xntm", 0), ("xntm", 1), ("xnstage", 0),
                      ("xnstage", 1)])
            mem.free(["xntm1", "xntm0", "xnstage1", "xnstage0"])
            outrow = [mem.alloc("outrow0", [128, NSC, D], BF16),
                      mem.alloc("outrow1", [128, NSC, D], BF16)]
        if routed:
            e = u
            gsl = e % 2
            if e == 0:
                load_gbuf(0)
            for ic in range(ICH):
                gus = []
                for mc in (ic, ICH + ic):
                    su_ = uslot[0] % 3
                    uslot[0] += 1
                    dma("gpsimd", ch[f"c_ub{mc % 2}"],
                        [(ubuf[su_][:], wug[e, mc], {})],
                        writes=[("ubuf", su_)])
                    gu = next_g()
                    gus.append(gu)

                    def umm(eng, su_=su_, gu=gu, gsl=gsl):
                        insts = []
                        for dc in range(DCH):
                            insts.append(eng.matmul(
                                PG[gu][:, 0:CAPS],
                                ubuf[su_][:, dc * 128:(dc + 1) * 128],
                                gb_mv(gsl, dc),
                                start=(dc == 0), stop=(dc == DCH - 1)))
                        return insts
                    pe(umm, reads=[("ubuf", su_), ("gbufs", gsl)],
                       writes=[("PG", gu)])
                gg_, gu_ = gus
                se(lambda s, gg_=gg_: s.activation(hsil[:, 0:CAPS],
                                                   PG[gg_][0:128, 0:CAPS],
                                                   AF.Silu),
                   reads=[("PG", gg_)], writes=[("hsil",), ("PG", gg_)])

                def hb(v, ic=ic, gu_=gu_):
                    return v.scalar_tensor_tensor(
                        hbuf[:, ic * CAPS:(ic + 1) * CAPS],
                        PG[gu_][0:128, 0:CAPS], 0.0, hsil[:, 0:CAPS],
                        ALU.bypass, ALU.mult)
                ve(hb, reads=[("PG", gu_), ("hsil",)],
                   writes=[("h", ic), ("PG", gu_)])
            if e + 1 < NE:
                load_gbuf(e + 1)
            for dcc in range(DCH):
                sd_ = dslot[0] % 2
                dslot[0] += 1
                dma("gpsimd", ch[f"c_db{sd_}"],
                    [(dbuf[sd_][:], wdn[e, dcc], {})],
                    writes=[("dbuf", sd_)])
                gd = next_g()

                def dmm(eng, sd_=sd_, gd=gd):
                    insts = []
                    for ic in range(ICH):
                        insts.append(eng.matmul(
                            PG[gd][:, 0:CAPS],
                            dbuf[sd_][:, ic * 128:(ic + 1) * 128],
                            hbuf[:, ic * CAPS:(ic + 1) * CAPS],
                            start=(ic == 0), stop=(ic == ICH - 1)))
                    return insts
                pe(dmm, reads=[("dbuf", sd_)]
                   + [("h", ic) for ic in range(ICH)],
                   writes=[("PG", gd)])
                zsl = dcc % 4
                se(lambda s, gd=gd, zsl=zsl: s.copy(
                       dzb[zsl][:], PG[gd][0:128, 0:CAPS]),
                   reads=[("PG", gd)], writes=[("dzb", zsl), ("PG", gd)])
                dma("sync", ch["c_or"],
                    [(outrow[e % 2][:, :, dcc * 128:(dcc + 1) * 128],
                      dzb[zsl][:], {"transpose": True})],
                    reads=[("dzb", zsl)], writes=[("outrow", e % 2, dcc)])
            dma("scalar", ch["c_ox"],
                [(oxe[e * CAPS:(e + 1) * CAPS, :]
                  .rearrange("(sc p) d -> p sc d", p=128),
                  outrow[e % 2][:].rearrange("p a b -> p (a b)"), {})],
                reads=[("outrow", e % 2, dcc) for dcc in range(DCH)],
                writes=[("oxe", e)])
            continue

        g0 = u - 8


        for ic in range(ICH):
            if ui <= 1 and ic % 3 == 0 and c_thunks:
                c_thunks.pop(0)()
            if ui >= 2 and ic % 3 == 0 and r_thunks:
                r_thunks.pop(0)()
            gus = []
            for mc in (ic, ICH + ic):
                su_ = uslot[0] % 3
                uslot[0] += 1
                dma("gpsimd", ch[f"c_u{mc % 2}"],
                    [(ubuf[su_][:], sug[g0, mc], {})],
                    writes=[("ubuf", su_)])
                gu = next_g()
                gus.append(gu)

                def umm(eng, su_=su_, gu=gu):
                    insts = []
                    for dc in range(DCH):
                        lhs = ubuf[su_][:, dc * 128:(dc + 1) * 128]
                        for q in range(2):
                            insts.append(eng.matmul(
                                PG[gu][:, q * 512:(q + 1) * 512], lhs,
                                xnT[:, dc * T + q * 512:
                                    dc * T + q * 512 + 512],
                                start=(dc == 0), stop=(dc == DCH - 1)))
                    return insts
                pe(umm, reads=[("ubuf", su_)] + xnT_ref, writes=[("PG", gu)])
            gg_, gu_ = gus
            se(lambda s, gg_=gg_: s.activation(hsil[:],
                                               PG[gg_][0:128, 0:1024],
                                               AF.Silu),
               reads=[("PG", gg_)], writes=[("hsil",), ("PG", gg_)])

            def hb(v, ic=ic, gu_=gu_):
                return v.scalar_tensor_tensor(
                    hbuf[:, ic * T:(ic + 1) * T],
                    PG[gu_][0:128, 0:1024], 0.0, hsil[:],
                    ALU.bypass, ALU.mult)
            ve(hb, reads=[("PG", gu_), ("hsil",)],
               writes=[("h", ic), ("PG", gu_)])
        if ui == 1:
            while c_thunks:
                c_thunks.pop(0)()
            r_prep()
            gmod[0] = 4
        for dcc in range(DCH):
            sd_ = dslot[0] % 2
            dslot[0] += 1
            dma("gpsimd", ch[f"c_d{sd_}"],
                [(dbuf[sd_][:], sdn[g0, dcc], {})],
                writes=[("dbuf", sd_)])
            gd = next_g()

            def dmm(eng, sd_=sd_, gd=gd):
                insts = []
                for ic in range(ICH):
                    lhs = dbuf[sd_][:, ic * 128:(ic + 1) * 128]
                    for q in range(2):
                        insts.append(eng.matmul(
                            PG[gd][:, q * 512:(q + 1) * 512], lhs,
                            hbuf[:, ic * T + q * 512:
                                   ic * T + q * 512 + 512],
                            start=(ic == 0), stop=(ic == ICH - 1)))
                return insts
            pe(dmm, reads=[("dbuf", sd_)] + [("h", ic) for ic in range(ICH)],
               writes=[("PG", gd)])
            if ui == 0:
                ve(lambda v, dcc=dcc, gd=gd: v.tensor_copy(
                       acc[:, dcc * T:(dcc + 1) * T], PG[gd][0:128, 0:1024]),
                   reads=[("PG", gd)], writes=[("acc", dcc), ("PG", gd)])
            else:
                ve(lambda v, dcc=dcc, gd=gd: v.tensor_tensor(
                       acc[:, dcc * T:(dcc + 1) * T],
                       acc[:, dcc * T:(dcc + 1) * T],
                       PG[gd][0:128, 0:1024], ALU.add),
                   reads=[("PG", gd), ("acc", dcc)],
                   writes=[("acc", dcc), ("PG", gd)])

    S.membar([("hsil",), ("gbufs", 0), ("gbufs", 1)]
             + [("dzb", i) for i in range(4)]
             + [("outrow", i % 2, dcc) for i in range(2)
                for dcc in range(DCH)]
             + [("dbuf", i) for i in range(2)]
             + [("ubuf", i) for i in range(3)]
             + [("h", ic) for ic in range(ICH)])
    mem.free(["outrow1", "outrow0", "dzb3", "dzb2", "dzb1", "dzb0", "hsil",
              "dbuf1", "dbuf0", "ubuf2", "ubuf1", "ubuf0", "hbuf"])

    g1 = [mem.alloc("g1a", [128, D], BF16), mem.alloc("g1b", [128, D], BF16)]
    g2 = [mem.alloc("g2a", [128, D], BF16), mem.alloc("g2b", [128, D], BF16)]
    tc1 = mem.alloc("tc1", [128, T])
    tc2 = mem.alloc("tc2", [128, T])
    silt = mem.alloc("silt", [128, T])
    xrl = [mem.alloc("xrl0", [128, D]), mem.alloc("xrl1", [128, D])]
    outt = [mem.alloc("outt0", [128, D]), mem.alloc("outt1", [128, D])]
    sgv = sig_g[:].rearrange("p (ec a) -> p ec a", ec=DCH)
    oxe_refs = [("oxe", e) for e in range(NE)]
    for tt in range(NT):
        so = tt % 2
        ge(lambda g, so=so: g.memset(g1[so][:], 0.0),
           writes=[("g1", so)])
        ge(lambda g, so=so: g.memset(g2[so][:], 0.0),
           writes=[("g2", so)])

        def ga1(eng, so=so, tt=tt):
            return eng.indirect_dma_start(
                out=g1[so][:], out_offset=None, in_=oxe,
                in_offset=IndirectOffsetOnAxis(
                    ap=gidxu[:, tt:tt + 1], axis=0),
                bounds_check=NE * CAPS - 1, oob_is_err=False)
        idma(ch["c_gt"], ga1,
             reads=oxe_refs + [("g1", so), ("gidxu", tt)],
             writes=[("g1", so)])

        def ga2(eng, so=so, tt=tt):
            return eng.indirect_dma_start(
                out=g2[so][:], out_offset=None, in_=oxe,
                in_offset=IndirectOffsetOnAxis(
                    ap=gidxu[:, 16 + tt:17 + tt], axis=0),
                bounds_check=NE * CAPS - 1, oob_is_err=False)
        idma(ch["c_gt"], ga2,
             reads=oxe_refs + [("g2", so), ("gidxu", 16 + tt)],
             writes=[("g2", so)])
        dma("sync", ch[f"c_xr{so}"],
            [(xrl[so][:], xo[tt * 128:(tt + 1) * 128, :], {})],
            writes=[("xrl", so)])
        for half in range(2):
            gf = next_g()

            def ftp(eng, tt=tt, half=half, gf=gf):
                insts = []
                for k in range(8):
                    dcc = half * 8 + k
                    insts.append(eng.transpose(
                        PG[gf][:, k * 128:(k + 1) * 128],
                        acc[:, dcc * T + tt * 128: dcc * T + (tt + 1) * 128],
                        s_idf[:]))
                return insts
            pe(ftp, reads=[("acc", half * 8 + k) for k in range(8)]
               + [("consts",)],
               writes=[("PG", gf)])
            hs = slice(half * 1024, (half + 1) * 1024)
            ve(lambda v, so=so, tt=tt, hs=hs: v.tensor_scalar(
                   tc2[:], g2[so][:, hs], wbuf[:, 8 + tt:9 + tt], None,
                   ALU.mult),
               reads=[("g2", so), ("wbuf", 8 + tt)], writes=[("tc2",)])
            ve(lambda v, so=so, tt=tt, hs=hs: v.scalar_tensor_tensor(
                   tc1[:], g1[so][:, hs], wbuf[:, tt:tt + 1], tc2[:],
                   ALU.mult, ALU.add),
               reads=[("g1", so), ("wbuf", tt), ("tc2",)],
               writes=[("tc1",)])
            ve(lambda v, gf=gf: v.tensor_tensor(tc1[:], tc1[:],
                                                PG[gf][0:128, 0:1024],
                                                ALU.add),
               reads=[("tc1",), ("PG", gf)], writes=[("tc1",), ("PG", gf)])
            se(lambda s: s.activation(silt[:], tc1[:], AF.Silu),
               reads=[("tc1",)], writes=[("silt",)])
            ve(lambda v, tt=tt, half=half: v.tensor_tensor(
                   tc1[:].rearrange("p (a b) -> p a b", a=8),
                   silt[:].rearrange("p (a b) -> p a b", a=8),
                   sgv[:, half * 8:(half + 1) * 8,
                       tt * 128:(tt + 1) * 128],
                   ALU.mult),
               reads=[("silt",)] + [("stm", ec)
                                    for ec in range(half * 8, half * 8 + 8)],
               writes=[("tc1",)])
            ve(lambda v, so=so, hs=hs: v.tensor_tensor(
                   outt[so][:, hs], tc1[:], xrl[so][:, hs], ALU.add),
               reads=[("tc1",), ("xrl", so)], writes=[("outt", so, half)])
        dma("sync", ch["c_out"],
            [(out_hbm[tt * 128:(tt + 1) * 128, :], outt[so][:], {})],
            reads=[("outt", so, 0), ("outt", so, 1)],
            writes=[("outdma", tt)])

    def fin(eng, sem, n):
        eng.nop().then_inc(sem, n)
    S.add("gpsimd", fin, chan=ch["fin"],
          extra_waits=[(ch["c_out"], ch["c_out"].total)])

    S.emit(nc, es)
    es.close()
    nc.finalize()
    return nc


_NC_CACHE = {}


def _consts():
    c = {}
    c["c_idbf"] = np.eye(128, dtype=bfnp)
    c["c_idf"] = np.eye(128, dtype=np.float32)
    c["c_onesrow"] = np.ones((1, 128), dtype=np.float32)
    c["c_onescolf"] = np.ones((128, 1), dtype=np.float32)
    c["c_onescolbf"] = np.ones((128, 1), dtype=bfnp)
    c["c_lstrict"] = np.triu(np.ones((128, 128), dtype=bfnp), k=1)
    c["c_iota"] = np.broadcast_to(np.arange(8, dtype=np.float32),
                                  (128, 8)).copy()
    return c


def make_in_maps(L0_attn_x, L0_attn_Wq, L0_attn_bq, L0_attn_Wk, L0_attn_bk,
                 L0_attn_Wv, L0_attn_bv, L0_attn_Wg, L0_attn_bg,
                 L0_ffn_router, L0_ffn_w_up_gate, L0_ffn_w_down,
                 L0_ffn_shared_w_up_gate, L0_ffn_shared_w_down):
    xflat = np.ascontiguousarray(np.asarray(L0_attn_x, dtype=np.float32)
                                 .reshape(TG, D))
    bcols = np.concatenate(
        [np.asarray(b, np.float32).reshape(16, 128).T
         for b in (L0_attn_bq, L0_attn_bk, L0_attn_bv, L0_attn_bg)], axis=1)
    rh = np.ascontiguousarray(
        np.asarray(L0_ffn_router, np.float32).reshape(16, 128, 8)
        .transpose(1, 0, 2).reshape(128, 128))
    wqT = np.ascontiguousarray(np.asarray(L0_attn_Wq, np.float32).T)
    wkT = np.ascontiguousarray(np.asarray(L0_attn_Wk, np.float32).T)
    wvT = np.ascontiguousarray(np.asarray(L0_attn_Wv, np.float32).T)
    wgT = np.ascontiguousarray(np.asarray(L0_attn_Wg, np.float32).T)
    wug_t = np.ascontiguousarray(
        np.asarray(L0_ffn_w_up_gate, np.float32)
        .reshape(8, 16, 128, 22, 128).transpose(0, 3, 2, 1, 4)
        .reshape(8, 22, 128, 2048))
    wdn_t = np.ascontiguousarray(
        np.asarray(L0_ffn_w_down, np.float32)
        .reshape(8, 11, 128, 16, 128).transpose(0, 3, 2, 1, 4)
        .reshape(8, 16, 128, 1408))
    sug_t = np.ascontiguousarray(
        np.asarray(L0_ffn_shared_w_up_gate, np.float32)
        .reshape(16, 128, 2, 4, 11, 128).transpose(3, 2, 4, 1, 0, 5)
        .reshape(4, 22, 128, 2048))
    sdn_t = np.ascontiguousarray(
        np.asarray(L0_ffn_shared_w_down, np.float32)
        .reshape(4, 11, 128, 16, 128).transpose(0, 3, 2, 1, 4)
        .reshape(4, 16, 128, 1408))
    consts = _consts()

    in_maps = []
    for c in range(8):
        sm = np.zeros((128, SLOTS), dtype=np.float32)
        for j in range(SLOTS):
            g = (j + 8 * (c + 1)) % 64
            if g < 8 * (c + 1):
                sm[:, j] = 1.0
        cm = np.full((128, 1), 1.0 if c % 2 == 1 else 0.0, dtype=np.float32)
        m = {
            "xgT": np.ascontiguousarray(
                np.roll(xflat, -1024 * (c + 1), axis=0).T),
            "xo": np.ascontiguousarray(xflat[1024 * c:1024 * (c + 1)]),
            "wq": wqT, "wk": wkT, "wv": wvT, "wgt": wgT,
            "bcols": bcols,
            "rhost": rh,
            "wug": wug_t,
            "wdn": wdn_t,
            "sug": sug_t,
            "sdn": sdn_t,
            "c_slotmask": sm,
            "c_carrymask": cm,
        }
        m.update(consts)
        in_maps.append(m)
    return in_maps


def kernel(**inputs):
    if "nc" not in _NC_CACHE:
        _NC_CACHE["nc"] = build_nc()
    nc = _NC_CACHE["nc"]
    in_maps = make_in_maps(**inputs)
    res = run_bass_kernel_spmd(nc, in_maps, core_ids=list(range(8)))
    out = np.concatenate([res.results[c]["out"] for c in range(8)], axis=0)
    return out.reshape(4, 2048, D)
